# version 14
# baseline (speedup 1.0000x reference)
import math
import numpy as np
import ml_dtypes
import concourse.bass as bass
import concourse.mybir as mybir
from concourse.bass_core_rust import IndirectOffsetOnAxis
from concourse.bass_utils import run_bass_kernel_spmd
from contextlib import ExitStack

F32 = mybir.dt.float32
BF16 = mybir.dt.bfloat16
I32 = mybir.dt.int32
AF = mybir.ActivationFunctionType
ALU = mybir.AluOpType
AX = mybir.AxisListType

T = 4096
HALO = 128
TH = T + HALO
NT = 32
NTH = 33
D = 1024
NF = 8
E = 8
FF = 4096
ECAP = 768
NSLOT = E * ECAP
NH = 8
DF = 128
L = 64
NCH = TH // L
XW = 2048
CAP = 4096
LN_DFI = math.log(DF ** -0.5)


class Ctr:
    def __init__(self, nc, name, step=1):
        self.sem = nc.alloc_semaphore(name=name)
        self.step = step
        self.n = 0

    def tick(self, instr):
        instr.then_inc(self.sem, self.step)
        self.n += self.step
        return self.n


class Eng:
    def __init__(self, eng):
        self.e = eng
        self.seen = {}

    def need(self, ctr, val=None):
        if val is None:
            val = ctr.n
        if val <= 0:
            return
        if self.seen.get(id(ctr), -1) >= val:
            return
        self.e.wait_ge(ctr.sem, val)
        self.seen[id(ctr)] = val


def build(no_coll=False, taps=()):
    nc = bass.Bass("TRN2", target_bir_lowering=False, debug=False,
                   dynamic_dma_scratch_size=16384)

    x_own = nc.dram_tensor("x_own", [T, D], F32, kind="ExternalInput").ap()
    x_halo = nc.dram_tensor("x_halo", [HALO, D], F32, kind="ExternalInput").ap()
    wqT = nc.dram_tensor("wqT", [D, D], F32, kind="ExternalInput").ap()
    wfT = nc.dram_tensor("wfT", [D, D], F32, kind="ExternalInput").ap()
    wiT = nc.dram_tensor("wiT", [D, D], F32, kind="ExternalInput").ap()
    woT = nc.dram_tensor("woT", [D, D], F32, kind="ExternalInput").ap()
    gwin = nc.dram_tensor("gwin", [NF, 128], F32, kind="ExternalInput").ap()
    rwin = nc.dram_tensor("rwin", [D, E], F32, kind="ExternalInput").ap()
    w1in = nc.dram_tensor("w1in", [E, D, FF], F32, kind="ExternalInput").ap()
    b1in = nc.dram_tensor("b1in", [E, 128, FF // 128], F32, kind="ExternalInput").ap()
    w2in = nc.dram_tensor("w2in", [E, FF, D], F32, kind="ExternalInput").ap()
    b2in = nc.dram_tensor("b2in", [E, D], F32, kind="ExternalInput").ap()
    c_ident = nc.dram_tensor("c_ident", [128, 128], BF16, kind="ExternalInput").ap()
    c_scanm = nc.dram_tensor("c_scanm", [128, TH], F32, kind="ExternalInput").ap()
    c_masksc = nc.dram_tensor("c_masksc", [64, 64], F32, kind="ExternalInput").ap()
    c_tril = nc.dram_tensor("c_tril", [128, 128], BF16, kind="ExternalInput").ap()
    c_iota8 = nc.dram_tensor("c_iota8", [128, 8], F32, kind="ExternalInput").ap()
    c_iota32 = nc.dram_tensor("c_iota32", [128, 32], F32, kind="ExternalInput").ap()
    c_mask84 = nc.dram_tensor("c_mask84", [128, 32], F32, kind="ExternalInput").ap()
    c_ones8 = nc.dram_tensor("c_ones8", [128, 8], F32, kind="ExternalInput").ap()
    c_onesc = nc.dram_tensor("c_onesc", [128, 1], BF16, kind="ExternalInput").ap()
    c_onesr = nc.dram_tensor("c_onesr", [1, 128], F32, kind="ExternalInput").ap()
    c_onesr16 = nc.dram_tensor("c_onesr16", [1, 768], BF16, kind="ExternalInput").ap()
    c_cmask = nc.dram_tensor("c_cmask", [8, 1], F32, kind="ExternalInput").ap()
    if no_coll:
        dbg_coreoff = nc.dram_tensor("dbg_coreoff", [1, 8], F32,
                                     kind="ExternalInput").ap()

    y_out = nc.dram_tensor("y", [T, D], F32, kind="ExternalOutput").ap()

    tap_outs = {}

    def tap(name, shape, dt=F32):
        tap_outs[name] = nc.dram_tensor("tap_" + name, shape, dt,
                                        kind="ExternalOutput").ap()
        return tap_outs[name]

    dram = lambda name, shape, dt: nc.dram_tensor(name, shape, dt,
                                                  kind="Internal").ap()
    xg_dram = dram("xg_dram", [NSLOT, D], BF16)
    moe_dram = dram("moe_dram", [NSLOT, D], BF16)
    q_dram = dram("q_dram", [D, TH], BF16)
    vraw_dram = dram("vraw_dram", [D, TH], F32)
    vtok_dram = dram("vtok_dram", [TH, D], BF16)
    ofm_dram = dram("ofm_dram", [D, T], BF16)
    s_dram = dram("s_dram", [T, D], BF16)
    cc_in = dram("cc_in", [1, 8], F32)
    cc_out = dram("cc_out", [8, 8], F32)

    rbound = nc.gpsimd.alloc_register("rbound")
    nc.gpsimd.reg_mov(rbound, NSLOT - 1)
    pe = Eng(nc.tensor)
    dve = Eng(nc.vector)
    act = Eng(nc.scalar)
    pool = Eng(nc.gpsimd)
    sp = Eng(nc.sync)

    cPE = Ctr(nc, "cPE")
    cDVE = Ctr(nc, "cDVE")
    cACT = Ctr(nc, "cACT")
    cCC = Ctr(nc, "cCC", 1)
    sync_sems = []
    pool_sems = []

    def DS(name, q="sync"):
        c = Ctr(nc, name, 16)
        (sync_sems if q == "sync" else pool_sems).append(c)
        return c

    sW = DS("sW")
    sWP = DS("sWP", "pool")
    sXT = [DS(f"sXT{b}", "pool") for b in range(2)]
    sXG = [DS(f"sXG{b}", "pool") for b in range(2)]
    sTAP = DS("sTAP")
    sCC1 = DS("sCC1", "pool")
    sCC2 = DS("sCC2", "pool")
    sQ = [DS(f"sQ{b}") for b in range(2)]
    sVB = [DS(f"sVB{b}") for b in range(2)]
    sVV = [DS(f"sVV{b}") for b in range(2)]
    sBL2 = [DS(f"sBL{b}") for b in range(2)]
    sOF2 = [DS(f"sOF{b}") for b in range(2)]
    sKT2 = [DS(f"sKT{b}") for b in range(2)]
    sWO = [DS(f"sWO{b}") for b in range(2)]
    sOSL = [DS(f"sOSL{b}") for b in range(2)]
    sST = DS("sST")
    sW1 = [DS(f"sW1{b}", "pool") for b in range(2)]
    sW2 = [DS(f"sW2{b}", "pool") for b in range(2)]
    sXGT = DS("sXGT")
    sB1 = DS("sB1")
    sB2s = DS("sB2s")
    sMO = [DS(f"sMO{b}") for b in range(2)]
    sXD = [DS(f"sXD{b}") for b in range(2)]
    sSL = [DS(f"sSL{b}") for b in range(2)]
    sMG = [DS(f"sMG{b}", "pool") for b in range(2)]
    sY = [DS(f"sY{b}") for b in range(2)]
    sCO = DS("sCO")

    ctx = ExitStack()

    def barrier():
        for g in (pe, dve, act, sp, pool):
            for c in (cPE, cDVE, cACT):
                g.need(c, c.n)
            for c in sync_sems + pool_sems:
                g.need(c, c.n)

    def sb(name, shape, dt):
        return ctx.enter_context(nc.sbuf_tensor(name, shape, dt))

    ident16 = sb("ident16", [128, 128], BF16)
    iota8 = sb("iota8", [128, 8], F32)
    ones8 = sb("ones8", [128, 8], F32)
    onesc = sb("onesc", [128, 1], BF16)
    onesr = sb("onesr", [1, 128], F32)
    cmask = sb("cmask", [8, 1], F32)
    gw_sb = sb("gw_sb", [128, 8], F32)
    rwh = sb("rwh", [128, 64], BF16)
    rwl = sb("rwl", [128, 64], BF16)
    slotix = sb("slotix", [128, NT], I32)
    gate_sb = sb("gate_sb", [128, NT], F32)
    pos_sb = sb("pos_sb", [128, NT], F32)
    oh_sb = sb("oh_sb", [128, NT * 8], BF16)
    runoff = sb("runoff", [1, 8], F32)
    coreoff = sb("coreoff", [1, 8], F32)
    coff_rep = sb("coff_rep", [128, 8], F32)
    actb = sb("actb", [128, 3], F32)

    for dst, src in [(ident16, c_ident), (iota8, c_iota8), (ones8, c_ones8),
                     (onesc, c_onesc), (onesr, c_onesr), (cmask, c_cmask)]:
        sW.tick(sp.e.dma_start(out=dst[:], in_=src))
    for ci in range(NF):
        sW.tick(sp.e.dma_start(out=gw_sb[:, ci:ci + 1],
                               in_=gwin[ci:ci + 1, :].rearrange("a p -> p a")))

    ctx1 = ExitStack()
    trilS = ctx1.enter_context(nc.sbuf_tensor("trilS", [128, 128], BF16))
    sW.tick(sp.e.dma_start(out=trilS[:], in_=c_tril))
    rwf = ctx1.enter_context(nc.sbuf_tensor("rwf", [128, 64], F32))
    rdiff = ctx1.enter_context(nc.sbuf_tensor("rdiff", [128, 64], F32))
    sW.tick(sp.e.dma_start(out=rwf[:].rearrange("p (c e) -> p c e", e=8),
                           in_=rwin.rearrange("(c p) e -> p c e", p=128)))
    mW_consts = sW.n
    dve.need(sW)
    cDVE.tick(dve.e.tensor_copy(out=rwh[:], in_=rwf[:]))
    dve.need(cDVE)
    cDVE.tick(dve.e.tensor_sub(out=rdiff[:], in0=rwf[:], in1=rwh[:]))
    dve.need(cDVE)
    cDVE.tick(dve.e.tensor_copy(out=rwl[:], in_=rdiff[:]))
    mW_rw = cDVE.n
    cDVE.tick(dve.e.memset(runoff[:], 0.0))
    mW_runoff0 = cDVE.n
    cDVE.tick(dve.e.memset(actb[:, 0:1], 1e-5))
    cDVE.tick(dve.e.memset(actb[:, 1:2], 1.0))
    cDVE.tick(dve.e.memset(actb[:, 2:3], LN_DFI))
    mW_actb = cDVE.n

    xnfm = ctx1.enter_context(nc.sbuf_tensor("xnfm", [128, NF * XW], BF16))

    ctxA = ExitStack()
    sbA = lambda n, s, d: ctxA.enter_context(nc.sbuf_tensor(n, s, d))
    psA = lambda n, s, d=F32: ctxA.enter_context(nc.psum_tensor(n, s, d))

    GT = 4
    GW = GT * 1024
    xtg = [sbA(f"xtg{b}", [128, GW], F32) for b in range(2)]
    sqg = sbA("sqg", [128, 1024], F32)
    hig = [sbA(f"hig{b}", [128, GW], BF16) for b in range(2)]
    logt = [sbA(f"logt{b}", [128, GW], BF16) for b in range(2)]
    loslabg = [sbA(f"loslabg{b}", [128, GW], BF16) for b in range(2)]
    statg = [sbA(f"statg{b}", [128, 32], F32) for b in range(2)]
    prb = [sbA(f"prb{b}", [128, 32], F32) for b in range(2)]
    ism = [sbA(f"ism{b}", [128, 32], F32) for b in range(2)]
    cumg = [sbA(f"cumg{b}", [128, 32], F32) for b in range(2)]
    posmg = [sbA(f"posmg{b}", [128, 32], F32) for b in range(2)]
    s4 = [sbA(f"s4_{b}", [128, 16], F32) for b in range(2)]
    runvec = sbA("runvec", [1, 32], F32)
    cnt4 = sbA("cnt4", [1, 32], F32)
    iota32 = sbA("iota32", [128, 32], F32)
    mask84 = sbA("mask84", [128, 32], F32)
    zb16 = sbA("zb16", [128, D], BF16)
    psTr = [psA(f"psTrA{b}", [128, 128], BF16) for b in range(2)]
    psLg = psA("psLg", [128, 32], F32)
    psPos = [psA(f"psPos{b}", [128, 32], F32) for b in range(2)]
    psCnt = psA("psCnt", [1, 32], F32)

    ctxB0 = ExitStack()
    sbB0 = lambda n, s, d: ctxB0.enter_context(nc.sbuf_tensor(n, s, d))
    psB0 = lambda n, s, d=F32: ctxB0.enter_context(nc.psum_tensor(n, s, d))
    wq_sb = sbB0("wq_sb", [128, NF * D], BF16)
    wf_sb = sbB0("wf_sb", [128, NF * D], BF16)
    wi_sb = sbB0("wi_sb", [128, NF * D], BF16)
    psQ = [psB0(f"psB0q{b}", [128, 512], F32) for b in range(2)]
    qb16 = [sbB0(f"qb16{b}", [128, 512], BF16) for b in range(2)]
    ub = [sbB0(f"ub{b}", [128, 512], F32) for b in range(2)]
    vb = [sbB0(f"vb{b}", [128, 512], F32) for b in range(2)]
    vvb = [sbB0(f"vvb{b}", [128, 1024], BF16) for b in range(2)]
    m_qb_sp = [0, 0]
    m_vb_sp = [0, 0]
    m_vv_sp = [0, 0]
    m_psq_ev = [0, 0]
    m_psv_ev = [0, 0]
    m_ub_act = [0, 0]

    SSQ, SS1, SMEAN, SRSTD, SM, SZ, SGATE, SEVAL = range(8)
    QPOS, QBAD, QOKL, QSLT = range(4)

    sA8 = DS("sA8")
    cSP_A = sA8
    cDVE.tick(dve.e.memset(zb16[:], 0.0))
    mA_zb = cDVE.n
    sZX = DS("sZX")
    sp.need(cDVE, mA_zb)
    for r0 in range(0, NSLOT, 128):
        sZX.tick(sp.e.dma_start(out=xg_dram[r0:r0 + 128, :], in_=zb16[:]))
    mA_zx = sZX.n
    pool.need(sZX, mA_zx)
    sA8.tick(sp.e.dma_start(out=iota32[:], in_=c_iota32))
    sA8.tick(sp.e.dma_start(out=mask84[:], in_=c_mask84))
    mA_c = sA8.n

    m_xnfm_tile = [0] * NTH
    m_runoff_dve = mW_runoff0

    sXT[0].tick(pool.e.dma_start(out=xtg[0][:, 0:1024], in_=x_halo))
    m_ld0 = sXT[0].n
    for dst, srcw in [(wq_sb, wqT), (wf_sb, wfT), (wi_sb, wiT)]:
        sWP.tick(pool.e.dma_start(
            out=dst[:].rearrange("p (c n) -> p c n", c=NF),
            in_=srcw.rearrange("(c p) n -> p c n", p=128)))
    mW_wproj = sWP.n
    act.need(sXT[0], m_ld0)
    cACT.tick(act.e.activation(out=sqg[:], in_=xtg[0][:, 0:1024], func=AF.Square,
                               accum_out=statg[0][:, SSQ:SSQ + 1]))
    m_sq0 = cACT.n
    dve.need(sXT[0], m_ld0)
    cDVE.tick(dve.e.reduce_sum(out=statg[0][:, SS1:SS1 + 1],
                               in_=xtg[0][:, 0:1024], axis=AX.X))
    dve.need(cDVE)
    cDVE.tick(dve.e.tensor_scalar(out=statg[0][:, SMEAN:SMEAN + 1],
                                  in0=statg[0][:, SS1:SS1 + 1],
                                  scalar1=1.0 / D, scalar2=None, op0=ALU.mult))
    dve.need(cDVE)
    cDVE.tick(dve.e.tensor_tensor(out=statg[0][:, SRSTD:SRSTD + 1],
                                  in0=statg[0][:, SMEAN:SMEAN + 1],
                                  in1=statg[0][:, SMEAN:SMEAN + 1], op=ALU.mult))
    dve.need(cACT, m_sq0)
    dve.need(cDVE)
    cDVE.tick(dve.e.scalar_tensor_tensor(
        out=statg[0][:, SRSTD:SRSTD + 1], in0=statg[0][:, SSQ:SSQ + 1],
        scalar=1.0 / D, in1=statg[0][:, SRSTD:SRSTD + 1],
        op0=ALU.mult, op1=ALU.subtract))
    act.need(cDVE)
    act.need(cDVE, mW_actb)
    cACT.tick(act.e.activation(out=statg[0][:, SRSTD:SRSTD + 1],
                               in_=statg[0][:, SRSTD:SRSTD + 1], func=AF.Ln,
                               bias=actb[:, 0:1], scale=1.0))
    act.need(cACT)
    cACT.tick(act.e.activation(out=statg[0][:, SRSTD:SRSTD + 1],
                               in_=statg[0][:, SRSTD:SRSTD + 1],
                               func=AF.Exp, scale=-0.5))
    m_rstd0 = cACT.n
    dve.need(cACT, m_rstd0)
    cDVE.tick(dve.e.tensor_scalar(out=xtg[0][:, 0:1024], in0=xtg[0][:, 0:1024],
                                  scalar1=statg[0][:, SMEAN:SMEAN + 1],
                                  scalar2=statg[0][:, SRSTD:SRSTD + 1],
                                  op0=ALU.subtract, op1=ALU.mult))
    dve.need(cDVE)
    cDVE.tick(dve.e.tensor_copy(out=hig[0][:, 0:1024], in_=xtg[0][:, 0:1024]))
    m_hi0 = cDVE.n
    m_psTr_evD = [0, 0]
    m_psTr_evA = [0, 0]
    for ci in range(NF):
        pp = ci % 2
        pe.need(cDVE, m_hi0)
        pe.need(cDVE, m_psTr_evD[pp])
        cPE.tick(pe.e.transpose(psTr[pp][:], hig[0][:, ci * 128:(ci + 1) * 128],
                                ident16[:]))
        dve.need(cPE)
        cDVE.tick(dve.e.tensor_copy(
            out=xnfm[:, ci * XW: ci * XW + 128], in_=psTr[pp][:]))
        m_psTr_evD[pp] = cDVE.n
    m_xnfm_tile[0] = cDVE.n
    m_hi_pe = cPE.n

    m_xt_act = [m_sq0, 0]
    m_xt_dve = [m_hi0, 0]
    m_stat_dve = [0, 0]
    m_sq_act = m_sq0
    m_hi_dve = [m_hi0, 0]
    m_hi_used_pe = [m_hi_pe, 0]
    m_lo_used_pe = [0, 0]
    m_loslab_pe = [0, 0]
    m_lg_dve = 0
    m_pos_dve = [0, 0]
    m_cnt_dve = 0
    m_s4_dve = [0, 0]
    m_hi_pl = [0, 0]
    m_hiv = [0, 0]
    m_lov = [0, 0]

    def a_h1(g):
        nonlocal m_sq_act
        b = g % 2
        t0 = g * GT * 128
        pool.need(cACT, m_xt_act[b])
        pool.need(cDVE, m_xt_dve[b])
        sXT[b].tick(pool.e.dma_start(
            out=xtg[b][:].rearrange("p (t d) -> p t d", d=1024),
            in_=x_own[t0:t0 + GT * 128, :].rearrange("(t p) d -> p t d", p=128)))
        m_ld = sXT[b].n

        for k in range(GT):
            act.need(sXT[b], m_ld)
            act.need(cACT, m_sq_act)
            act.need(cDVE, m_stat_dve[b])
            cACT.tick(act.e.activation(
                out=sqg[:], in_=xtg[b][:, k * 1024:(k + 1) * 1024],
                func=AF.Square,
                accum_out=statg[b][:, k * 8 + SSQ: k * 8 + SSQ + 1]))
            m_sq_act = cACT.n
        m_sq = cACT.n
        dve.need(sXT[b], m_ld)
        dve.need(cDVE, m_stat_dve[b])
        cDVE.tick(dve.e.tensor_reduce(
            out=statg[b][:, SS1::8].rearrange("p (t o) -> p t o", o=1),
            in_=xtg[b][:].rearrange("p (t d) -> p t d", d=1024),
            axis=AX.X, op=ALU.add))
        dve.need(cDVE)
        cDVE.tick(dve.e.tensor_scalar(out=statg[b][:, SMEAN::8],
                                      in0=statg[b][:, SS1::8],
                                      scalar1=1.0 / D, scalar2=None,
                                      op0=ALU.mult))
        dve.need(cDVE)
        cDVE.tick(dve.e.tensor_tensor(out=statg[b][:, SRSTD::8],
                                      in0=statg[b][:, SMEAN::8],
                                      in1=statg[b][:, SMEAN::8], op=ALU.mult))
        dve.need(cACT, m_sq)
        dve.need(cDVE)
        cDVE.tick(dve.e.scalar_tensor_tensor(
            out=statg[b][:, SRSTD::8], in0=statg[b][:, SSQ::8],
            scalar=1.0 / D, in1=statg[b][:, SRSTD::8],
            op0=ALU.mult, op1=ALU.subtract))
        m_var = cDVE.n
        act.need(cDVE, m_var)
        cACT.tick(act.e.activation(out=statg[b][:, SRSTD::8],
                                   in_=statg[b][:, SRSTD::8], func=AF.Ln,
                                   bias=actb[:, 0:1], scale=1.0))
        act.need(cACT)
        cACT.tick(act.e.activation(out=statg[b][:, SRSTD::8],
                                   in_=statg[b][:, SRSTD::8],
                                   func=AF.Exp, scale=-0.5))
        m_rstd = cACT.n
        for k in range(GT):
            dve.need(cACT, m_rstd)
            dve.need(cDVE)
            cDVE.tick(dve.e.tensor_scalar(
                out=xtg[b][:, k * 1024:(k + 1) * 1024],
                in0=xtg[b][:, k * 1024:(k + 1) * 1024],
                scalar1=statg[b][:, k * 8 + SMEAN: k * 8 + SMEAN + 1],
                scalar2=statg[b][:, k * 8 + SRSTD: k * 8 + SRSTD + 1],
                op0=ALU.subtract, op1=ALU.mult))
        m_xn = cDVE.n
        m_xt_act[b] = m_sq
        dve.need(cDVE, m_xn)
        dve.need(cDVE, m_hi_dve[b])
        dve.need(cPE, m_hi_used_pe[b])
        dve.need(sXG[b], m_hi_pl[b])
        cDVE.tick(dve.e.tensor_copy(out=hig[b][:], in_=xtg[b][:]))
        m_hi = cDVE.n
        m_hiv[b] = m_hi
        dve.need(cPE, m_lo_used_pe[b])
        dve.need(cDVE)
        cDVE.tick(dve.e.tensor_sub(out=logt[b][:], in0=xtg[b][:], in1=hig[b][:]))
        m_lo = cDVE.n
        m_lov[b] = m_lo
        m_hi_dve[b] = cDVE.n
        m_xt_dve[b] = cDVE.n


    def a_h2(g):
        nonlocal m_runoff_dve, m_cnt_dve, m_lg_dve
        b = g % 2
        m_hi = m_hiv[b]
        m_lo = m_lov[b]
        for kt in range(GT):
            ti = g * GT + kt + 1
            for ci in range(NF):
                pp = ci % 2
                pe.need(cDVE, m_hi)
                pe.need(cDVE, m_psTr_evD[pp])
                pe.need(cACT, m_psTr_evA[pp])
                cPE.tick(pe.e.transpose(
                    psTr[pp][:],
                    hig[b][:, kt * 1024 + ci * 128: kt * 1024 + (ci + 1) * 128],
                    ident16[:]))
                dve.need(cPE)
                wc = (ti * 128) % XW
                cDVE.tick(dve.e.tensor_copy(
                    out=xnfm[:, ci * XW + wc: ci * XW + wc + 128],
                    in_=psTr[pp][:]))
                m_psTr_evD[pp] = cDVE.n
            m_xnfm_tile[ti] = cDVE.n
        m_hi_used_pe[b] = cPE.n
        for kt in range(GT):
            for ci in range(NF):
                pp = ci % 2
                pe.need(cDVE, m_lo)
                pe.need(cDVE, m_psTr_evD[pp])
                pe.need(cACT, m_psTr_evA[pp])
                pe.need(cPE, m_loslab_pe[b])
                cPE.tick(pe.e.transpose(
                    psTr[pp][:],
                    logt[b][:, kt * 1024 + ci * 128: kt * 1024 + (ci + 1) * 128],
                    ident16[:]))
                act.need(cPE)
                cACT.tick(act.e.copy(
                    out=loslabg[b][:, kt * 1024 + ci * 128: kt * 1024 + (ci + 1) * 128],
                    in_=psTr[pp][:]))
                m_psTr_evA[pp] = cACT.n
        m_lo_used_pe[b] = cPE.n
        m_loslab = cACT.n

        pe.need(cACT, m_loslab)
        pe.need(sW, mW_consts)
        pe.need(cDVE, mW_rw)
        pe.need(cDVE, m_lg_dve)
        for kt in range(GT):
            ti = g * GT + kt + 1
            wc = (ti * 128) % XW
            nmm = 0
            for ci in range(NF):
                xb = xnfm[:, ci * XW + wc: ci * XW + wc + 128]
                lb = loslabg[b][:, kt * 1024 + ci * 128: kt * 1024 + (ci + 1) * 128]
                for lhsT, rhs in ((xb, rwh), (xb, rwl), (lb, rwh)):
                    nmm += 1
                    cPE.tick(pe.e.matmul(psLg[:, kt * 8:(kt + 1) * 8], lhsT,
                                         rhs[:, ci * 8:(ci + 1) * 8],
                                         start=(nmm == 1), stop=(nmm == 24)))
        m_lg = cPE.n
        m_loslab_pe[b] = cPE.n

        lg3 = psLg[:].rearrange("p (t e) -> p t e", e=8)
        dve.need(cPE, m_lg)
        dve.need(cDVE, m_s4_dve[b])
        cDVE.tick(dve.e.tensor_reduce(
            out=statg[b][:, SM::8].rearrange("p (t o) -> p t o", o=1),
            in_=lg3, axis=AX.X, op=ALU.max))
        dve.need(cDVE)
        cDVE.tick(dve.e.tensor_tensor(
            out=prb[b][:].rearrange("p (t e) -> p t e", e=8), in0=lg3,
            in1=statg[b][:, SM::8].rearrange("p (t o) -> p t o", o=1)
                .broadcast_to([128, 4, 8]),
            op=ALU.subtract))
        m_lgsub = cDVE.n
        act.need(cDVE, m_lgsub)
        cACT.tick(act.e.activation(out=prb[b][:], in_=prb[b][:], func=AF.Exp))
        m_exp = cACT.n
        dve.need(cACT, m_exp)
        cDVE.tick(dve.e.tensor_reduce(
            out=statg[b][:, SZ::8].rearrange("p (t o) -> p t o", o=1),
            in_=prb[b][:].rearrange("p (t e) -> p t e", e=8),
            axis=AX.X, op=ALU.add))
        dve.need(cDVE)
        cDVE.tick(dve.e.reciprocal(out=statg[b][:, SGATE::8],
                                   in_=statg[b][:, SZ::8]))
        dve.need(cDVE)
        cDVE.tick(dve.e.tensor_tensor(
            out=ism[b][:].rearrange("p (t e) -> p t e", e=8), in0=lg3,
            in1=statg[b][:, SM::8].rearrange("p (t o) -> p t o", o=1)
                .broadcast_to([128, 4, 8]),
            op=ALU.is_ge))
        m_lg_dve = cDVE.n
        dve.need(cDVE)
        dve.need(sA8, mA_c)
        cDVE.tick(dve.e.tensor_tensor_scan(out=cumg[b][:], data0=mask84[:],
                                           data1=ism[b][:], initial=0.0,
                                           op0=ALU.mult, op1=ALU.add))
        dve.need(cDVE)
        cDVE.tick(dve.e.tensor_scalar(out=cumg[b][:], in0=cumg[b][:],
                                      scalar1=1.0, scalar2=None,
                                      op0=ALU.is_equal))
        dve.need(cDVE)
        cDVE.tick(dve.e.tensor_tensor(out=oh_sb[:, g * 32:(g + 1) * 32],
                                      in0=ism[b][:], in1=cumg[b][:],
                                      op=ALU.mult))
        m_oh = cDVE.n
        dve.need(cDVE)
        cDVE.tick(dve.e.tensor_tensor(out=ism[b][:],
                                      in0=oh_sb[:, g * 32:(g + 1) * 32],
                                      in1=iota32[:], op=ALU.mult))
        dve.need(cDVE)
        cDVE.tick(dve.e.tensor_reduce(
            out=statg[b][:, SEVAL::8].rearrange("p (t o) -> p t o", o=1),
            in_=ism[b][:].rearrange("p (t e) -> p t e", e=8),
            axis=AX.X, op=ALU.add))
        m_eval = cDVE.n

        pe.need(cDVE, m_oh)
        pe.need(cDVE, m_pos_dve[b])
        for kt in range(GT):
            cPE.tick(pe.e.matmul(psPos[b][:, kt * 8:(kt + 1) * 8], trilS[:],
                                 oh_sb[:, (g * GT + kt) * 8:(g * GT + kt + 1) * 8],
                                 start=(kt == 0), stop=False))
        pe.need(cDVE, m_cnt_dve)
        cPE.tick(pe.e.matmul(psCnt[:], onesc[:], oh_sb[:, g * 32:(g + 1) * 32],
                             start=True, stop=True))
        m_cntmm = cPE.n
        dve.need(cPE, m_cntmm)
        dve.need(cDVE, m_runoff_dve)
        cDVE.tick(dve.e.tensor_copy(out=cnt4[:], in_=psCnt[:]))
        dve.need(cDVE)
        cDVE.tick(dve.e.tensor_copy(out=runvec[0:1, 0:8], in_=runoff[:]))
        for k in range(1, GT + 1):
            dve.need(cDVE)
            if k < GT:
                cDVE.tick(dve.e.tensor_tensor(
                    out=runvec[0:1, k * 8:(k + 1) * 8],
                    in0=runvec[0:1, (k - 1) * 8: k * 8],
                    in1=cnt4[0:1, (k - 1) * 8: k * 8], op=ALU.add))
            else:
                cDVE.tick(dve.e.tensor_tensor(
                    out=runoff[:], in0=runvec[0:1, (k - 1) * 8: k * 8],
                    in1=cnt4[0:1, (k - 1) * 8: k * 8], op=ALU.add))
        m_runoff_dve = cDVE.n
        m_cnt_dve = cDVE.n
        m_runvec = cDVE.n
        pe.need(cDVE, m_runvec)
        cPE.tick(pe.e.matmul(psPos[b][:], onesr[:], runvec[:],
                             start=False, stop=True))
        m_posmm = cPE.n
        dve.need(cPE, m_posmm)
        cDVE.tick(dve.e.tensor_tensor(out=posmg[b][:],
                                      in0=oh_sb[:, g * 32:(g + 1) * 32],
                                      in1=psPos[b][:], op=ALU.mult))
        m_pos_dve[b] = cDVE.n
        dve.need(cDVE)
        cDVE.tick(dve.e.tensor_reduce(
            out=pos_sb[:, g * GT:(g + 1) * GT]
                .rearrange("p (t o) -> p t o", o=1),
            in_=posmg[b][:].rearrange("p (t e) -> p t e", e=8),
            axis=AX.X, op=ALU.add))
        m_pos = cDVE.n

        dve.need(cDVE, m_pos)
        cDVE.tick(dve.e.tensor_scalar(out=s4[b][:, QBAD::4],
                                      in0=pos_sb[:, g * GT:(g + 1) * GT],
                                      scalar1=float(ECAP), scalar2=None,
                                      op0=ALU.is_ge))
        dve.need(cDVE)
        cDVE.tick(dve.e.tensor_scalar(out=s4[b][:, QOKL::4],
                                      in0=s4[b][:, QBAD::4],
                                      scalar1=-1.0, scalar2=1.0,
                                      op0=ALU.mult, op1=ALU.add))
        dve.need(cDVE)
        cDVE.tick(dve.e.tensor_tensor(out=gate_sb[:, g * GT:(g + 1) * GT],
                                      in0=statg[b][:, SGATE::8],
                                      in1=s4[b][:, QOKL::4], op=ALU.mult))
        dve.need(cDVE)
        cDVE.tick(dve.e.scalar_tensor_tensor(
            out=s4[b][:, QSLT::4], in0=statg[b][:, SEVAL::8],
            scalar=float(ECAP), in1=pos_sb[:, g * GT:(g + 1) * GT],
            op0=ALU.mult, op1=ALU.add))
        m_stat_dve[b] = cDVE.n
        dve.need(cDVE)
        cDVE.tick(dve.e.scalar_tensor_tensor(
            out=s4[b][:, QSLT::4], in0=s4[b][:, QBAD::4], scalar=1.0e6,
            in1=s4[b][:, QSLT::4], op0=ALU.mult, op1=ALU.add))
        dve.need(cDVE)
        cDVE.tick(dve.e.tensor_copy(out=slotix[:, g * GT:(g + 1) * GT],
                                    in_=s4[b][:, QSLT::4]))
        m_slot = cDVE.n
        m_s4_dve[b] = cDVE.n

        for kt in range(GT):
            io = g * GT + kt
            pool.need(cDVE, m_slot)
            pool.need(cDVE, m_hi)
            sXG[b].tick(pool.e.indirect_dma_start(
                out=xg_dram,
                out_offset=IndirectOffsetOnAxis(slotix[:, io:io + 1], 0),
                in_=hig[b][:, kt * 1024:(kt + 1) * 1024], in_offset=None,
                bounds_check=rbound, oob_is_err=False))
        m_hi_pl[b] = sXG[b].n

    def b0_slab(t0, sl):
        tw = t0 % XW
        pe.need(sWP, mW_wproj)
        pe.need(cDVE, m_xnfm_tile[(t0 + sl - 1) // 128])
        for h in range(NH):
            bb = h % 2
            pe.need(cACT, m_psq_ev[bb])
            pe.need(cDVE, m_psv_ev[bb])
            for ci in range(NF):
                cPE.tick(pe.e.matmul(
                    psQ[bb][:, 0:sl],
                    wq_sb[:, ci * D + h * 128: ci * D + (h + 1) * 128],
                    xnfm[:, ci * XW + tw: ci * XW + tw + sl],
                    start=(ci == 0), stop=(ci == NF - 1)))
            m_mmq = cPE.n
            act.need(cPE, m_mmq)
            act.need(sQ[bb], m_qb_sp[bb])
            cACT.tick(act.e.activation(out=qb16[bb][:, 0:sl],
                                       in_=psQ[bb][:, 0:sl], func=AF.Silu))
            m_psq_ev[bb] = cACT.n
            sp.need(cACT)
            sQ[bb].tick(sp.e.dma_start(
                out=q_dram[h * 128:(h + 1) * 128, t0:t0 + sl],
                in_=qb16[bb][:, 0:sl]))
            m_qb_sp[bb] = sQ[bb].n
        for h in range(NH):
            bb = h % 2
            pe.need(cACT, m_psq_ev[bb])
            pe.need(cDVE, m_psv_ev[bb])
            for ci in range(NF):
                cPE.tick(pe.e.matmul(
                    psQ[bb][:, 0:sl],
                    wf_sb[:, ci * D + h * 128: ci * D + (h + 1) * 128],
                    xnfm[:, ci * XW + tw: ci * XW + tw + sl],
                    start=(ci == 0), stop=(ci == NF - 1)))
            m_mmz = cPE.n
            act.need(cPE, m_mmz)
            act.need(cACT, m_ub_act[bb])
            cACT.tick(act.e.activation(out=ub[bb][:, 0:sl], in_=psQ[bb][:, 0:sl],
                                       func=AF.Exp, scale=-1.0))
            m_psq_ev[bb] = cACT.n
            act.need(cACT)
            act.need(sVB[bb], m_vb_sp[bb])
            cACT.tick(act.e.activation(out=vb[bb][:, 0:sl], in_=ub[bb][:, 0:sl],
                                       func=AF.Ln, bias=actb[:, 1:2]))
            m_ub_act[bb] = cACT.n
            sp.need(cACT)
            sVB[bb].tick(sp.e.dma_start(
                out=vraw_dram[h * 128:(h + 1) * 128, t0:t0 + sl],
                in_=vb[bb][:, 0:sl]))
            m_vb_sp[bb] = sVB[bb].n
        for ts in range(sl // 128):
            bb = ts % 2
            tt0 = t0 + ts * 128
            tw2 = tt0 % XW
            for dh in range(2):
                pe.need(cDVE, m_psv_ev[bb])
                pe.need(cACT, m_psq_ev[bb])
                for ci in range(NF):
                    cPE.tick(pe.e.matmul(
                        psQ[bb][:],
                        xnfm[:, ci * XW + tw2: ci * XW + tw2 + 128],
                        wi_sb[:, ci * D + dh * 512: ci * D + (dh + 1) * 512],
                        start=(ci == 0), stop=(ci == NF - 1)))
                m_mmv = cPE.n
                dve.need(cPE, m_mmv)
                dve.need(sVV[bb], m_vv_sp[bb])
                cDVE.tick(dve.e.tensor_copy(
                    out=vvb[bb][:, dh * 512:(dh + 1) * 512], in_=psQ[bb][:]))
                m_psv_ev[bb] = cDVE.n
            sp.need(cDVE, m_psv_ev[bb])
            sVV[bb].tick(sp.e.dma_start(out=vtok_dram[tt0:tt0 + 128, :],
                                        in_=vvb[bb][:]))
            m_vv_sp[bb] = sVV[bb].n

    a_h1(0)
    for g in range(8):
        a_h2(g)
        if g + 1 < 8:
            a_h1(g + 1)
        b0_slab(g * 512, 512)
    b0_slab(4096, 128)

    mA_dve = cDVE.n
    mA_counts = m_runoff_dve

    if "oh" in taps:
        tap("oh", [128, NT * 8], BF16)
        sp.need(cDVE, mA_dve)
        sTAP.tick(sp.e.dma_start(out=tap_outs["oh"], in_=oh_sb[:]))
    if "pos" in taps:
        tap("pos", [128, NT])
        sp.need(cDVE, mA_dve)
        sTAP.tick(sp.e.dma_start(out=tap_outs["pos"], in_=pos_sb[:]))
    if "gate" in taps:
        tap("gate", [128, NT])
        sp.need(cDVE, mA_dve)
        sTAP.tick(sp.e.dma_start(out=tap_outs["gate"], in_=gate_sb[:]))
    if "counts" in taps:
        tap("counts", [1, 8])
        sp.need(cDVE, mA_counts)
        sTAP.tick(sp.e.dma_start(out=tap_outs["counts"], in_=runoff[:]))
    if "slot" in taps:
        tap("slot", [128, NT], I32)
        sp.need(cDVE, mA_dve)
        sTAP.tick(sp.e.dma_start(out=tap_outs["slot"], in_=slotix[:]))

    if not no_coll:
        pool.need(cDVE, mA_counts)
        sCC1.tick(pool.e.dma_start(out=cc_in, in_=runoff[:]))
        pool.need(sCC1)
        cCC.tick(pool.e.collective_compute(
            "AllGather", ALU.bypass, replica_groups=[[0, 1, 2, 3, 4, 5, 6, 7]],
            ins=[cc_in], outs=[cc_out]))

    for bb in range(2):
        sp.need(sQ[bb], m_qb_sp[bb])
        sp.need(sVB[bb], m_vb_sp[bb])
        sp.need(sVV[bb], m_vv_sp[bb])
    if "q" in taps:
        tap("q", [D, TH], BF16)
        sTAP.tick(sp.e.dma_start(out=tap_outs["q"], in_=q_dram))
    if "vraw" in taps:
        tap("vraw", [D, TH])
        sTAP.tick(sp.e.dma_start(out=tap_outs["vraw"], in_=vraw_dram))
    if "vtok" in taps:
        tap("vtok", [TH, D], BF16)
        sTAP.tick(sp.e.dma_start(out=tap_outs["vtok"], in_=vtok_dram))

    ctxB0.close()
    ctxA.close()
    ctx1.close()
    barrier()

    SEG = 22
    SEGT = SEG * L
    NSEGG = SEG // 2
    ctxB = ExitStack()
    sbB = lambda n, s, d: ctxB.enter_context(nc.sbuf_tensor(n, s, d))
    psB = lambda n, s, d=F32: ctxB.enter_context(nc.psum_tensor(n, s, d))

    scanm = sbB("scanm", [128, TH], F32)
    masksc = sbB("masksc", [64, 64], F32)
    qhS = [[sbB(f"qhS{p}_{h}", [128, SEGT], BF16) for h in range(2)]
           for p in range(2)]
    vrS = [[sbB(f"vrS{p}_{h}", [128, SEGT], F32) for h in range(2)]
           for p in range(2)]
    vvS = [[sbB(f"vvS{p}_{h}", [128, SEGT], BF16) for h in range(2)]
           for p in range(2)]
    qtS = [[sbB(f"qtS{p}_{h}", [128, SEGT], BF16) for h in range(2)]
           for p in range(2)]
    ktS = [[sbB(f"ktS{p}_{h}", [128, SEGT], BF16) for h in range(2)]
           for p in range(2)]
    khS = [[sbB(f"khS{p}_{h}", [128, SEGT], BF16) for h in range(2)]
           for p in range(2)]
    ktokS = [[sbB(f"ktokS{p}_{h}", [128, SEGT], BF16) for h in range(2)]
             for p in range(2)]
    BnegS = [[sbB(f"BnegS{p}_{h}", [128, SEGT], F32) for h in range(2)]
             for p in range(2)]
    EmBS = [[sbB(f"EmBS{p}_{h}", [128, SEG * (L + 1)], F32) for h in range(2)]
            for p in range(2)]
    aendS = [[sbB(f"aendS{p}_{h}", [128, SEG], F32) for h in range(2)]
             for p in range(2)]
    S16 = [sbB(f"S16B{h}", [128, 128], BF16) for h in range(2)]
    sc16 = [sbB(f"sc16B{h}", [128, 64], BF16) for h in range(2)]
    ofh = [sbB(f"ofhB{h}", [128, T], BF16) for h in range(2)]
    psSC = [psB(f"psSCB{h}", [64, 64], F32) for h in range(2)]
    psO = [psB(f"psOB{h}", [128, 64], F32) for h in range(2)]
    psDS = [[psB(f"psDSB{h}_{b}", [128, 256], F32) for b in range(2)]
            for h in range(2)]

    sBL2[0].tick(sp.e.dma_start(out=scanm[:], in_=c_scanm))
    sBL2[0].tick(sp.e.dma_start(out=masksc[:], in_=c_masksc))
    mB_scanm_ld = sBL2[0].n
    for p in range(2):
        for h2 in range(2):
            EmBr = EmBS[p][h2][:].rearrange("p (g n) -> p g n", n=L + 1)
            cDVE.tick(dve.e.memset(EmBr[:, :, 0:1], 1.0))
    mB_emb1 = cDVE.n

    NSEG = 3
    NST = (NH // 2) * NSEG
    cPOOL = Ctr(nc, "cPOOL")

    mP_pe = [0, 0]
    mP_dve = [0, 0]
    mP_act = [0, 0]
    mP_pool = [0, 0]
    m_ld = [0, 0]
    m_ofh_sp = [0, 0]

    def b_loads(st):
        p = st % 2
        hp, s = divmod(st, NSEG)
        t0 = s * SEGT
        sp.need(cPE, mP_pe[p])
        sp.need(cDVE, mP_dve[p])
        sp.need(cACT, mP_act[p])
        sp.need(cPOOL, mP_pool[p])
        vt = vtok_dram.rearrange("(j p) d -> p j d", p=128)
        for h2 in range(2):
            h = hp * 2 + h2
            sBL2[p].tick(sp.e.dma_start(out=qhS[p][h2][:],
                                    in_=q_dram[h * 128:(h + 1) * 128, t0:t0 + SEGT]))
            sBL2[p].tick(sp.e.dma_start(out=vrS[p][h2][:],
                                    in_=vraw_dram[h * 128:(h + 1) * 128, t0:t0 + SEGT]))
            vvr = vvS[p][h2][:].rearrange("p (j d) -> p j d", d=128)
            g0 = s * NSEGG
            sBL2[p].tick(sp.e.dma_start(out=vvr[:, 0:6, :],
                                    in_=vt[:, g0:g0 + 6, h * 128:(h + 1) * 128]))
            sBL2[p].tick(sp.e.dma_start(out=vvr[:, 6:NSEGG, :],
                                    in_=vt[:, g0 + 6:g0 + NSEGG, h * 128:(h + 1) * 128]))
        m_ld[p] = sBL2[p].n

    def b_prep_emit(st):
        p = st % 2
        hp, s = divmod(st, NSEG)
        ops = []
        marks = {}

        def scan_h(h2):
            def t():
                dve.need(sBL2[0], sBL2[0].n)
                dve.need(sBL2[1], sBL2[1].n)
                dve.need(cPE, mP_pe[p])
                dve.need(cPOOL, mP_pool[p])
                cDVE.tick(dve.e.tensor_tensor_scan(
                    out=BnegS[p][h2][:], data0=scanm[:, s * SEGT:(s + 1) * SEGT],
                    data1=vrS[p][h2][:], initial=0.0,
                    op0=ALU.mult, op1=ALU.add))
                marks[("scan", h2)] = cDVE.n
            return t

        def emb_h(h2):
            def t():
                act.need(cDVE, marks[("scan", h2)])
                act.need(cDVE, mB_emb1)
                act.need(cPOOL, mP_pool[p])
                EmBr = EmBS[p][h2][:].rearrange("p (g n) -> p g n", n=L + 1)
                cACT.tick(act.e.activation(
                    out=EmBr[:, :, 1:L + 1],
                    in_=BnegS[p][h2][:].rearrange("p (g n) -> p g n", n=L),
                    func=AF.Exp, scale=1.0))
                cACT.tick(act.e.activation(
                    out=aendS[p][h2][:].rearrange("p (g o) -> p g o", o=1),
                    in_=BnegS[p][h2][:].rearrange("p (g n) -> p g n", n=L)[:, :, L - 1:L],
                    func=AF.Exp, scale=-1.0))
                act.need(cACT, cACT.n)
                cACT.tick(act.e.activation(out=BnegS[p][h2][:], in_=BnegS[p][h2][:],
                                           func=AF.Exp, scale=-1.0,
                                           bias=actb[:, 2:3]))
                marks[("emb", h2)] = cACT.n
            return t

        def kq_h(h2):
            def t():
                pool.need(cACT, marks[("emb", h2)])
                pool.need(cPE, mP_pe[p])
                EmBr = EmBS[p][h2][:].rearrange("p (g n) -> p g n", n=L + 1)
                cPOOL.tick(pool.e.tensor_tensor(
                    out=ktS[p][h2][:].rearrange("p (g n) -> p g n", n=L),
                    in0=EmBr[:, :, 1:L + 1], in1=EmBr[:, :, 0:L],
                    op=ALU.subtract))
                pool.need(cPOOL, cPOOL.n)
                cPOOL.tick(pool.e.tensor_tensor(
                    out=khS[p][h2][:].rearrange("p (g n) -> p g n", n=L),
                    in0=ktS[p][h2][:].rearrange("p (g n) -> p g n", n=L),
                    in1=aendS[p][h2][:].rearrange("p (g o) -> p g o", o=1)
                        .broadcast_to([128, SEG, L]),
                    op=ALU.mult))
                cPOOL.tick(pool.e.tensor_tensor(
                    out=qtS[p][h2][:], in0=qhS[p][h2][:], in1=BnegS[p][h2][:],
                    op=ALU.mult))
                marks[("kq", h2)] = cPOOL.n
            return t

        def tr_h(h2, g):
            def t():
                sp.need(cPOOL, marks[("kq", h2)])
                sp.need(cPE, mP_pe[p])
                sKT2[p].tick(sp.e.dma_start(
                    out=ktokS[p][h2][:, g * 128:(g + 1) * 128],
                    in_=khS[p][h2][:, g * 128:(g + 1) * 128],
                    transpose=True))
                marks[("ktr",)] = sKT2[p].n
            return t

        for h2 in range(2):
            ops.append(scan_h(h2))
            ops.append(emb_h(h2))
            ops.append(kq_h(h2))
        for g in range(NSEGG):
            for h2 in range(2):
                ops.append(tr_h(h2, g))
        return ops, marks

    m_sc_dve = [0, 0]
    m_psds_dve = [[0, 0], [0, 0]]
    m_o_dve = [0, 0]

    def b_chunks(st, prep_ops, prep_marks):
        p = st % 2
        hp, s = divmod(st, NSEG)
        nonlocal_marks = b_state[hp]
        m_state = nonlocal_marks["m_state"]
        m_omm = nonlocal_marks["m_omm"]
        pq = list(prep_ops)
        it = 0
        for cl in range(SEG):
            cg = s * SEG + cl
            own = cg >= 2
            j, po = cl // 2, (cl % 2) * 64
            dsb = cl % 2
            dsls = [psDS[h2][dsb][:, ((cl // 2) % 2) * 128:
                                  ((cl // 2) % 2 + 1) * 128] for h2 in range(2)]
            vvv = [vvS[p][h2][:].rearrange("p (j d) -> p j d", d=128)
                   [po:po + 64, j, :] for h2 in range(2)]
            m_ds = [0, 0]
            m_scmm = [0, 0]
            for h2 in range(2):
                pe.need(sKT2[p], prep_marks_cur[("ktr",)])
                pe.need(cPOOL, prep_marks_cur[("kq", h2)])
                pe.need(cDVE, m_psds_dve[h2][dsb])
                cPE.tick(pe.e.matmul(
                    dsls[h2], ktokS[p][h2][po:po + 64, j * 128:(j + 1) * 128],
                    vvv[h2], start=True, stop=True))
                m_ds[h2] = cPE.n
            if own:
                for h2 in range(2):
                    pe.need(cDVE, m_sc_dve[h2])
                    cPE.tick(pe.e.matmul(psSC[h2][:],
                                         ktS[p][h2][:, cl * 64:(cl + 1) * 64],
                                         qtS[p][h2][:, cl * 64:(cl + 1) * 64],
                                         start=True, stop=True))
                    m_scmm[h2] = cPE.n
                for h2 in range(2):
                    dve.need(cPE, m_scmm[h2])
                    cDVE.tick(dve.e.tensor_tensor(out=sc16[h2][po:po + 64, :],
                                                  in0=psSC[h2][:],
                                                  in1=masksc[:], op=ALU.mult))
                    m_sc_dve[h2] = cDVE.n
                for h2 in range(2):
                    pe.need(cDVE, m_sc_dve[h2])
                    pe.need(cACT, m_o_dve[h2])
                    cPE.tick(pe.e.matmul(psO[h2][:], vvv[h2],
                                         sc16[h2][po:po + 64, :],
                                         start=True, stop=False))
                    pe.need(cDVE, m_state[h2])
                    cPE.tick(pe.e.matmul(psO[h2][:], S16[h2][:],
                                         qtS[p][h2][:, cl * 64:(cl + 1) * 64],
                                         start=False, stop=True))
                    m_omm[h2] = cPE.n
            for h2 in range(2):
                dve.need(cPE, m_ds[h2])
                dve.need(cACT, prep_marks_cur[("emb", h2)])
                dve.need(cDVE, m_state[h2])
                if own:
                    dve.need(cPE, m_omm[h2])
                cDVE.tick(dve.e.scalar_tensor_tensor(
                    out=S16[h2][:], in0=S16[h2][:],
                    scalar=aendS[p][h2][:, cl:cl + 1], in1=dsls[h2],
                    op0=ALU.mult, op1=ALU.add))
                m_psds_dve[h2][dsb] = cDVE.n
                m_state[h2] = cDVE.n
            if own:
                for h2 in range(2):
                    act.need(cPE, m_omm[h2])
                    act.need(sOF2[h2], m_ofh_sp[h2])
                    cACT.tick(act.e.copy(
                        out=ofh[h2][:, (cg - 2) * 64:(cg - 1) * 64],
                        in_=psO[h2][:]))
                    m_o_dve[h2] = cACT.n
            it += 1
            if it >= 4:
                for _ in range(2):
                    if pq:
                        pq.pop(0)()
        while pq:
            pq.pop(0)()
        nonlocal_marks["m_state"] = m_state
        nonlocal_marks["m_omm"] = m_omm

    b_state = {}
    b_loads(0)
    prep0_ops, prep0_marks = b_prep_emit(0)
    for t in prep0_ops:
        t()
    prep_marks_cur = prep0_marks
    b_loads(1)
    next_prep = b_prep_emit(1)
    for st in range(NST):
        p = st % 2
        hp, s = divmod(st, NSEG)
        if s == 0:
            for h2 in range(2):
                dve.need(cACT, mP_act[p])
                dve.need(cDVE, cDVE.n)
                dve.need(cPE, b_state.get(hp - 1, {}).get("m_omm", [0, 0])[h2]
                         if hp > 0 else 0)
                cDVE.tick(dve.e.memset(S16[h2][:], 0.0))
            b_state[hp] = dict(m_state=[cDVE.n, cDVE.n], m_omm=[0, 0])
        if st + 1 < NST:
            prep_ops_next, prep_marks_next = next_prep
        else:
            prep_ops_next, prep_marks_next = [], {}
        b_chunks(st, prep_ops_next, prep_marks_next)
        mP_pe[p] = cPE.n
        mP_dve[p] = cDVE.n
        mP_act[p] = cACT.n
        mP_pool[p] = cPOOL.n
        prep_marks_cur = prep_marks_next if prep_marks_next else prep_marks_cur
        if st + 2 < NST:
            b_loads(st + 2)
            next_prep = b_prep_emit(st + 2)
        if s == NSEG - 1:
            for h2 in range(2):
                h = hp * 2 + h2
                sp.need(cACT, cACT.n)
                sOF2[h2].tick(sp.e.dma_start(out=ofm_dram[h * 128:(h + 1) * 128, :],
                                             in_=ofh[h2][:]))
                m_ofh_sp[h2] = sOF2[h2].n

    if "ofm" in taps:
        tap("ofm", [D, T], BF16)
        for h2 in range(2):
            sp.need(sOF2[h2], m_ofh_sp[h2])
        sTAP.tick(sp.e.dma_start(out=tap_outs["ofm"], in_=ofm_dram))

    ctxB.close()
    barrier()

    ctxB2 = ExitStack()
    sbB2 = lambda n, s, d: ctxB2.enter_context(nc.sbuf_tensor(n, s, d))
    psB2 = lambda n, s, d=F32: ctxB2.enter_context(nc.psum_tensor(n, s, d))

    wo16 = sbB2("wo16", [128, NF * D], BF16)
    wof = [sbB2(f"wof{b}", [128, D], F32) for b in range(2)]
    onesc16 = sbB2("onesc16", [128, 1], BF16)
    oslab = [sbB2(f"oslab{b}", [128, NF * 512], BF16) for b in range(2)]
    o2s = sbB2("o2s", [128, NF * 512], BF16)
    of16 = sbB2("of16", [128, NF * 512], BF16)
    rstdr = sbB2("rstdr", [1, 512], F32)
    s16b = [sbB2(f"s16b{b}", [128, 512], BF16) for b in range(2)]
    stok = sbB2("stok", [128, 4096], BF16)
    psSS = psB2("psSS", [1, 512], F32)
    psRep = psB2("psRep", [128, 512], F32)
    psS = [psB2(f"psS{b}", [128, 512], F32) for b in range(2)]
    psTr2 = [psB2(f"psTr2{b}", [128, 128], BF16) for b in range(2)]

    cDVE.tick(dve.e.memset(onesc16[:], 1.0))
    mB2_ones = cDVE.n
    m_wof_dve = [0, 0]
    for ci in range(NF):
        bb = ci % 2
        sp.need(cDVE, m_wof_dve[bb])
        sWO[bb].tick(sp.e.dma_start(out=wof[bb][:],
                                    in_=woT[ci * 128:(ci + 1) * 128, :]))
        dve.need(sWO[bb])
        dve.need(sW, mW_consts)
        cDVE.tick(dve.e.tensor_scalar(out=wo16[:, ci * D:(ci + 1) * D],
                                      in0=wof[bb][:],
                                      scalar1=gw_sb[:, ci:ci + 1], scalar2=None,
                                      op0=ALU.mult))
        m_wof_dve[bb] = cDVE.n
    mB2_wo = cDVE.n

    m_osl_act = [0, 0]
    m_osl_dve = [0, 0]
    m_o2_pe = 0
    m_of_pe = 0
    m_psS_dve = [0, 0]
    m_s16b_pe = [0, 0]
    m_psTr2_ev = [0, 0]
    m_rep_dve = 0
    m_stok_sp = 0
    ofr = ofm_dram.rearrange("(c p) t -> p c t", p=128)
    for sl8 in range(8):
        t0 = sl8 * 512
        bb = sl8 % 2
        sp.need(cACT, m_osl_act[bb])
        sp.need(cDVE, m_osl_dve[bb])
        sp.need(sOF2[0], m_ofh_sp[0])
        sp.need(sOF2[1], m_ofh_sp[1])
        sOSL[bb].tick(sp.e.dma_start(
            out=oslab[bb][:].rearrange("p (c n) -> p c n", n=512),
            in_=ofr[:, :, t0:t0 + 512]))
        m_osl = sOSL[bb].n
        act.need(sOSL[bb], m_osl)
        act.need(cPE, m_o2_pe)
        cACT.tick(act.e.activation(out=o2s[:], in_=oslab[bb][:], func=AF.Square))
        m_o2 = cACT.n
        m_osl_act[bb] = cACT.n
        pe.need(cACT, m_o2)
        pe.need(cDVE, mB2_ones)
        pe.need(cACT, 0)
        for ci in range(NF):
            cPE.tick(pe.e.matmul(psSS[:], onesc16[:],
                                 o2s[:, ci * 512:(ci + 1) * 512],
                                 start=(ci == 0), stop=(ci == NF - 1)))
        m_ss = cPE.n
        m_o2_pe = cPE.n
        act.need(cPE, m_ss)
        cACT.tick(act.e.activation(out=rstdr[:], in_=psSS[:], func=AF.Ln,
                                   scale=1.0 / D, bias=actb[0:1, 0:1]))
        act.need(cACT)
        cACT.tick(act.e.activation(out=rstdr[:], in_=rstdr[:], func=AF.Exp,
                                   scale=-0.5))
        m_rstd = cACT.n
        pe.need(cACT, m_rstd)
        pe.need(cDVE, m_rep_dve)
        cPE.tick(pe.e.matmul(psRep[:], onesr[:], rstdr[:], start=True, stop=True))
        m_rep = cPE.n
        dve.need(cPE, m_rep)
        dve.need(cPE, m_of_pe)
        for ci in range(NF):
            cDVE.tick(dve.e.tensor_tensor(out=of16[:, ci * 512:(ci + 1) * 512],
                                          in0=oslab[bb][:, ci * 512:(ci + 1) * 512],
                                          in1=psRep[:], op=ALU.mult))
        m_of = cDVE.n
        m_osl_dve[bb] = cDVE.n
        m_rep_dve = cDVE.n
        for dc in range(NF):
            b2 = dc % 2
            pe.need(cDVE, m_of)
            pe.need(cDVE, mB2_wo)
            pe.need(cDVE, m_psS_dve[b2])
            for ci in range(NF):
                cPE.tick(pe.e.matmul(
                    psS[b2][:],
                    wo16[:, ci * D + dc * 128: ci * D + (dc + 1) * 128],
                    of16[:, ci * 512:(ci + 1) * 512],
                    start=(ci == 0), stop=(ci == NF - 1)))
            m_smm = cPE.n
            m_of_pe = cPE.n
            dve.need(cPE, m_smm)
            dve.need(cPE, m_s16b_pe[b2])
            cDVE.tick(dve.e.tensor_copy(out=s16b[b2][:], in_=psS[b2][:]))
            m_s16c = cDVE.n
            m_psS_dve[b2] = cDVE.n
            for ts in range(4):
                pp = ts % 2
                pe.need(cDVE, m_s16c)
                pe.need(cDVE, m_psTr2_ev[pp])
                cPE.tick(pe.e.transpose(psTr2[pp][:],
                                        s16b[b2][:, ts * 128:(ts + 1) * 128],
                                        ident16[:]))
                dve.need(cPE)
                dve.need(sST, m_stok_sp)
                cDVE.tick(dve.e.tensor_copy(
                    out=stok[:, ts * 1024 + dc * 128: ts * 1024 + (dc + 1) * 128],
                    in_=psTr2[pp][:]))
                m_psTr2_ev[pp] = cDVE.n
            m_s16b_pe[b2] = cPE.n
        m_stok = cDVE.n
        for ts in range(4):
            sp.need(cDVE, m_stok)
            sST.tick(sp.e.dma_start(
                out=s_dram[t0 + ts * 128: t0 + (ts + 1) * 128, :],
                in_=stok[:, ts * 1024:(ts + 1) * 1024]))
        m_stok_sp = sST.n
    mB2_sp = sST.n

    if "s" in taps:
        tap("s", [T, D], BF16)
        sp.need(sST, mB2_sp)
        sTAP.tick(sp.e.dma_start(out=tap_outs["s"], in_=s_dram))

    ctxB2.close()
    barrier()

    ctxC = ExitStack()
    sbC = lambda n, s, d: ctxC.enter_context(nc.sbuf_tensor(n, s, d))
    psC = lambda n, s, d=F32: ctxC.enter_context(nc.psum_tensor(n, s, d))

    w1h = [sbC(f"w1h{b}", [128, NF * 2048], BF16) for b in range(2)]
    w2s = [sbC(f"w2s{b}", [128, 8 * 512], BF16) for b in range(2)]
    acc_sb = sbC("acc_sb", [128, 6 * 512], F32)
    xgfm = sbC("xgfm", [128, NF * ECAP], BF16)
    h_sb = sbC("h_sb", [128, 32 * ECAP], BF16)
    b1p = sbC("b1p", [128, FF // 128], F32)
    onesr16 = sbC("onesr16", [1, 768], BF16)
    b2f = sbC("b2f", [1, D], F32)
    b2h16 = sbC("b2h16", [1, D], BF16)
    b2l16 = sbC("b2l16", [1, D], BF16)
    mo6 = sbC("mo6", [128, 6 * D], BF16)
    psH = [psC(f"psH{b}", [128, 384], F32) for b in range(2)]
    psO2 = [psC(f"psO2_{j}", [128, 512], F32) for j in range(6)]

    w1r = w1in.rearrange("e (c p) f -> e p c f", p=128)
    w2r = w2in.rearrange("e (fc p) d -> e p fc d", p=128)
    xgr = xg_dram.rearrange("(jj p) d -> p jj d", p=128)

    m_w1_pe = [0, 0]
    m_w2_pe = [0, 0]
    m_xgt_pe = 0
    m_xgfm_pe = 0
    m_h_pe = 0
    m_psH_act = [0, 0]
    m_h_act = 0
    m_b1_act = 0
    m_b2_pe = 0
    m_b2_dve = 0
    m_psO2_dve = [0] * 6
    m_acc_dve = [0] * 6
    m_mo_sp = 0
    def c_w1_thunks(e_):
        thunks = []
        for fh in range(2):
            def t(fh=fh):
                pool.need(cPE, m_w1_pe[fh])
                sW1[fh].tick(pool.e.dma_start(
                    out=w1h[fh][:].rearrange("p (c f) -> p c f", c=NF),
                    in_=w1r[e_, :, :, fh * 2048:(fh + 1) * 2048]))
            thunks.append(t)
        return thunks

    def c_loads(e_):
        sp.need(cPE, m_xgt_pe)
        for bq in range(2):
            sp.need(sXG[bq], m_hi_pl[bq])
        sp.need(sZX, mA_zx)
        for j in range(6):
            r0 = e_ * ECAP + j * 128
            for ci in range(NF):
                sXGT.tick(sp.e.dma_start(
                    out=xgfm[:, ci * ECAP + j * 128: ci * ECAP + (j + 1) * 128],
                    in_=xg_dram[r0:r0 + 128, ci * 128:(ci + 1) * 128],
                    transpose=True))
        sp.need(cACT, m_b1_act)
        sB1.tick(sp.e.dma_start(out=b1p[:], in_=b1in[e_, :, :]))
        sp.need(cPE, m_b2_pe)
        sp.need(cDVE, m_b2_dve)
        sB2s.tick(sp.e.dma_start(out=b2f[:], in_=b2in[e_:e_ + 1, :]))
        return dict(xgl=sXGT.n, b1l=sB1.n, bl=sB2s.n)

    cDVE.tick(dve.e.memset(onesr16[:], 1.0))
    mC_ones16 = cDVE.n
    for t in c_w1_thunks(0):
        t()
    c_marks = c_loads(0)
    w1_thunks = []
    for e_ in range(E):
        m_xgl = c_marks["xgl"]
        m_b1l = c_marks["b1l"]
        m_bl = c_marks["bl"]
        dve.need(sB2s, m_bl)
        dve.need(cPE, m_b2_pe)
        cDVE.tick(dve.e.tensor_copy(out=b2h16[:], in_=b2f[:]))
        dve.need(cDVE)
        cDVE.tick(dve.e.tensor_sub(out=b2f[:], in0=b2f[:], in1=b2h16[:]))
        dve.need(cDVE)
        cDVE.tick(dve.e.tensor_copy(out=b2l16[:], in_=b2f[:]))
        m_b2_dve = cDVE.n
        m_b16 = cDVE.n

        m_xgfm = m_xgl

        for fh in range(2):
            for fc in range(16):
                fcg = fh * 16 + fc
                for th in range(2):
                    pb = (fcg * 2 + th) % 2
                    pe.need(sW1[0], sW1[0].n)
                    pe.need(sW1[1], sW1[1].n)
                    pe.need(sXGT, m_xgfm)
                    pe.need(cACT, m_psH_act[pb])
                    for ci in range(NF):
                        cPE.tick(pe.e.matmul(
                            psH[pb][:],
                            w1h[fh][:, ci * 2048 + fc * 128: ci * 2048 + (fc + 1) * 128],
                            xgfm[:, ci * ECAP + th * 384: ci * ECAP + (th + 1) * 384],
                            start=(ci == 0), stop=(ci == NF - 1)))
                    m_h1 = cPE.n
                    act.need(cPE, m_h1)
                    act.need(cPE, m_h_pe)
                    act.need(sB1, m_b1l)
                    cACT.tick(act.e.activation(
                        out=h_sb[:, fcg * ECAP + th * 384: fcg * ECAP + (th + 1) * 384],
                        in_=psH[pb][:], func=AF.Relu,
                        bias=b1p[:, fcg:fcg + 1], scale=1.0))
                    m_psH_act[pb] = cACT.n
        m_h_act = cACT.n
        m_b1_act = cACT.n
        m_w1_pe = [cPE.n, cPE.n]
        m_xgt_pe = cPE.n
        if e_ + 1 < E:
            c_marks = c_loads(e_ + 1)
            w1_thunks = c_w1_thunks(e_ + 1)

        def w2_load(q, dh):
            qb = q % 2
            pool.need(cPE, m_w2_pe[qb])
            sW2[qb].tick(pool.e.dma_start(
                out=w2s[qb][:].rearrange("p (s d) -> p s d", s=8),
                in_=w2r[e_, :, q * 8:(q + 1) * 8, dh * 512:(dh + 1) * 512]))
            return sW2[qb].n

        m_w2l = [0, 0, 0, 0]
        for dh in range(2):
            m_w2l[0] = w2_load(0, dh)
            m_w2l[1] = w2_load(1, dh)
            for q in range(4):
                qb = q % 2
                if w1_thunks and (q == 1 or q == 3):
                    w1_thunks.pop(0)()
                for jj in range(6):
                    pe.need(sW2[qb], m_w2l[q])
                    pe.need(cACT, m_h_act)
                    if q in (0, 2):
                        pe.need(cDVE, m_psO2_dve[jj])
                    for s8 in range(8):
                        fcg = q * 8 + s8
                        cPE.tick(pe.e.matmul(
                            psO2[jj][:],
                            h_sb[:, fcg * ECAP + jj * 128:
                                 fcg * ECAP + (jj + 1) * 128],
                            w2s[qb][:, s8 * 512:(s8 + 1) * 512],
                            start=(fcg % 16 == 0),
                            stop=(q == 1 and s8 == 7)))
                    if q == 3:
                        cPE.tick(pe.e.matmul(
                            psO2[jj][:], onesr16[0:1, 0:128],
                            b2h16[0:1, dh * 512:(dh + 1) * 512],
                            start=False, stop=False))
                        cPE.tick(pe.e.matmul(
                            psO2[jj][:], onesr16[0:1, 0:128],
                            b2l16[0:1, dh * 512:(dh + 1) * 512],
                            start=False, stop=True))
                    m_mm2 = cPE.n
                    if q == 1:
                        dve.need(cPE, m_mm2)
                        dve.need(cDVE, m_acc_dve[jj])
                        cDVE.tick(dve.e.tensor_copy(
                            out=acc_sb[:, jj * 512:(jj + 1) * 512],
                            in_=psO2[jj][:]))
                        m_psO2_dve[jj] = cDVE.n
                    elif q == 3:
                        dve.need(cPE, m_mm2)
                        dve.need(sMO[0], m_mo_sp)
                        cDVE.tick(dve.e.tensor_tensor(
                            out=mo6[:, jj * D + dh * 512:
                                 jj * D + (dh + 1) * 512],
                            in0=psO2[jj][:],
                            in1=acc_sb[:, jj * 512:(jj + 1) * 512],
                            op=ALU.add))
                        m_psO2_dve[jj] = cDVE.n
                        m_acc_dve[jj] = cDVE.n
                m_w2_pe[qb] = cPE.n
                if q + 2 < 4:
                    m_w2l[q + 2] = w2_load(q + 2, dh)
        m_mo = cDVE.n
        for jj in range(6):
            sp.need(cDVE, m_mo)
            sMO[0].tick(sp.e.dma_start(
                out=moe_dram[e_ * ECAP + jj * 128: e_ * ECAP + (jj + 1) * 128, :],
                in_=mo6[:, jj * D:(jj + 1) * D]))
        m_mo_sp = sMO[0].n
        m_b2_pe = cPE.n
        m_h_pe = cPE.n
    mC_mo = m_mo_sp

    if "moe" in taps:
        tap("moe", [NSLOT, D], BF16)
        sp.need(sMO[0], mC_mo)
        sTAP.tick(sp.e.dma_start(out=tap_outs["moe"], in_=moe_dram))
    if "xg" in taps:
        tap("xg", [NSLOT, D], BF16)
        for bq in range(2):
            sp.need(sXG[bq], m_hi_pl[bq])
        sTAP.tick(sp.e.dma_start(out=tap_outs["xg"], in_=xg_dram))

    ctxC.close()
    barrier()

    ctxD = ExitStack()
    sbD = lambda n, s, d: ctxD.enter_context(nc.sbuf_tensor(n, s, d))
    psD = lambda n, s, d=F32: ctxD.enter_context(nc.psum_tensor(n, s, d))

    cco = sbD("cco", [8, 8], F32)
    psCO = psD("psCO", [1, 8], F32)
    psCOr = psD("psCOr", [128, 8], F32)
    if no_coll:
        sCO.tick(sp.e.dma_start(out=coreoff[:], in_=dbg_coreoff))
        pe.need(sCO)
    else:
        pool.need(cCC, 1)
        sCC2.tick(pool.e.dma_start(out=cco[:], in_=cc_out))
        pe.need(sCC2)
        cPE.tick(pe.e.matmul(psCO[:], cmask[:], cco[:], start=True, stop=True))
        dve.need(cPE)
        cDVE.tick(dve.e.tensor_copy(out=coreoff[:], in_=psCO[:]))
        pe.need(cDVE)
    cPE.tick(pe.e.matmul(psCOr[:], onesr[:], coreoff[:], start=True, stop=True))
    dve.need(cPE)
    cDVE.tick(dve.e.tensor_copy(out=coff_rep[:], in_=psCOr[:]))
    mD_rep = cDVE.n

    xd = [sbD(f"xd{b}", [128, D], F32) for b in range(2)]
    mt16 = [sbD(f"mt16{b}", [128, D], BF16) for b in range(2)]
    st16 = [sbD(f"st16{b}", [128, D], BF16) for b in range(2)]
    mg = [sbD(f"mg{b}", [128, D], F32) for b in range(2)]
    gs = [sbD(f"gs{b}", [128, D], F32) for b in range(2)]
    outb = [sbD(f"outb{b}", [128, D], F32) for b in range(2)]
    d8 = [sbD(f"d8_{b}", [128, 32], F32) for b in range(2)]
    dc4 = [sbD(f"dc4_{b}", [128, 4], F32) for b in range(2)]

    m_xd_dve = [0, 0]
    m_mt_dve = [0, 0]
    m_st_dve = [0, 0]
    m_mg_act = [0, 0]
    m_gs_dve = [0, 0]
    m_out_sp = [0, 0]
    m_ge_dve = [0, 0]
    ge4 = [sbD(f"ge4_{b}", [128, 4], F32) for b in range(2)]
    for gq in range(8):
        bq = gq % 2
        dve.need(cDVE, mD_rep)
        dve.need(cDVE, m_ge_dve[bq])
        cDVE.tick(dve.e.tensor_tensor(
            out=d8[bq][:].rearrange("p (t e) -> p t e", e=8),
            in0=oh_sb[:, gq * 32:(gq + 1) * 32]
                .rearrange("p (t e) -> p t e", e=8),
            in1=coff_rep[:].rearrange("p (o e) -> p o e", o=1)
                .broadcast_to([128, 4, 8]),
            op=ALU.mult))
        dve.need(cDVE)
        cDVE.tick(dve.e.tensor_reduce(
            out=dc4[bq][:].rearrange("p (t o) -> p t o", o=1),
            in_=d8[bq][:].rearrange("p (t e) -> p t e", e=8),
            axis=AX.X, op=ALU.add))
        dve.need(cDVE)
        cDVE.tick(dve.e.tensor_tensor(out=dc4[bq][:], in0=dc4[bq][:],
                                      in1=pos_sb[:, gq * 4:(gq + 1) * 4],
                                      op=ALU.add))
        dve.need(cDVE)
        cDVE.tick(dve.e.tensor_scalar(out=dc4[bq][:], in0=dc4[bq][:],
                                      scalar1=float(CAP), scalar2=None,
                                      op0=ALU.is_lt))
        dve.need(cDVE)
        cDVE.tick(dve.e.tensor_tensor(out=ge4[bq][:], in0=dc4[bq][:],
                                      in1=gate_sb[:, gq * 4:(gq + 1) * 4],
                                      op=ALU.mult))
        m_ge = cDVE.n
        m_ge_dve[bq] = cDVE.n
        for kt in range(4):
            io = gq * 4 + kt
            b = io % 2
            sp.need(cDVE, m_xd_dve[b])
            sXD[b].tick(sp.e.dma_start(out=xd[b][:],
                                       in_=x_own[io * 128:(io + 1) * 128, :]))
            m_xl = sXD[b].n
            pool.need(cDVE, m_mt_dve[b])
            pool.need(sMO[0], mC_mo)
            sMG[b].tick(pool.e.indirect_dma_start(
                out=mt16[b][:], out_offset=None,
                in_=moe_dram,
                in_offset=IndirectOffsetOnAxis(slotix[:, io:io + 1], 0),
                bounds_check=rbound, oob_is_err=False))
            m_ml = sMG[b].n
            sp.need(cDVE, m_st_dve[b])
            sp.need(sST, mB2_sp)
            sSL[b].tick(sp.e.dma_start(out=st16[b][:],
                                       in_=s_dram[io * 128:(io + 1) * 128, :]))
            m_sl = sSL[b].n
            dve.need(sMG[b], m_ml)
            dve.need(cACT, m_mg_act[b])
            dve.need(cDVE, m_ge)
            cDVE.tick(dve.e.tensor_scalar(out=mg[b][:], in0=mt16[b][:],
                                          scalar1=ge4[bq][:, kt:kt + 1],
                                          scalar2=None, op0=ALU.mult))
            m_mt_dve[b] = cDVE.n
            m_mg = cDVE.n
            act.need(cDVE, m_mg)
            act.need(cDVE, m_gs_dve[b])
            cACT.tick(act.e.activation(out=gs[b][:], in_=mg[b][:], func=AF.Silu))
            m_mg_act[b] = cACT.n
            m_gsa = cACT.n
            dve.need(cACT, m_gsa)
            dve.need(sSL[b], m_sl)
            dve.need(sY[b], m_out_sp[b])
            cDVE.tick(dve.e.tensor_tensor(out=outb[b][:], in0=gs[b][:],
                                          in1=st16[b][:], op=ALU.mult))
            m_st_dve[b] = cDVE.n
            m_gs_dve[b] = cDVE.n
            dve.need(cDVE)
            dve.need(sXD[b], m_xl)
            cDVE.tick(dve.e.tensor_add(out=outb[b][:], in0=outb[b][:],
                                       in1=xd[b][:]))
            m_xd_dve[b] = cDVE.n
            m_out = cDVE.n
            sp.need(cDVE, m_out)
            sY[b].tick(sp.e.dma_start(out=y_out[io * 128:(io + 1) * 128, :],
                                      in_=outb[b][:]))
            m_out_sp[b] = sY[b].n

    for c in sync_sems:
        sp.need(c, c.n)
    for c in pool_sems:
        pool.need(c, c.n)

    ctxD.close()
    ctx.close()
    return nc, tap_outs


def _consts():
    scanm = np.ones((128, TH), dtype=np.float32)
    scanm[:, ::L] = 0.0
    masksc = np.tril(np.ones((64, 64), dtype=np.float32))
    masksc = np.triu(np.ones((64, 64), dtype=np.float32))
    tril = (np.arange(128)[:, None] < np.arange(128)[None, :]).astype(ml_dtypes.bfloat16)
    return {
        "c_ident": np.eye(128, dtype=ml_dtypes.bfloat16),
        "c_scanm": scanm,
        "c_masksc": masksc,
        "c_tril": tril,
        "c_iota8": np.broadcast_to(np.arange(8, dtype=np.float32), (128, 8)).copy(),
        "c_iota32": np.broadcast_to(np.tile(np.arange(8, dtype=np.float32), 4),
                                    (128, 32)).copy(),
        "c_mask84": np.broadcast_to(
            np.where(np.arange(32) % 8 == 0, 0.0, 1.0).astype(np.float32),
            (128, 32)).copy(),
        "c_ones8": np.ones((128, 8), dtype=np.float32),
        "c_onesc": np.ones((128, 1), dtype=ml_dtypes.bfloat16),
        "c_onesr": np.ones((1, 128), dtype=np.float32),
        "c_onesr16": np.ones((1, 768), dtype=ml_dtypes.bfloat16),
    }


def make_in_maps(inputs, no_coll=False, coreoffs=None):
    x = np.ascontiguousarray(
        np.asarray(inputs["L0_attn_hidden_states"], dtype=np.float32)
        .reshape(2 * 16384, D))
    com = _consts()
    wqT = np.ascontiguousarray(np.asarray(inputs["L0_attn_Wq"], np.float32).T)
    wfT = np.ascontiguousarray(np.asarray(inputs["L0_attn_Wf"], np.float32).T)
    wiT = np.ascontiguousarray(np.asarray(inputs["L0_attn_Wi"], np.float32).T)
    woT = np.ascontiguousarray(np.asarray(inputs["L0_attn_Wo"], np.float32).T)
    gw = np.ascontiguousarray(
        np.asarray(inputs["L0_attn_g_weight"], np.float32).reshape(NF, 128))
    rw = np.ascontiguousarray(np.asarray(inputs["L0_ffn_router_w"], np.float32))
    w1 = np.ascontiguousarray(np.asarray(inputs["L0_ffn_w1"], np.float32))
    b1 = np.ascontiguousarray(
        np.asarray(inputs["L0_ffn_b1"], np.float32)
        .reshape(E, FF // 128, 128).transpose(0, 2, 1))
    w2 = np.ascontiguousarray(np.asarray(inputs["L0_ffn_w2"], np.float32))
    b2 = np.ascontiguousarray(np.asarray(inputs["L0_ffn_b2"], np.float32))
    in_maps = []
    for c in range(8):
        own = np.ascontiguousarray(x[c * T:(c + 1) * T])
        if c % 4 == 0:
            halo = np.zeros((HALO, D), dtype=np.float32)
        else:
            halo = np.ascontiguousarray(x[c * T - HALO:c * T])
        cmsk = (np.arange(8) < c).astype(np.float32).reshape(8, 1)
        m = dict(x_own=own, x_halo=halo, wqT=wqT, wfT=wfT, wiT=wiT, woT=woT,
                 gwin=gw, rwin=rw, w1in=w1, b1in=b1, w2in=w2, b2in=b2,
                 c_cmask=np.ascontiguousarray(cmsk), **com)
        if no_coll:
            m["dbg_coreoff"] = np.ascontiguousarray(
                coreoffs[c].reshape(1, 8).astype(np.float32))
        in_maps.append(m)
    return in_maps


def kernel(**inputs):
    nc, _ = build()
    in_maps = make_in_maps(inputs)
    res = run_bass_kernel_spmd(nc, in_maps, core_ids=list(range(8)))
    y = np.concatenate([np.asarray(r["y"]) for r in res.results], axis=0)
    return y.reshape(2, 16384, D)


# version 15
# speedup vs baseline: 1.0313x; 1.0313x over previous
import math
import numpy as np
import ml_dtypes
import concourse.bass as bass
import concourse.mybir as mybir
from concourse.bass_core_rust import IndirectOffsetOnAxis
from concourse.bass_utils import run_bass_kernel_spmd
from contextlib import ExitStack

F32 = mybir.dt.float32
BF16 = mybir.dt.bfloat16
I32 = mybir.dt.int32
AF = mybir.ActivationFunctionType
ALU = mybir.AluOpType
AX = mybir.AxisListType

T = 4096
HALO = 128
TH = T + HALO
NT = 32
NTH = 33
D = 1024
NF = 8
E = 8
FF = 4096
ECAP = 768
NSLOT = E * ECAP
NH = 8
DF = 128
L = 64
NCH = TH // L
XW = 2048
CAP = 4096
LN_DFI = math.log(DF ** -0.5)


class Ctr:
    def __init__(self, nc, name, step=1):
        self.sem = nc.alloc_semaphore(name=name)
        self.step = step
        self.n = 0

    def tick(self, instr):
        instr.then_inc(self.sem, self.step)
        self.n += self.step
        return self.n


class Eng:
    def __init__(self, eng):
        self.e = eng
        self.seen = {}

    def need(self, ctr, val=None):
        if val is None:
            val = ctr.n
        if val <= 0:
            return
        if self.seen.get(id(ctr), -1) >= val:
            return
        self.e.wait_ge(ctr.sem, val)
        self.seen[id(ctr)] = val


def build(no_coll=False, taps=()):
    nc = bass.Bass("TRN2", target_bir_lowering=False, debug=False,
                   dynamic_dma_scratch_size=16384)

    x_own = nc.dram_tensor("x_own", [T, D], F32, kind="ExternalInput").ap()
    x_halo = nc.dram_tensor("x_halo", [HALO, D], F32, kind="ExternalInput").ap()
    wqT = nc.dram_tensor("wqT", [D, D], F32, kind="ExternalInput").ap()
    wfT = nc.dram_tensor("wfT", [D, D], F32, kind="ExternalInput").ap()
    wiT = nc.dram_tensor("wiT", [D, D], F32, kind="ExternalInput").ap()
    woT = nc.dram_tensor("woT", [D, D], F32, kind="ExternalInput").ap()
    gwin = nc.dram_tensor("gwin", [NF, 128], F32, kind="ExternalInput").ap()
    rwin = nc.dram_tensor("rwin", [D, E], F32, kind="ExternalInput").ap()
    w1in = nc.dram_tensor("w1in", [E, D, FF], F32, kind="ExternalInput").ap()
    b1in = nc.dram_tensor("b1in", [E, 128, FF // 128], F32, kind="ExternalInput").ap()
    w2in = nc.dram_tensor("w2in", [E, FF, D], F32, kind="ExternalInput").ap()
    b2in = nc.dram_tensor("b2in", [E, D], F32, kind="ExternalInput").ap()
    c_ident = nc.dram_tensor("c_ident", [128, 128], BF16, kind="ExternalInput").ap()
    c_scanm = nc.dram_tensor("c_scanm", [128, TH], F32, kind="ExternalInput").ap()
    c_masksc = nc.dram_tensor("c_masksc", [64, 64], F32, kind="ExternalInput").ap()
    c_tril = nc.dram_tensor("c_tril", [128, 128], BF16, kind="ExternalInput").ap()
    c_iota8 = nc.dram_tensor("c_iota8", [128, 8], F32, kind="ExternalInput").ap()
    c_iota32 = nc.dram_tensor("c_iota32", [128, 32], F32, kind="ExternalInput").ap()
    c_mask84 = nc.dram_tensor("c_mask84", [128, 32], F32, kind="ExternalInput").ap()
    c_ones8 = nc.dram_tensor("c_ones8", [128, 8], F32, kind="ExternalInput").ap()
    c_onesc = nc.dram_tensor("c_onesc", [128, 1], BF16, kind="ExternalInput").ap()
    c_onesr = nc.dram_tensor("c_onesr", [1, 128], F32, kind="ExternalInput").ap()
    c_onesr16 = nc.dram_tensor("c_onesr16", [1, 768], BF16, kind="ExternalInput").ap()
    c_cmask = nc.dram_tensor("c_cmask", [8, 1], F32, kind="ExternalInput").ap()
    if no_coll:
        dbg_coreoff = nc.dram_tensor("dbg_coreoff", [1, 8], F32,
                                     kind="ExternalInput").ap()

    y_out = nc.dram_tensor("y", [T, D], F32, kind="ExternalOutput").ap()

    tap_outs = {}

    def tap(name, shape, dt=F32):
        tap_outs[name] = nc.dram_tensor("tap_" + name, shape, dt,
                                        kind="ExternalOutput").ap()
        return tap_outs[name]

    dram = lambda name, shape, dt: nc.dram_tensor(name, shape, dt,
                                                  kind="Internal").ap()
    xg_dram = dram("xg_dram", [NSLOT, D], BF16)
    moe_dram = dram("moe_dram", [NSLOT, D], BF16)
    q_dram = dram("q_dram", [D, TH], BF16)
    vraw_dram = dram("vraw_dram", [D, TH], F32)
    vtok_dram = dram("vtok_dram", [TH, D], BF16)
    ofm_dram = dram("ofm_dram", [D, T], BF16)
    s_dram = dram("s_dram", [T, D], BF16)
    cc_in = dram("cc_in", [1, 8], F32)
    cc_out = dram("cc_out", [8, 8], F32)

    rbound = nc.gpsimd.alloc_register("rbound")
    nc.gpsimd.reg_mov(rbound, NSLOT - 1)
    pe = Eng(nc.tensor)
    dve = Eng(nc.vector)
    act = Eng(nc.scalar)
    pool = Eng(nc.gpsimd)
    sp = Eng(nc.sync)

    cPE = Ctr(nc, "cPE")
    cDVE = Ctr(nc, "cDVE")
    cACT = Ctr(nc, "cACT")
    cCC = Ctr(nc, "cCC", 1)
    sync_sems = []
    pool_sems = []

    def DS(name, q="sync"):
        c = Ctr(nc, name, 16)
        (sync_sems if q == "sync" else pool_sems).append(c)
        return c

    sW = DS("sW")
    sWP = DS("sWP", "pool")
    sXT = [DS(f"sXT{b}", "pool") for b in range(2)]
    sXG = [DS(f"sXG{b}", "pool") for b in range(2)]
    sTAP = DS("sTAP")
    sCC1 = DS("sCC1", "pool")
    sCC2 = DS("sCC2", "pool")
    sQ = [DS(f"sQ{b}") for b in range(2)]
    sVB = [DS(f"sVB{b}") for b in range(2)]
    sVV = [DS(f"sVV{b}") for b in range(2)]
    sBL2 = [DS(f"sBL{b}") for b in range(2)]
    sOF2 = [DS(f"sOF{b}") for b in range(2)]
    sKT2 = [DS(f"sKT{b}") for b in range(2)]
    sWO = [DS(f"sWO{b}") for b in range(2)]
    sOSL = [DS(f"sOSL{b}") for b in range(2)]
    sST = DS("sST")
    sW1 = [DS(f"sW1{b}", "pool") for b in range(2)]
    sW2 = [DS(f"sW2{b}", "pool") for b in range(2)]
    sXGT = DS("sXGT")
    sB1 = DS("sB1")
    sB2s = DS("sB2s")
    sMO = [DS(f"sMO{b}") for b in range(2)]
    sXD = [DS(f"sXD{b}") for b in range(2)]
    sSL = [DS(f"sSL{b}") for b in range(2)]
    sMG = [DS(f"sMG{b}", "pool") for b in range(2)]
    sY = [DS(f"sY{b}") for b in range(2)]
    sCO = DS("sCO")

    ctx = ExitStack()

    def barrier():
        for g in (pe, dve, act, sp, pool):
            for c in (cPE, cDVE, cACT):
                g.need(c, c.n)
            for c in sync_sems + pool_sems:
                g.need(c, c.n)

    def sb(name, shape, dt):
        return ctx.enter_context(nc.sbuf_tensor(name, shape, dt))

    ident16 = sb("ident16", [128, 128], BF16)
    iota8 = sb("iota8", [128, 8], F32)
    ones8 = sb("ones8", [128, 8], F32)
    onesc = sb("onesc", [128, 1], BF16)
    onesr = sb("onesr", [1, 128], F32)
    cmask = sb("cmask", [8, 1], F32)
    gw_sb = sb("gw_sb", [128, 8], F32)
    rwh = sb("rwh", [128, 64], BF16)
    rwl = sb("rwl", [128, 64], BF16)
    slotix = sb("slotix", [128, NT], I32)
    gate_sb = sb("gate_sb", [128, NT], F32)
    pos_sb = sb("pos_sb", [128, NT], F32)
    oh_sb = sb("oh_sb", [128, NT * 8], BF16)
    runoff = sb("runoff", [1, 8], F32)
    coreoff = sb("coreoff", [1, 8], F32)
    coff_rep = sb("coff_rep", [128, 8], F32)
    actb = sb("actb", [128, 3], F32)

    for dst, src in [(ident16, c_ident), (iota8, c_iota8), (ones8, c_ones8),
                     (onesc, c_onesc), (onesr, c_onesr), (cmask, c_cmask)]:
        sW.tick(sp.e.dma_start(out=dst[:], in_=src))
    for ci in range(NF):
        sW.tick(sp.e.dma_start(out=gw_sb[:, ci:ci + 1],
                               in_=gwin[ci:ci + 1, :].rearrange("a p -> p a")))

    ctx1 = ExitStack()
    trilS = ctx1.enter_context(nc.sbuf_tensor("trilS", [128, 128], BF16))
    sW.tick(sp.e.dma_start(out=trilS[:], in_=c_tril))
    rwf = ctx1.enter_context(nc.sbuf_tensor("rwf", [128, 64], F32))
    rdiff = ctx1.enter_context(nc.sbuf_tensor("rdiff", [128, 64], F32))
    sW.tick(sp.e.dma_start(out=rwf[:].rearrange("p (c e) -> p c e", e=8),
                           in_=rwin.rearrange("(c p) e -> p c e", p=128)))
    mW_consts = sW.n
    dve.need(sW)
    cDVE.tick(dve.e.tensor_copy(out=rwh[:], in_=rwf[:]))
    dve.need(cDVE)
    cDVE.tick(dve.e.tensor_sub(out=rdiff[:], in0=rwf[:], in1=rwh[:]))
    dve.need(cDVE)
    cDVE.tick(dve.e.tensor_copy(out=rwl[:], in_=rdiff[:]))
    mW_rw = cDVE.n
    cDVE.tick(dve.e.memset(runoff[:], 0.0))
    mW_runoff0 = cDVE.n
    cDVE.tick(dve.e.memset(actb[:, 0:1], 1e-5))
    cDVE.tick(dve.e.memset(actb[:, 1:2], 1.0))
    cDVE.tick(dve.e.memset(actb[:, 2:3], LN_DFI))
    mW_actb = cDVE.n

    xnfm = ctx1.enter_context(nc.sbuf_tensor("xnfm", [128, NF * XW], BF16))

    ctxA = ExitStack()
    sbA = lambda n, s, d: ctxA.enter_context(nc.sbuf_tensor(n, s, d))
    psA = lambda n, s, d=F32: ctxA.enter_context(nc.psum_tensor(n, s, d))

    GT = 4
    GW = GT * 1024
    xtg = [sbA(f"xtg{b}", [128, GW], F32) for b in range(2)]
    sqg = sbA("sqg", [128, 1024], F32)
    hig = [sbA(f"hig{b}", [128, GW], BF16) for b in range(2)]
    logt = [sbA(f"logt{b}", [128, GW], BF16) for b in range(2)]
    loslabg = [sbA(f"loslabg{b}", [128, GW], BF16) for b in range(2)]
    statg = [sbA(f"statg{b}", [128, 32], F32) for b in range(2)]
    prb = [sbA(f"prb{b}", [128, 32], F32) for b in range(2)]
    ism = [sbA(f"ism{b}", [128, 32], F32) for b in range(2)]
    cumg = [sbA(f"cumg{b}", [128, 32], F32) for b in range(2)]
    posmg = [sbA(f"posmg{b}", [128, 32], F32) for b in range(2)]
    s4 = [sbA(f"s4_{b}", [128, 16], F32) for b in range(2)]
    runvec = sbA("runvec", [1, 32], F32)
    cnt4 = sbA("cnt4", [1, 32], F32)
    iota32 = sbA("iota32", [128, 32], F32)
    mask84 = sbA("mask84", [128, 32], F32)
    zb16 = sbA("zb16", [128, D], BF16)
    psTr = [psA(f"psTrA{b}", [128, 128], BF16) for b in range(2)]
    psLg = psA("psLg", [128, 32], F32)
    psPos = [psA(f"psPos{b}", [128, 32], F32) for b in range(2)]
    psCnt = psA("psCnt", [1, 32], F32)

    ctxB0 = ExitStack()
    sbB0 = lambda n, s, d: ctxB0.enter_context(nc.sbuf_tensor(n, s, d))
    psB0 = lambda n, s, d=F32: ctxB0.enter_context(nc.psum_tensor(n, s, d))
    wq_sb = sbB0("wq_sb", [128, NF * D], BF16)
    wf_sb = sbB0("wf_sb", [128, NF * D], BF16)
    wi_sb = sbB0("wi_sb", [128, NF * D], BF16)
    psQ = [psB0(f"psB0q{b}", [128, 512], F32) for b in range(2)]
    qb16 = [sbB0(f"qb16{b}", [128, 512], BF16) for b in range(2)]
    ub = [sbB0(f"ub{b}", [128, 512], F32) for b in range(2)]
    vb = [sbB0(f"vb{b}", [128, 512], F32) for b in range(2)]
    vvb = [sbB0(f"vvb{b}", [128, 1024], BF16) for b in range(2)]
    m_qb_sp = [0, 0]
    m_vb_sp = [0, 0]
    m_vv_sp = [0, 0]
    m_psq_ev = [0, 0]
    m_psv_ev = [0, 0]
    m_ub_act = [0, 0]

    SSQ, SS1, SMEAN, SRSTD, SM, SZ, SGATE, SEVAL = range(8)
    QPOS, QBAD, QOKL, QSLT = range(4)

    sA8 = DS("sA8")
    cSP_A = sA8
    cDVE.tick(dve.e.memset(zb16[:], 0.0))
    mA_zb = cDVE.n
    sZX = DS("sZX")
    sp.need(cDVE, mA_zb)
    for r0 in range(0, NSLOT, 128):
        sZX.tick(sp.e.dma_start(out=xg_dram[r0:r0 + 128, :], in_=zb16[:]))
    mA_zx = sZX.n
    pool.need(sZX, mA_zx)
    sA8.tick(sp.e.dma_start(out=iota32[:], in_=c_iota32))
    sA8.tick(sp.e.dma_start(out=mask84[:], in_=c_mask84))
    mA_c = sA8.n

    m_xnfm_tile = [0] * NTH
    m_runoff_dve = mW_runoff0

    sXT[0].tick(pool.e.dma_start(out=xtg[0][:, 0:1024], in_=x_halo))
    m_ld0 = sXT[0].n
    act.need(sXT[0], m_ld0)
    cACT.tick(act.e.activation(out=sqg[:], in_=xtg[0][:, 0:1024], func=AF.Square,
                               accum_out=statg[0][:, SSQ:SSQ + 1]))
    m_sq0 = cACT.n
    dve.need(sXT[0], m_ld0)
    cDVE.tick(dve.e.reduce_sum(out=statg[0][:, SS1:SS1 + 1],
                               in_=xtg[0][:, 0:1024], axis=AX.X))
    dve.need(cDVE)
    cDVE.tick(dve.e.tensor_scalar(out=statg[0][:, SMEAN:SMEAN + 1],
                                  in0=statg[0][:, SS1:SS1 + 1],
                                  scalar1=1.0 / D, scalar2=None, op0=ALU.mult))
    dve.need(cDVE)
    cDVE.tick(dve.e.tensor_tensor(out=statg[0][:, SRSTD:SRSTD + 1],
                                  in0=statg[0][:, SMEAN:SMEAN + 1],
                                  in1=statg[0][:, SMEAN:SMEAN + 1], op=ALU.mult))
    dve.need(cACT, m_sq0)
    dve.need(cDVE)
    cDVE.tick(dve.e.scalar_tensor_tensor(
        out=statg[0][:, SRSTD:SRSTD + 1], in0=statg[0][:, SSQ:SSQ + 1],
        scalar=1.0 / D, in1=statg[0][:, SRSTD:SRSTD + 1],
        op0=ALU.mult, op1=ALU.subtract))
    act.need(cDVE)
    act.need(cDVE, mW_actb)
    cACT.tick(act.e.activation(out=statg[0][:, SRSTD:SRSTD + 1],
                               in_=statg[0][:, SRSTD:SRSTD + 1], func=AF.Ln,
                               bias=actb[:, 0:1], scale=1.0))
    act.need(cACT)
    cACT.tick(act.e.activation(out=statg[0][:, SRSTD:SRSTD + 1],
                               in_=statg[0][:, SRSTD:SRSTD + 1],
                               func=AF.Exp, scale=-0.5))
    m_rstd0 = cACT.n
    dve.need(cACT, m_rstd0)
    cDVE.tick(dve.e.tensor_scalar(out=xtg[0][:, 0:1024], in0=xtg[0][:, 0:1024],
                                  scalar1=statg[0][:, SMEAN:SMEAN + 1],
                                  scalar2=statg[0][:, SRSTD:SRSTD + 1],
                                  op0=ALU.subtract, op1=ALU.mult))
    dve.need(cDVE)
    cDVE.tick(dve.e.tensor_copy(out=hig[0][:, 0:1024], in_=xtg[0][:, 0:1024]))
    m_hi0 = cDVE.n
    m_psTr_evD = [0, 0]
    m_psTr_evA = [0, 0]
    for ci in range(NF):
        pp = ci % 2
        pe.need(cDVE, m_hi0)
        pe.need(cDVE, m_psTr_evD[pp])
        cPE.tick(pe.e.transpose(psTr[pp][:], hig[0][:, ci * 128:(ci + 1) * 128],
                                ident16[:]))
        dve.need(cPE)
        cDVE.tick(dve.e.tensor_copy(
            out=xnfm[:, ci * XW: ci * XW + 128], in_=psTr[pp][:]))
        m_psTr_evD[pp] = cDVE.n
    m_xnfm_tile[0] = cDVE.n
    m_hi_pe = cPE.n

    m_xt_act = [m_sq0, 0]
    m_xt_dve = [m_hi0, 0]
    m_stat_dve = [0, 0]
    m_sq_act = m_sq0
    m_hi_dve = [m_hi0, 0]
    m_hi_used_pe = [m_hi_pe, 0]
    m_lo_used_pe = [0, 0]
    m_loslab_pe = [0, 0]
    m_lg_dve = 0
    m_ohv = [0, 0]
    m_pos_dve = [0, 0]
    m_cnt_dve = 0
    m_s4_dve = [0, 0]
    m_hi_pl = [0, 0]
    m_hiv = [0, 0]
    m_lov = [0, 0]

    def a_h1(g):
        nonlocal m_sq_act
        b = g % 2
        t0 = g * GT * 128
        pool.need(cACT, m_xt_act[b])
        pool.need(cDVE, m_xt_dve[b])
        sXT[b].tick(pool.e.dma_start(
            out=xtg[b][:].rearrange("p (t d) -> p t d", d=1024),
            in_=x_own[t0:t0 + GT * 128, :].rearrange("(t p) d -> p t d", p=128)))
        m_ld = sXT[b].n

        for k in range(GT):
            act.need(sXT[b], m_ld)
            act.need(cACT, m_sq_act)
            act.need(cDVE, m_stat_dve[b])
            cACT.tick(act.e.activation(
                out=sqg[:], in_=xtg[b][:, k * 1024:(k + 1) * 1024],
                func=AF.Square,
                accum_out=statg[b][:, k * 8 + SSQ: k * 8 + SSQ + 1]))
            m_sq_act = cACT.n
        m_sq = cACT.n
        dve.need(sXT[b], m_ld)
        dve.need(cDVE, m_stat_dve[b])
        cDVE.tick(dve.e.tensor_reduce(
            out=statg[b][:, SS1::8].rearrange("p (t o) -> p t o", o=1),
            in_=xtg[b][:].rearrange("p (t d) -> p t d", d=1024),
            axis=AX.X, op=ALU.add))
        dve.need(cDVE)
        cDVE.tick(dve.e.tensor_scalar(out=statg[b][:, SMEAN::8],
                                      in0=statg[b][:, SS1::8],
                                      scalar1=1.0 / D, scalar2=None,
                                      op0=ALU.mult))
        dve.need(cDVE)
        cDVE.tick(dve.e.tensor_tensor(out=statg[b][:, SRSTD::8],
                                      in0=statg[b][:, SMEAN::8],
                                      in1=statg[b][:, SMEAN::8], op=ALU.mult))
        dve.need(cACT, m_sq)
        dve.need(cDVE)
        cDVE.tick(dve.e.scalar_tensor_tensor(
            out=statg[b][:, SRSTD::8], in0=statg[b][:, SSQ::8],
            scalar=1.0 / D, in1=statg[b][:, SRSTD::8],
            op0=ALU.mult, op1=ALU.subtract))
        m_var = cDVE.n
        act.need(cDVE, m_var)
        cACT.tick(act.e.activation(out=statg[b][:, SRSTD::8],
                                   in_=statg[b][:, SRSTD::8], func=AF.Ln,
                                   bias=actb[:, 0:1], scale=1.0))
        act.need(cACT)
        cACT.tick(act.e.activation(out=statg[b][:, SRSTD::8],
                                   in_=statg[b][:, SRSTD::8],
                                   func=AF.Exp, scale=-0.5))
        m_rstd = cACT.n
        for k in range(GT):
            dve.need(cACT, m_rstd)
            dve.need(cDVE)
            cDVE.tick(dve.e.tensor_scalar(
                out=xtg[b][:, k * 1024:(k + 1) * 1024],
                in0=xtg[b][:, k * 1024:(k + 1) * 1024],
                scalar1=statg[b][:, k * 8 + SMEAN: k * 8 + SMEAN + 1],
                scalar2=statg[b][:, k * 8 + SRSTD: k * 8 + SRSTD + 1],
                op0=ALU.subtract, op1=ALU.mult))
        m_xn = cDVE.n
        m_xt_act[b] = m_sq
        dve.need(cDVE, m_xn)
        dve.need(cDVE, m_hi_dve[b])
        dve.need(cPE, m_hi_used_pe[b])
        dve.need(sXG[b], m_hi_pl[b])
        cDVE.tick(dve.e.tensor_copy(out=hig[b][:], in_=xtg[b][:]))
        m_hi = cDVE.n
        m_hiv[b] = m_hi
        dve.need(cPE, m_lo_used_pe[b])
        dve.need(cDVE)
        cDVE.tick(dve.e.tensor_sub(out=logt[b][:], in0=xtg[b][:], in1=hig[b][:]))
        m_lo = cDVE.n
        m_lov[b] = m_lo
        m_hi_dve[b] = cDVE.n
        m_xt_dve[b] = cDVE.n


    def a_h2(g):
        nonlocal m_runoff_dve, m_cnt_dve, m_lg_dve
        b = g % 2
        m_hi = m_hiv[b]
        m_lo = m_lov[b]
        for kt in range(GT):
            ti = g * GT + kt + 1
            for ci in range(NF):
                pp = ci % 2
                pe.need(cDVE, m_hi)
                pe.need(cDVE, m_psTr_evD[pp])
                pe.need(cACT, m_psTr_evA[pp])
                cPE.tick(pe.e.transpose(
                    psTr[pp][:],
                    hig[b][:, kt * 1024 + ci * 128: kt * 1024 + (ci + 1) * 128],
                    ident16[:]))
                dve.need(cPE)
                wc = (ti * 128) % XW
                cDVE.tick(dve.e.tensor_copy(
                    out=xnfm[:, ci * XW + wc: ci * XW + wc + 128],
                    in_=psTr[pp][:]))
                m_psTr_evD[pp] = cDVE.n
            m_xnfm_tile[ti] = cDVE.n
        m_hi_used_pe[b] = cPE.n
        for kt in range(GT):
            for ci in range(NF):
                pp = ci % 2
                pe.need(cDVE, m_lo)
                pe.need(cDVE, m_psTr_evD[pp])
                pe.need(cACT, m_psTr_evA[pp])
                pe.need(cPE, m_loslab_pe[b])
                cPE.tick(pe.e.transpose(
                    psTr[pp][:],
                    logt[b][:, kt * 1024 + ci * 128: kt * 1024 + (ci + 1) * 128],
                    ident16[:]))
                act.need(cPE)
                cACT.tick(act.e.copy(
                    out=loslabg[b][:, kt * 1024 + ci * 128: kt * 1024 + (ci + 1) * 128],
                    in_=psTr[pp][:]))
                m_psTr_evA[pp] = cACT.n
        m_lo_used_pe[b] = cPE.n
        m_loslab = cACT.n

        pe.need(cACT, m_loslab)
        pe.need(sW, mW_consts)
        pe.need(cDVE, mW_rw)
        pe.need(cDVE, m_lg_dve)
        for kt in range(GT):
            ti = g * GT + kt + 1
            wc = (ti * 128) % XW
            nmm = 0
            for ci in range(NF):
                xb = xnfm[:, ci * XW + wc: ci * XW + wc + 128]
                lb = loslabg[b][:, kt * 1024 + ci * 128: kt * 1024 + (ci + 1) * 128]
                for lhsT, rhs in ((xb, rwh), (xb, rwl), (lb, rwh)):
                    nmm += 1
                    cPE.tick(pe.e.matmul(psLg[:, kt * 8:(kt + 1) * 8], lhsT,
                                         rhs[:, ci * 8:(ci + 1) * 8],
                                         start=(nmm == 1), stop=(nmm == 24)))
        m_lg = cPE.n
        m_loslab_pe[b] = cPE.n

        lg3 = psLg[:].rearrange("p (t e) -> p t e", e=8)
        dve.need(cPE, m_lg)
        dve.need(cDVE, m_s4_dve[b])
        cDVE.tick(dve.e.tensor_reduce(
            out=statg[b][:, SM::8].rearrange("p (t o) -> p t o", o=1),
            in_=lg3, axis=AX.X, op=ALU.max))
        dve.need(cDVE)
        cDVE.tick(dve.e.tensor_tensor(
            out=prb[b][:].rearrange("p (t e) -> p t e", e=8), in0=lg3,
            in1=statg[b][:, SM::8].rearrange("p (t o) -> p t o", o=1)
                .broadcast_to([128, 4, 8]),
            op=ALU.subtract))
        m_lgsub = cDVE.n
        act.need(cDVE, m_lgsub)
        cACT.tick(act.e.activation(out=prb[b][:], in_=prb[b][:], func=AF.Exp))
        m_exp = cACT.n
        dve.need(cACT, m_exp)
        cDVE.tick(dve.e.tensor_reduce(
            out=statg[b][:, SZ::8].rearrange("p (t o) -> p t o", o=1),
            in_=prb[b][:].rearrange("p (t e) -> p t e", e=8),
            axis=AX.X, op=ALU.add))
        dve.need(cDVE)
        cDVE.tick(dve.e.reciprocal(out=statg[b][:, SGATE::8],
                                   in_=statg[b][:, SZ::8]))
        dve.need(cDVE)
        cDVE.tick(dve.e.tensor_tensor(
            out=ism[b][:].rearrange("p (t e) -> p t e", e=8), in0=lg3,
            in1=statg[b][:, SM::8].rearrange("p (t o) -> p t o", o=1)
                .broadcast_to([128, 4, 8]),
            op=ALU.is_ge))
        m_lg_dve = cDVE.n
        dve.need(cDVE)
        dve.need(sA8, mA_c)
        cDVE.tick(dve.e.tensor_tensor_scan(out=cumg[b][:], data0=mask84[:],
                                           data1=ism[b][:], initial=0.0,
                                           op0=ALU.mult, op1=ALU.add))
        dve.need(cDVE)
        cDVE.tick(dve.e.tensor_scalar(out=cumg[b][:], in0=cumg[b][:],
                                      scalar1=1.0, scalar2=None,
                                      op0=ALU.is_equal))
        dve.need(cDVE)
        cDVE.tick(dve.e.tensor_tensor(out=oh_sb[:, g * 32:(g + 1) * 32],
                                      in0=ism[b][:], in1=cumg[b][:],
                                      op=ALU.mult))
        m_oh = cDVE.n
        dve.need(cDVE)
        cDVE.tick(dve.e.tensor_tensor(out=ism[b][:],
                                      in0=oh_sb[:, g * 32:(g + 1) * 32],
                                      in1=iota32[:], op=ALU.mult))
        dve.need(cDVE)
        cDVE.tick(dve.e.tensor_reduce(
            out=statg[b][:, SEVAL::8].rearrange("p (t o) -> p t o", o=1),
            in_=ism[b][:].rearrange("p (t e) -> p t e", e=8),
            axis=AX.X, op=ALU.add))
        m_ohv[b] = m_oh

    def a_h2b(g):
        nonlocal m_runoff_dve, m_cnt_dve
        b = g % 2
        m_oh = m_ohv[b]
        m_hi = m_hiv[b]

        pe.need(cDVE, m_oh)
        pe.need(cDVE, m_pos_dve[b])
        for kt in range(GT):
            cPE.tick(pe.e.matmul(psPos[b][:, kt * 8:(kt + 1) * 8], trilS[:],
                                 oh_sb[:, (g * GT + kt) * 8:(g * GT + kt + 1) * 8],
                                 start=(kt == 0), stop=False))
        pe.need(cDVE, m_cnt_dve)
        cPE.tick(pe.e.matmul(psCnt[:], onesc[:], oh_sb[:, g * 32:(g + 1) * 32],
                             start=True, stop=True))
        m_cntmm = cPE.n
        dve.need(cPE, m_cntmm)
        dve.need(cDVE, m_runoff_dve)
        cDVE.tick(dve.e.tensor_copy(out=cnt4[:], in_=psCnt[:]))
        dve.need(cDVE)
        cDVE.tick(dve.e.tensor_copy(out=runvec[0:1, 0:8], in_=runoff[:]))
        for k in range(1, GT + 1):
            dve.need(cDVE)
            if k < GT:
                cDVE.tick(dve.e.tensor_tensor(
                    out=runvec[0:1, k * 8:(k + 1) * 8],
                    in0=runvec[0:1, (k - 1) * 8: k * 8],
                    in1=cnt4[0:1, (k - 1) * 8: k * 8], op=ALU.add))
            else:
                cDVE.tick(dve.e.tensor_tensor(
                    out=runoff[:], in0=runvec[0:1, (k - 1) * 8: k * 8],
                    in1=cnt4[0:1, (k - 1) * 8: k * 8], op=ALU.add))
        m_runoff_dve = cDVE.n
        m_cnt_dve = cDVE.n
        m_runvec = cDVE.n
        pe.need(cDVE, m_runvec)
        cPE.tick(pe.e.matmul(psPos[b][:], onesr[:], runvec[:],
                             start=False, stop=True))
        m_posmm = cPE.n
        dve.need(cPE, m_posmm)
        cDVE.tick(dve.e.tensor_tensor(out=posmg[b][:],
                                      in0=oh_sb[:, g * 32:(g + 1) * 32],
                                      in1=psPos[b][:], op=ALU.mult))
        m_pos_dve[b] = cDVE.n
        dve.need(cDVE)
        cDVE.tick(dve.e.tensor_reduce(
            out=pos_sb[:, g * GT:(g + 1) * GT]
                .rearrange("p (t o) -> p t o", o=1),
            in_=posmg[b][:].rearrange("p (t e) -> p t e", e=8),
            axis=AX.X, op=ALU.add))
        m_pos = cDVE.n

        dve.need(cDVE, m_pos)
        cDVE.tick(dve.e.tensor_scalar(out=s4[b][:, QBAD::4],
                                      in0=pos_sb[:, g * GT:(g + 1) * GT],
                                      scalar1=float(ECAP), scalar2=None,
                                      op0=ALU.is_ge))
        dve.need(cDVE)
        cDVE.tick(dve.e.tensor_scalar(out=s4[b][:, QOKL::4],
                                      in0=s4[b][:, QBAD::4],
                                      scalar1=-1.0, scalar2=1.0,
                                      op0=ALU.mult, op1=ALU.add))
        dve.need(cDVE)
        cDVE.tick(dve.e.tensor_tensor(out=gate_sb[:, g * GT:(g + 1) * GT],
                                      in0=statg[b][:, SGATE::8],
                                      in1=s4[b][:, QOKL::4], op=ALU.mult))
        dve.need(cDVE)
        cDVE.tick(dve.e.scalar_tensor_tensor(
            out=s4[b][:, QSLT::4], in0=statg[b][:, SEVAL::8],
            scalar=float(ECAP), in1=pos_sb[:, g * GT:(g + 1) * GT],
            op0=ALU.mult, op1=ALU.add))
        m_stat_dve[b] = cDVE.n
        dve.need(cDVE)
        cDVE.tick(dve.e.scalar_tensor_tensor(
            out=s4[b][:, QSLT::4], in0=s4[b][:, QBAD::4], scalar=1.0e6,
            in1=s4[b][:, QSLT::4], op0=ALU.mult, op1=ALU.add))
        dve.need(cDVE)
        cDVE.tick(dve.e.tensor_copy(out=slotix[:, g * GT:(g + 1) * GT],
                                    in_=s4[b][:, QSLT::4]))
        m_slot = cDVE.n
        m_s4_dve[b] = cDVE.n

        for kt in range(GT):
            io = g * GT + kt
            pool.need(cDVE, m_slot)
            pool.need(cDVE, m_hi)
            sXG[b].tick(pool.e.indirect_dma_start(
                out=xg_dram,
                out_offset=IndirectOffsetOnAxis(slotix[:, io:io + 1], 0),
                in_=hig[b][:, kt * 1024:(kt + 1) * 1024], in_offset=None,
                bounds_check=rbound, oob_is_err=False))
        m_hi_pl[b] = sXG[b].n

    def b0_slab(t0, sl):
        tw = t0 % XW
        pe.need(sWP, mW_wproj)
        pe.need(cDVE, m_xnfm_tile[(t0 + sl - 1) // 128])
        for h in range(NH):
            bb = h % 2
            pe.need(cACT, m_psq_ev[bb])
            pe.need(cDVE, m_psv_ev[bb])
            for ci in range(NF):
                cPE.tick(pe.e.matmul(
                    psQ[bb][:, 0:sl],
                    wq_sb[:, ci * D + h * 128: ci * D + (h + 1) * 128],
                    xnfm[:, ci * XW + tw: ci * XW + tw + sl],
                    start=(ci == 0), stop=(ci == NF - 1)))
            m_mmq = cPE.n
            act.need(cPE, m_mmq)
            act.need(sQ[bb], m_qb_sp[bb])
            cACT.tick(act.e.activation(out=qb16[bb][:, 0:sl],
                                       in_=psQ[bb][:, 0:sl], func=AF.Silu))
            m_psq_ev[bb] = cACT.n
            sp.need(cACT)
            sQ[bb].tick(sp.e.dma_start(
                out=q_dram[h * 128:(h + 1) * 128, t0:t0 + sl],
                in_=qb16[bb][:, 0:sl]))
            m_qb_sp[bb] = sQ[bb].n
        for h in range(NH):
            bb = h % 2
            pe.need(cACT, m_psq_ev[bb])
            pe.need(cDVE, m_psv_ev[bb])
            for ci in range(NF):
                cPE.tick(pe.e.matmul(
                    psQ[bb][:, 0:sl],
                    wf_sb[:, ci * D + h * 128: ci * D + (h + 1) * 128],
                    xnfm[:, ci * XW + tw: ci * XW + tw + sl],
                    start=(ci == 0), stop=(ci == NF - 1)))
            m_mmz = cPE.n
            act.need(cPE, m_mmz)
            act.need(cACT, m_ub_act[bb])
            cACT.tick(act.e.activation(out=ub[bb][:, 0:sl], in_=psQ[bb][:, 0:sl],
                                       func=AF.Exp, scale=-1.0))
            m_psq_ev[bb] = cACT.n
            act.need(cACT)
            act.need(sVB[bb], m_vb_sp[bb])
            cACT.tick(act.e.activation(out=vb[bb][:, 0:sl], in_=ub[bb][:, 0:sl],
                                       func=AF.Ln, bias=actb[:, 1:2]))
            m_ub_act[bb] = cACT.n
            sp.need(cACT)
            sVB[bb].tick(sp.e.dma_start(
                out=vraw_dram[h * 128:(h + 1) * 128, t0:t0 + sl],
                in_=vb[bb][:, 0:sl]))
            m_vb_sp[bb] = sVB[bb].n
        for ts in range(sl // 128):
            bb = ts % 2
            tt0 = t0 + ts * 128
            tw2 = tt0 % XW
            for dh in range(2):
                pe.need(cDVE, m_psv_ev[bb])
                pe.need(cACT, m_psq_ev[bb])
                for ci in range(NF):
                    cPE.tick(pe.e.matmul(
                        psQ[bb][:],
                        xnfm[:, ci * XW + tw2: ci * XW + tw2 + 128],
                        wi_sb[:, ci * D + dh * 512: ci * D + (dh + 1) * 512],
                        start=(ci == 0), stop=(ci == NF - 1)))
                m_mmv = cPE.n
                dve.need(cPE, m_mmv)
                dve.need(sVV[bb], m_vv_sp[bb])
                cDVE.tick(dve.e.tensor_copy(
                    out=vvb[bb][:, dh * 512:(dh + 1) * 512], in_=psQ[bb][:]))
                m_psv_ev[bb] = cDVE.n
            sp.need(cDVE, m_psv_ev[bb])
            sVV[bb].tick(sp.e.dma_start(out=vtok_dram[tt0:tt0 + 128, :],
                                        in_=vvb[bb][:]))
            m_vv_sp[bb] = sVV[bb].n

    a_h1(0)
    for dst, srcw in [(wq_sb, wqT), (wf_sb, wfT), (wi_sb, wiT)]:
        sWP.tick(pool.e.dma_start(
            out=dst[:].rearrange("p (c n) -> p c n", c=NF),
            in_=srcw.rearrange("(c p) n -> p c n", p=128)))
    mW_wproj = sWP.n
    for g in range(8):
        a_h2(g)
        if g + 1 < 8:
            a_h1(g + 1)
        b0_slab(g * 512, 512)
        a_h2b(g)
    b0_slab(4096, 128)

    mA_dve = cDVE.n
    mA_counts = m_runoff_dve

    if "oh" in taps:
        tap("oh", [128, NT * 8], BF16)
        sp.need(cDVE, mA_dve)
        sTAP.tick(sp.e.dma_start(out=tap_outs["oh"], in_=oh_sb[:]))
    if "pos" in taps:
        tap("pos", [128, NT])
        sp.need(cDVE, mA_dve)
        sTAP.tick(sp.e.dma_start(out=tap_outs["pos"], in_=pos_sb[:]))
    if "gate" in taps:
        tap("gate", [128, NT])
        sp.need(cDVE, mA_dve)
        sTAP.tick(sp.e.dma_start(out=tap_outs["gate"], in_=gate_sb[:]))
    if "counts" in taps:
        tap("counts", [1, 8])
        sp.need(cDVE, mA_counts)
        sTAP.tick(sp.e.dma_start(out=tap_outs["counts"], in_=runoff[:]))
    if "slot" in taps:
        tap("slot", [128, NT], I32)
        sp.need(cDVE, mA_dve)
        sTAP.tick(sp.e.dma_start(out=tap_outs["slot"], in_=slotix[:]))

    if not no_coll:
        pool.need(cDVE, mA_counts)
        sCC1.tick(pool.e.dma_start(out=cc_in, in_=runoff[:]))
        pool.need(sCC1)
        cCC.tick(pool.e.collective_compute(
            "AllGather", ALU.bypass, replica_groups=[[0, 1, 2, 3, 4, 5, 6, 7]],
            ins=[cc_in], outs=[cc_out]))

    for bb in range(2):
        sp.need(sQ[bb], m_qb_sp[bb])
        sp.need(sVB[bb], m_vb_sp[bb])
        sp.need(sVV[bb], m_vv_sp[bb])
    if "q" in taps:
        tap("q", [D, TH], BF16)
        sTAP.tick(sp.e.dma_start(out=tap_outs["q"], in_=q_dram))
    if "vraw" in taps:
        tap("vraw", [D, TH])
        sTAP.tick(sp.e.dma_start(out=tap_outs["vraw"], in_=vraw_dram))
    if "vtok" in taps:
        tap("vtok", [TH, D], BF16)
        sTAP.tick(sp.e.dma_start(out=tap_outs["vtok"], in_=vtok_dram))

    ctxB0.close()
    ctxA.close()
    ctx1.close()
    barrier()

    SEG = 22
    SEGT = SEG * L
    NSEGG = SEG // 2
    ctxB = ExitStack()
    sbB = lambda n, s, d: ctxB.enter_context(nc.sbuf_tensor(n, s, d))
    psB = lambda n, s, d=F32: ctxB.enter_context(nc.psum_tensor(n, s, d))

    scanm = sbB("scanm", [128, TH], F32)
    masksc = sbB("masksc", [64, 64], F32)
    qhS = [[sbB(f"qhS{p}_{h}", [128, SEGT], BF16) for h in range(2)]
           for p in range(2)]
    vrS = [[sbB(f"vrS{p}_{h}", [128, SEGT], F32) for h in range(2)]
           for p in range(2)]
    vvS = [[sbB(f"vvS{p}_{h}", [128, SEGT], BF16) for h in range(2)]
           for p in range(2)]
    qtS = [[sbB(f"qtS{p}_{h}", [128, SEGT], BF16) for h in range(2)]
           for p in range(2)]
    ktS = [[sbB(f"ktS{p}_{h}", [128, SEGT], BF16) for h in range(2)]
           for p in range(2)]
    khS = [[sbB(f"khS{p}_{h}", [128, SEGT], BF16) for h in range(2)]
           for p in range(2)]
    ktokS = [[sbB(f"ktokS{p}_{h}", [128, SEGT], BF16) for h in range(2)]
             for p in range(2)]
    BnegS = [[sbB(f"BnegS{p}_{h}", [128, SEGT], F32) for h in range(2)]
             for p in range(2)]
    EmBS = [[sbB(f"EmBS{p}_{h}", [128, SEG * (L + 1)], F32) for h in range(2)]
            for p in range(2)]
    aendS = [[sbB(f"aendS{p}_{h}", [128, SEG], F32) for h in range(2)]
             for p in range(2)]
    S16 = [sbB(f"S16B{h}", [128, 128], BF16) for h in range(2)]
    sc16 = [sbB(f"sc16B{h}", [128, 64], BF16) for h in range(2)]
    ofh = [sbB(f"ofhB{h}", [128, T], BF16) for h in range(2)]
    psSC = [psB(f"psSCB{h}", [64, 64], F32) for h in range(2)]
    psO = [psB(f"psOB{h}", [128, 64], F32) for h in range(2)]
    psDS = [[psB(f"psDSB{h}_{b}", [128, 256], F32) for b in range(2)]
            for h in range(2)]

    sBL2[0].tick(sp.e.dma_start(out=scanm[:], in_=c_scanm))
    sBL2[0].tick(sp.e.dma_start(out=masksc[:], in_=c_masksc))
    mB_scanm_ld = sBL2[0].n
    for p in range(2):
        for h2 in range(2):
            EmBr = EmBS[p][h2][:].rearrange("p (g n) -> p g n", n=L + 1)
            cDVE.tick(dve.e.memset(EmBr[:, :, 0:1], 1.0))
    mB_emb1 = cDVE.n

    NSEG = 3
    NST = (NH // 2) * NSEG
    cPOOL = Ctr(nc, "cPOOL")

    mP_pe = [0, 0]
    mP_dve = [0, 0]
    mP_act = [0, 0]
    mP_pool = [0, 0]
    m_ld = [0, 0]
    m_ofh_sp = [0, 0]

    def b_loads(st):
        p = st % 2
        hp, s = divmod(st, NSEG)
        t0 = s * SEGT
        sp.need(cPE, mP_pe[p])
        sp.need(cDVE, mP_dve[p])
        sp.need(cACT, mP_act[p])
        sp.need(cPOOL, mP_pool[p])
        vt = vtok_dram.rearrange("(j p) d -> p j d", p=128)
        for h2 in range(2):
            h = hp * 2 + h2
            sBL2[p].tick(sp.e.dma_start(out=qhS[p][h2][:],
                                    in_=q_dram[h * 128:(h + 1) * 128, t0:t0 + SEGT]))
            sBL2[p].tick(sp.e.dma_start(out=vrS[p][h2][:],
                                    in_=vraw_dram[h * 128:(h + 1) * 128, t0:t0 + SEGT]))
            vvr = vvS[p][h2][:].rearrange("p (j d) -> p j d", d=128)
            g0 = s * NSEGG
            sBL2[p].tick(sp.e.dma_start(out=vvr[:, 0:6, :],
                                    in_=vt[:, g0:g0 + 6, h * 128:(h + 1) * 128]))
            sBL2[p].tick(sp.e.dma_start(out=vvr[:, 6:NSEGG, :],
                                    in_=vt[:, g0 + 6:g0 + NSEGG, h * 128:(h + 1) * 128]))
        m_ld[p] = sBL2[p].n

    def b_prep_emit(st):
        p = st % 2
        hp, s = divmod(st, NSEG)
        ops = []
        marks = {}

        def scan_h(h2):
            def t():
                dve.need(sBL2[0], sBL2[0].n)
                dve.need(sBL2[1], sBL2[1].n)
                dve.need(cPE, mP_pe[p])
                dve.need(cPOOL, mP_pool[p])
                cDVE.tick(dve.e.tensor_tensor_scan(
                    out=BnegS[p][h2][:], data0=scanm[:, s * SEGT:(s + 1) * SEGT],
                    data1=vrS[p][h2][:], initial=0.0,
                    op0=ALU.mult, op1=ALU.add))
                marks[("scan", h2)] = cDVE.n
            return t

        def emb_h(h2):
            def t():
                act.need(cDVE, marks[("scan", h2)])
                act.need(cDVE, mB_emb1)
                act.need(cPOOL, mP_pool[p])
                EmBr = EmBS[p][h2][:].rearrange("p (g n) -> p g n", n=L + 1)
                cACT.tick(act.e.activation(
                    out=EmBr[:, :, 1:L + 1],
                    in_=BnegS[p][h2][:].rearrange("p (g n) -> p g n", n=L),
                    func=AF.Exp, scale=1.0))
                cACT.tick(act.e.activation(
                    out=aendS[p][h2][:].rearrange("p (g o) -> p g o", o=1),
                    in_=BnegS[p][h2][:].rearrange("p (g n) -> p g n", n=L)[:, :, L - 1:L],
                    func=AF.Exp, scale=-1.0))
                act.need(cACT, cACT.n)
                cACT.tick(act.e.activation(out=BnegS[p][h2][:], in_=BnegS[p][h2][:],
                                           func=AF.Exp, scale=-1.0,
                                           bias=actb[:, 2:3]))
                marks[("emb", h2)] = cACT.n
            return t

        def kq_h(h2):
            def t():
                pool.need(cACT, marks[("emb", h2)])
                pool.need(cPE, mP_pe[p])
                EmBr = EmBS[p][h2][:].rearrange("p (g n) -> p g n", n=L + 1)
                cPOOL.tick(pool.e.tensor_tensor(
                    out=ktS[p][h2][:].rearrange("p (g n) -> p g n", n=L),
                    in0=EmBr[:, :, 1:L + 1], in1=EmBr[:, :, 0:L],
                    op=ALU.subtract))
                pool.need(cPOOL, cPOOL.n)
                cPOOL.tick(pool.e.tensor_tensor(
                    out=khS[p][h2][:].rearrange("p (g n) -> p g n", n=L),
                    in0=ktS[p][h2][:].rearrange("p (g n) -> p g n", n=L),
                    in1=aendS[p][h2][:].rearrange("p (g o) -> p g o", o=1)
                        .broadcast_to([128, SEG, L]),
                    op=ALU.mult))
                cPOOL.tick(pool.e.tensor_tensor(
                    out=qtS[p][h2][:], in0=qhS[p][h2][:], in1=BnegS[p][h2][:],
                    op=ALU.mult))
                marks[("kq", h2)] = cPOOL.n
            return t

        def tr_h(h2, g):
            def t():
                sp.need(cPOOL, marks[("kq", h2)])
                sp.need(cPE, mP_pe[p])
                sKT2[p].tick(sp.e.dma_start(
                    out=ktokS[p][h2][:, g * 128:(g + 1) * 128],
                    in_=khS[p][h2][:, g * 128:(g + 1) * 128],
                    transpose=True))
                marks[("ktr",)] = sKT2[p].n
            return t

        for h2 in range(2):
            ops.append(scan_h(h2))
            ops.append(emb_h(h2))
            ops.append(kq_h(h2))
        for g in range(NSEGG):
            for h2 in range(2):
                ops.append(tr_h(h2, g))
        return ops, marks

    m_sc_dve = [0, 0]
    m_psds_dve = [[0, 0], [0, 0]]
    m_o_dve = [0, 0]

    def b_chunks(st, prep_ops, prep_marks):
        p = st % 2
        hp, s = divmod(st, NSEG)
        nonlocal_marks = b_state[hp]
        m_state = nonlocal_marks["m_state"]
        m_omm = nonlocal_marks["m_omm"]
        pq = list(prep_ops)
        it = 0
        for cl in range(SEG):
            cg = s * SEG + cl
            own = cg >= 2
            j, po = cl // 2, (cl % 2) * 64
            dsb = cl % 2
            dsls = [psDS[h2][dsb][:, ((cl // 2) % 2) * 128:
                                  ((cl // 2) % 2 + 1) * 128] for h2 in range(2)]
            vvv = [vvS[p][h2][:].rearrange("p (j d) -> p j d", d=128)
                   [po:po + 64, j, :] for h2 in range(2)]
            m_ds = [0, 0]
            m_scmm = [0, 0]
            for h2 in range(2):
                pe.need(sKT2[p], prep_marks_cur[("ktr",)])
                pe.need(cPOOL, prep_marks_cur[("kq", h2)])
                pe.need(cDVE, m_psds_dve[h2][dsb])
                cPE.tick(pe.e.matmul(
                    dsls[h2], ktokS[p][h2][po:po + 64, j * 128:(j + 1) * 128],
                    vvv[h2], start=True, stop=True))
                m_ds[h2] = cPE.n
            if own:
                for h2 in range(2):
                    pe.need(cDVE, m_sc_dve[h2])
                    cPE.tick(pe.e.matmul(psSC[h2][:],
                                         ktS[p][h2][:, cl * 64:(cl + 1) * 64],
                                         qtS[p][h2][:, cl * 64:(cl + 1) * 64],
                                         start=True, stop=True))
                    m_scmm[h2] = cPE.n
                for h2 in range(2):
                    dve.need(cPE, m_scmm[h2])
                    cDVE.tick(dve.e.tensor_tensor(out=sc16[h2][po:po + 64, :],
                                                  in0=psSC[h2][:],
                                                  in1=masksc[:], op=ALU.mult))
                    m_sc_dve[h2] = cDVE.n
                for h2 in range(2):
                    pe.need(cDVE, m_sc_dve[h2])
                    pe.need(cACT, m_o_dve[h2])
                    cPE.tick(pe.e.matmul(psO[h2][:], vvv[h2],
                                         sc16[h2][po:po + 64, :],
                                         start=True, stop=False))
                    pe.need(cDVE, m_state[h2])
                    cPE.tick(pe.e.matmul(psO[h2][:], S16[h2][:],
                                         qtS[p][h2][:, cl * 64:(cl + 1) * 64],
                                         start=False, stop=True))
                    m_omm[h2] = cPE.n
            for h2 in range(2):
                dve.need(cPE, m_ds[h2])
                dve.need(cACT, prep_marks_cur[("emb", h2)])
                dve.need(cDVE, m_state[h2])
                if own:
                    dve.need(cPE, m_omm[h2])
                cDVE.tick(dve.e.scalar_tensor_tensor(
                    out=S16[h2][:], in0=S16[h2][:],
                    scalar=aendS[p][h2][:, cl:cl + 1], in1=dsls[h2],
                    op0=ALU.mult, op1=ALU.add))
                m_psds_dve[h2][dsb] = cDVE.n
                m_state[h2] = cDVE.n
            if own:
                for h2 in range(2):
                    act.need(cPE, m_omm[h2])
                    act.need(sOF2[h2], m_ofh_sp[h2])
                    cACT.tick(act.e.copy(
                        out=ofh[h2][:, (cg - 2) * 64:(cg - 1) * 64],
                        in_=psO[h2][:]))
                    m_o_dve[h2] = cACT.n
            it += 1
            if it >= 4:
                for _ in range(2):
                    if pq:
                        pq.pop(0)()
        while pq:
            pq.pop(0)()
        nonlocal_marks["m_state"] = m_state
        nonlocal_marks["m_omm"] = m_omm

    b_state = {}
    b_loads(0)
    prep0_ops, prep0_marks = b_prep_emit(0)
    for t in prep0_ops:
        t()
    prep_marks_cur = prep0_marks
    b_loads(1)
    next_prep = b_prep_emit(1)
    for st in range(NST):
        p = st % 2
        hp, s = divmod(st, NSEG)
        if s == 0:
            for h2 in range(2):
                dve.need(cACT, mP_act[p])
                dve.need(cDVE, cDVE.n)
                dve.need(cPE, b_state.get(hp - 1, {}).get("m_omm", [0, 0])[h2]
                         if hp > 0 else 0)
                cDVE.tick(dve.e.memset(S16[h2][:], 0.0))
            b_state[hp] = dict(m_state=[cDVE.n, cDVE.n], m_omm=[0, 0])
        if st + 1 < NST:
            prep_ops_next, prep_marks_next = next_prep
        else:
            prep_ops_next, prep_marks_next = [], {}
        b_chunks(st, prep_ops_next, prep_marks_next)
        mP_pe[p] = cPE.n
        mP_dve[p] = cDVE.n
        mP_act[p] = cACT.n
        mP_pool[p] = cPOOL.n
        prep_marks_cur = prep_marks_next if prep_marks_next else prep_marks_cur
        if st + 2 < NST:
            b_loads(st + 2)
            next_prep = b_prep_emit(st + 2)
        if s == NSEG - 1:
            for h2 in range(2):
                h = hp * 2 + h2
                sp.need(cACT, cACT.n)
                sOF2[h2].tick(sp.e.dma_start(out=ofm_dram[h * 128:(h + 1) * 128, :],
                                             in_=ofh[h2][:]))
                m_ofh_sp[h2] = sOF2[h2].n

    if "ofm" in taps:
        tap("ofm", [D, T], BF16)
        for h2 in range(2):
            sp.need(sOF2[h2], m_ofh_sp[h2])
        sTAP.tick(sp.e.dma_start(out=tap_outs["ofm"], in_=ofm_dram))

    ctxB.close()
    barrier()

    ctxB2 = ExitStack()
    sbB2 = lambda n, s, d: ctxB2.enter_context(nc.sbuf_tensor(n, s, d))
    psB2 = lambda n, s, d=F32: ctxB2.enter_context(nc.psum_tensor(n, s, d))

    wo16 = sbB2("wo16", [128, NF * D], BF16)
    wof = [sbB2(f"wof{b}", [128, D], F32) for b in range(2)]
    onesc16 = sbB2("onesc16", [128, 1], BF16)
    oslab = [sbB2(f"oslab{b}", [128, NF * 512], BF16) for b in range(2)]
    o2s = sbB2("o2s", [128, NF * 512], BF16)
    of16 = sbB2("of16", [128, NF * 512], BF16)
    rstdr = sbB2("rstdr", [1, 512], F32)
    s16b = [sbB2(f"s16b{b}", [128, 512], BF16) for b in range(2)]
    stok = sbB2("stok", [128, 4096], BF16)
    psSS = psB2("psSS", [1, 512], F32)
    psRep = psB2("psRep", [128, 512], F32)
    psS = [psB2(f"psS{b}", [128, 512], F32) for b in range(2)]
    psTr2 = [psB2(f"psTr2{b}", [128, 128], BF16) for b in range(2)]

    cDVE.tick(dve.e.memset(onesc16[:], 1.0))
    mB2_ones = cDVE.n
    m_wof_dve = [0, 0]
    for ci in range(NF):
        bb = ci % 2
        sp.need(cDVE, m_wof_dve[bb])
        sWO[bb].tick(sp.e.dma_start(out=wof[bb][:],
                                    in_=woT[ci * 128:(ci + 1) * 128, :]))
        dve.need(sWO[bb])
        dve.need(sW, mW_consts)
        cDVE.tick(dve.e.tensor_scalar(out=wo16[:, ci * D:(ci + 1) * D],
                                      in0=wof[bb][:],
                                      scalar1=gw_sb[:, ci:ci + 1], scalar2=None,
                                      op0=ALU.mult))
        m_wof_dve[bb] = cDVE.n
    mB2_wo = cDVE.n

    m_osl_act = [0, 0]
    m_osl_dve = [0, 0]
    m_o2_pe = 0
    m_of_pe = 0
    m_psS_dve = [0, 0]
    m_s16b_pe = [0, 0]
    m_psTr2_ev = [0, 0]
    m_rep_dve = 0
    m_stok_sp = 0
    ofr = ofm_dram.rearrange("(c p) t -> p c t", p=128)
    for sl8 in range(8):
        t0 = sl8 * 512
        bb = sl8 % 2
        sp.need(cACT, m_osl_act[bb])
        sp.need(cDVE, m_osl_dve[bb])
        sp.need(sOF2[0], m_ofh_sp[0])
        sp.need(sOF2[1], m_ofh_sp[1])
        sOSL[bb].tick(sp.e.dma_start(
            out=oslab[bb][:].rearrange("p (c n) -> p c n", n=512),
            in_=ofr[:, :, t0:t0 + 512]))
        m_osl = sOSL[bb].n
        act.need(sOSL[bb], m_osl)
        act.need(cPE, m_o2_pe)
        cACT.tick(act.e.activation(out=o2s[:], in_=oslab[bb][:], func=AF.Square))
        m_o2 = cACT.n
        m_osl_act[bb] = cACT.n
        pe.need(cACT, m_o2)
        pe.need(cDVE, mB2_ones)
        pe.need(cACT, 0)
        for ci in range(NF):
            cPE.tick(pe.e.matmul(psSS[:], onesc16[:],
                                 o2s[:, ci * 512:(ci + 1) * 512],
                                 start=(ci == 0), stop=(ci == NF - 1)))
        m_ss = cPE.n
        m_o2_pe = cPE.n
        act.need(cPE, m_ss)
        cACT.tick(act.e.activation(out=rstdr[:], in_=psSS[:], func=AF.Ln,
                                   scale=1.0 / D, bias=actb[0:1, 0:1]))
        act.need(cACT)
        cACT.tick(act.e.activation(out=rstdr[:], in_=rstdr[:], func=AF.Exp,
                                   scale=-0.5))
        m_rstd = cACT.n
        pe.need(cACT, m_rstd)
        pe.need(cDVE, m_rep_dve)
        cPE.tick(pe.e.matmul(psRep[:], onesr[:], rstdr[:], start=True, stop=True))
        m_rep = cPE.n
        dve.need(cPE, m_rep)
        dve.need(cPE, m_of_pe)
        for ci in range(NF):
            cDVE.tick(dve.e.tensor_tensor(out=of16[:, ci * 512:(ci + 1) * 512],
                                          in0=oslab[bb][:, ci * 512:(ci + 1) * 512],
                                          in1=psRep[:], op=ALU.mult))
        m_of = cDVE.n
        m_osl_dve[bb] = cDVE.n
        m_rep_dve = cDVE.n
        for dc in range(NF):
            b2 = dc % 2
            pe.need(cDVE, m_of)
            pe.need(cDVE, mB2_wo)
            pe.need(cDVE, m_psS_dve[b2])
            for ci in range(NF):
                cPE.tick(pe.e.matmul(
                    psS[b2][:],
                    wo16[:, ci * D + dc * 128: ci * D + (dc + 1) * 128],
                    of16[:, ci * 512:(ci + 1) * 512],
                    start=(ci == 0), stop=(ci == NF - 1)))
            m_smm = cPE.n
            m_of_pe = cPE.n
            dve.need(cPE, m_smm)
            dve.need(cPE, m_s16b_pe[b2])
            cDVE.tick(dve.e.tensor_copy(out=s16b[b2][:], in_=psS[b2][:]))
            m_s16c = cDVE.n
            m_psS_dve[b2] = cDVE.n
            for ts in range(4):
                pp = ts % 2
                pe.need(cDVE, m_s16c)
                pe.need(cDVE, m_psTr2_ev[pp])
                cPE.tick(pe.e.transpose(psTr2[pp][:],
                                        s16b[b2][:, ts * 128:(ts + 1) * 128],
                                        ident16[:]))
                dve.need(cPE)
                dve.need(sST, m_stok_sp)
                cDVE.tick(dve.e.tensor_copy(
                    out=stok[:, ts * 1024 + dc * 128: ts * 1024 + (dc + 1) * 128],
                    in_=psTr2[pp][:]))
                m_psTr2_ev[pp] = cDVE.n
            m_s16b_pe[b2] = cPE.n
        m_stok = cDVE.n
        for ts in range(4):
            sp.need(cDVE, m_stok)
            sST.tick(sp.e.dma_start(
                out=s_dram[t0 + ts * 128: t0 + (ts + 1) * 128, :],
                in_=stok[:, ts * 1024:(ts + 1) * 1024]))
        m_stok_sp = sST.n
    mB2_sp = sST.n

    if "s" in taps:
        tap("s", [T, D], BF16)
        sp.need(sST, mB2_sp)
        sTAP.tick(sp.e.dma_start(out=tap_outs["s"], in_=s_dram))

    ctxB2.close()
    barrier()

    ctxC = ExitStack()
    sbC = lambda n, s, d: ctxC.enter_context(nc.sbuf_tensor(n, s, d))
    psC = lambda n, s, d=F32: ctxC.enter_context(nc.psum_tensor(n, s, d))

    w1h = [sbC(f"w1h{b}", [128, NF * 2048], BF16) for b in range(2)]
    w2s = [sbC(f"w2s{b}", [128, 8 * 512], BF16) for b in range(2)]
    acc_sb = sbC("acc_sb", [128, 6 * 512], F32)
    xgfm = sbC("xgfm", [128, NF * ECAP], BF16)
    h_sb = sbC("h_sb", [128, 32 * ECAP], BF16)
    b1p = sbC("b1p", [128, FF // 128], F32)
    onesr16 = sbC("onesr16", [1, 768], BF16)
    b2f = sbC("b2f", [1, D], F32)
    b2h16 = sbC("b2h16", [1, D], BF16)
    b2l16 = sbC("b2l16", [1, D], BF16)
    mo6 = sbC("mo6", [128, 6 * D], BF16)
    psH = [psC(f"psH{b}", [128, 384], F32) for b in range(2)]
    psO2 = [psC(f"psO2_{j}", [128, 512], F32) for j in range(6)]

    w1r = w1in.rearrange("e (c p) f -> e p c f", p=128)
    w2r = w2in.rearrange("e (fc p) d -> e p fc d", p=128)
    xgr = xg_dram.rearrange("(jj p) d -> p jj d", p=128)

    m_w1_pe = [0, 0]
    m_w2_pe = [0, 0]
    m_xgt_pe = 0
    m_xgfm_pe = 0
    m_h_pe = 0
    m_psH_act = [0, 0]
    m_h_act = 0
    m_b1_act = 0
    m_b2_pe = 0
    m_b2_dve = 0
    m_psO2_dve = [0] * 6
    m_acc_dve = [0] * 6
    m_mo_sp = 0
    def c_w1_thunks(e_):
        thunks = []
        for fh in range(2):
            def t(fh=fh):
                pool.need(cPE, m_w1_pe[fh])
                sW1[fh].tick(pool.e.dma_start(
                    out=w1h[fh][:].rearrange("p (c f) -> p c f", c=NF),
                    in_=w1r[e_, :, :, fh * 2048:(fh + 1) * 2048]))
            thunks.append(t)
        return thunks

    def c_loads(e_):
        sp.need(cPE, m_xgt_pe)
        for bq in range(2):
            sp.need(sXG[bq], m_hi_pl[bq])
        sp.need(sZX, mA_zx)
        for j in range(6):
            r0 = e_ * ECAP + j * 128
            for ci in range(NF):
                sXGT.tick(sp.e.dma_start(
                    out=xgfm[:, ci * ECAP + j * 128: ci * ECAP + (j + 1) * 128],
                    in_=xg_dram[r0:r0 + 128, ci * 128:(ci + 1) * 128],
                    transpose=True))
        sp.need(cACT, m_b1_act)
        sB1.tick(sp.e.dma_start(out=b1p[:], in_=b1in[e_, :, :]))
        sp.need(cPE, m_b2_pe)
        sp.need(cDVE, m_b2_dve)
        sB2s.tick(sp.e.dma_start(out=b2f[:], in_=b2in[e_:e_ + 1, :]))
        return dict(xgl=sXGT.n, b1l=sB1.n, bl=sB2s.n)

    cDVE.tick(dve.e.memset(onesr16[:], 1.0))
    mC_ones16 = cDVE.n
    for t in c_w1_thunks(0):
        t()
    c_marks = c_loads(0)
    w1_thunks = []
    for e_ in range(E):
        m_xgl = c_marks["xgl"]
        m_b1l = c_marks["b1l"]
        m_bl = c_marks["bl"]
        dve.need(sB2s, m_bl)
        dve.need(cPE, m_b2_pe)
        cDVE.tick(dve.e.tensor_copy(out=b2h16[:], in_=b2f[:]))
        dve.need(cDVE)
        cDVE.tick(dve.e.tensor_sub(out=b2f[:], in0=b2f[:], in1=b2h16[:]))
        dve.need(cDVE)
        cDVE.tick(dve.e.tensor_copy(out=b2l16[:], in_=b2f[:]))
        m_b2_dve = cDVE.n
        m_b16 = cDVE.n

        m_xgfm = m_xgl

        for fh in range(2):
            for fc in range(16):
                fcg = fh * 16 + fc
                for th in range(2):
                    pb = (fcg * 2 + th) % 2
                    pe.need(sW1[0], sW1[0].n)
                    pe.need(sW1[1], sW1[1].n)
                    pe.need(sXGT, m_xgfm)
                    pe.need(cACT, m_psH_act[pb])
                    for ci in range(NF):
                        cPE.tick(pe.e.matmul(
                            psH[pb][:],
                            w1h[fh][:, ci * 2048 + fc * 128: ci * 2048 + (fc + 1) * 128],
                            xgfm[:, ci * ECAP + th * 384: ci * ECAP + (th + 1) * 384],
                            start=(ci == 0), stop=(ci == NF - 1)))
                    m_h1 = cPE.n
                    act.need(cPE, m_h1)
                    act.need(cPE, m_h_pe)
                    act.need(sB1, m_b1l)
                    cACT.tick(act.e.activation(
                        out=h_sb[:, fcg * ECAP + th * 384: fcg * ECAP + (th + 1) * 384],
                        in_=psH[pb][:], func=AF.Relu,
                        bias=b1p[:, fcg:fcg + 1], scale=1.0))
                    m_psH_act[pb] = cACT.n
        m_h_act = cACT.n
        m_b1_act = cACT.n
        m_w1_pe = [cPE.n, cPE.n]
        m_xgt_pe = cPE.n
        if e_ + 1 < E:
            c_marks = c_loads(e_ + 1)
            w1_thunks = c_w1_thunks(e_ + 1)

        def w2_load(q, dh):
            qb = q % 2
            pool.need(cPE, m_w2_pe[qb])
            sW2[qb].tick(pool.e.dma_start(
                out=w2s[qb][:].rearrange("p (s d) -> p s d", s=8),
                in_=w2r[e_, :, q * 8:(q + 1) * 8, dh * 512:(dh + 1) * 512]))
            return sW2[qb].n

        m_w2l = [0, 0, 0, 0]
        for dh in range(2):
            m_w2l[0] = w2_load(0, dh)
            m_w2l[1] = w2_load(1, dh)
            for q in range(4):
                qb = q % 2
                if w1_thunks and (q == 1 or q == 3):
                    w1_thunks.pop(0)()
                for jj in range(6):
                    pe.need(sW2[qb], m_w2l[q])
                    pe.need(cACT, m_h_act)
                    if q in (0, 2):
                        pe.need(cDVE, m_psO2_dve[jj])
                    for s8 in range(8):
                        fcg = q * 8 + s8
                        cPE.tick(pe.e.matmul(
                            psO2[jj][:],
                            h_sb[:, fcg * ECAP + jj * 128:
                                 fcg * ECAP + (jj + 1) * 128],
                            w2s[qb][:, s8 * 512:(s8 + 1) * 512],
                            start=(fcg % 16 == 0),
                            stop=(q == 1 and s8 == 7)))
                    if q == 3:
                        cPE.tick(pe.e.matmul(
                            psO2[jj][:], onesr16[0:1, 0:128],
                            b2h16[0:1, dh * 512:(dh + 1) * 512],
                            start=False, stop=False))
                        cPE.tick(pe.e.matmul(
                            psO2[jj][:], onesr16[0:1, 0:128],
                            b2l16[0:1, dh * 512:(dh + 1) * 512],
                            start=False, stop=True))
                    m_mm2 = cPE.n
                    if q == 1:
                        dve.need(cPE, m_mm2)
                        dve.need(cDVE, m_acc_dve[jj])
                        cDVE.tick(dve.e.tensor_copy(
                            out=acc_sb[:, jj * 512:(jj + 1) * 512],
                            in_=psO2[jj][:]))
                        m_psO2_dve[jj] = cDVE.n
                    elif q == 3:
                        dve.need(cPE, m_mm2)
                        dve.need(sMO[0], m_mo_sp)
                        cDVE.tick(dve.e.tensor_tensor(
                            out=mo6[:, jj * D + dh * 512:
                                 jj * D + (dh + 1) * 512],
                            in0=psO2[jj][:],
                            in1=acc_sb[:, jj * 512:(jj + 1) * 512],
                            op=ALU.add))
                        m_psO2_dve[jj] = cDVE.n
                        m_acc_dve[jj] = cDVE.n
                m_w2_pe[qb] = cPE.n
                if q + 2 < 4:
                    m_w2l[q + 2] = w2_load(q + 2, dh)
        m_mo = cDVE.n
        for jj in range(6):
            sp.need(cDVE, m_mo)
            sMO[0].tick(sp.e.dma_start(
                out=moe_dram[e_ * ECAP + jj * 128: e_ * ECAP + (jj + 1) * 128, :],
                in_=mo6[:, jj * D:(jj + 1) * D]))
        m_mo_sp = sMO[0].n
        m_b2_pe = cPE.n
        m_h_pe = cPE.n
    mC_mo = m_mo_sp

    if "moe" in taps:
        tap("moe", [NSLOT, D], BF16)
        sp.need(sMO[0], mC_mo)
        sTAP.tick(sp.e.dma_start(out=tap_outs["moe"], in_=moe_dram))
    if "xg" in taps:
        tap("xg", [NSLOT, D], BF16)
        for bq in range(2):
            sp.need(sXG[bq], m_hi_pl[bq])
        sTAP.tick(sp.e.dma_start(out=tap_outs["xg"], in_=xg_dram))

    ctxC.close()
    barrier()

    ctxD = ExitStack()
    sbD = lambda n, s, d: ctxD.enter_context(nc.sbuf_tensor(n, s, d))
    psD = lambda n, s, d=F32: ctxD.enter_context(nc.psum_tensor(n, s, d))

    cco = sbD("cco", [8, 8], F32)
    psCO = psD("psCO", [1, 8], F32)
    psCOr = psD("psCOr", [128, 8], F32)
    if no_coll:
        sCO.tick(sp.e.dma_start(out=coreoff[:], in_=dbg_coreoff))
        pe.need(sCO)
    else:
        pool.need(cCC, 1)
        sCC2.tick(pool.e.dma_start(out=cco[:], in_=cc_out))
        pe.need(sCC2)
        cPE.tick(pe.e.matmul(psCO[:], cmask[:], cco[:], start=True, stop=True))
        dve.need(cPE)
        cDVE.tick(dve.e.tensor_copy(out=coreoff[:], in_=psCO[:]))
        pe.need(cDVE)
    cPE.tick(pe.e.matmul(psCOr[:], onesr[:], coreoff[:], start=True, stop=True))
    dve.need(cPE)
    cDVE.tick(dve.e.tensor_copy(out=coff_rep[:], in_=psCOr[:]))
    mD_rep = cDVE.n

    xd = [sbD(f"xd{b}", [128, D], F32) for b in range(2)]
    mt16 = [sbD(f"mt16{b}", [128, D], BF16) for b in range(2)]
    st16 = [sbD(f"st16{b}", [128, D], BF16) for b in range(2)]
    mg = [sbD(f"mg{b}", [128, D], F32) for b in range(2)]
    gs = [sbD(f"gs{b}", [128, D], F32) for b in range(2)]
    outb = [sbD(f"outb{b}", [128, D], F32) for b in range(2)]
    d8 = [sbD(f"d8_{b}", [128, 32], F32) for b in range(2)]
    dc4 = [sbD(f"dc4_{b}", [128, 4], F32) for b in range(2)]

    m_xd_dve = [0, 0]
    m_mt_dve = [0, 0]
    m_st_dve = [0, 0]
    m_mg_act = [0, 0]
    m_gs_dve = [0, 0]
    m_out_sp = [0, 0]
    m_ge_dve = [0, 0]
    ge4 = [sbD(f"ge4_{b}", [128, 4], F32) for b in range(2)]
    for gq in range(8):
        bq = gq % 2
        dve.need(cDVE, mD_rep)
        dve.need(cDVE, m_ge_dve[bq])
        cDVE.tick(dve.e.tensor_tensor(
            out=d8[bq][:].rearrange("p (t e) -> p t e", e=8),
            in0=oh_sb[:, gq * 32:(gq + 1) * 32]
                .rearrange("p (t e) -> p t e", e=8),
            in1=coff_rep[:].rearrange("p (o e) -> p o e", o=1)
                .broadcast_to([128, 4, 8]),
            op=ALU.mult))
        dve.need(cDVE)
        cDVE.tick(dve.e.tensor_reduce(
            out=dc4[bq][:].rearrange("p (t o) -> p t o", o=1),
            in_=d8[bq][:].rearrange("p (t e) -> p t e", e=8),
            axis=AX.X, op=ALU.add))
        dve.need(cDVE)
        cDVE.tick(dve.e.tensor_tensor(out=dc4[bq][:], in0=dc4[bq][:],
                                      in1=pos_sb[:, gq * 4:(gq + 1) * 4],
                                      op=ALU.add))
        dve.need(cDVE)
        cDVE.tick(dve.e.tensor_scalar(out=dc4[bq][:], in0=dc4[bq][:],
                                      scalar1=float(CAP), scalar2=None,
                                      op0=ALU.is_lt))
        dve.need(cDVE)
        cDVE.tick(dve.e.tensor_tensor(out=ge4[bq][:], in0=dc4[bq][:],
                                      in1=gate_sb[:, gq * 4:(gq + 1) * 4],
                                      op=ALU.mult))
        m_ge = cDVE.n
        m_ge_dve[bq] = cDVE.n
        for kt in range(4):
            io = gq * 4 + kt
            b = io % 2
            sp.need(cDVE, m_xd_dve[b])
            sXD[b].tick(sp.e.dma_start(out=xd[b][:],
                                       in_=x_own[io * 128:(io + 1) * 128, :]))
            m_xl = sXD[b].n
            pool.need(cDVE, m_mt_dve[b])
            pool.need(sMO[0], mC_mo)
            sMG[b].tick(pool.e.indirect_dma_start(
                out=mt16[b][:], out_offset=None,
                in_=moe_dram,
                in_offset=IndirectOffsetOnAxis(slotix[:, io:io + 1], 0),
                bounds_check=rbound, oob_is_err=False))
            m_ml = sMG[b].n
            sp.need(cDVE, m_st_dve[b])
            sp.need(sST, mB2_sp)
            sSL[b].tick(sp.e.dma_start(out=st16[b][:],
                                       in_=s_dram[io * 128:(io + 1) * 128, :]))
            m_sl = sSL[b].n
            dve.need(sMG[b], m_ml)
            dve.need(cACT, m_mg_act[b])
            dve.need(cDVE, m_ge)
            cDVE.tick(dve.e.tensor_scalar(out=mg[b][:], in0=mt16[b][:],
                                          scalar1=ge4[bq][:, kt:kt + 1],
                                          scalar2=None, op0=ALU.mult))
            m_mt_dve[b] = cDVE.n
            m_mg = cDVE.n
            act.need(cDVE, m_mg)
            act.need(cDVE, m_gs_dve[b])
            cACT.tick(act.e.activation(out=gs[b][:], in_=mg[b][:], func=AF.Silu))
            m_mg_act[b] = cACT.n
            m_gsa = cACT.n
            dve.need(cACT, m_gsa)
            dve.need(sSL[b], m_sl)
            dve.need(sY[b], m_out_sp[b])
            cDVE.tick(dve.e.tensor_tensor(out=outb[b][:], in0=gs[b][:],
                                          in1=st16[b][:], op=ALU.mult))
            m_st_dve[b] = cDVE.n
            m_gs_dve[b] = cDVE.n
            dve.need(cDVE)
            dve.need(sXD[b], m_xl)
            cDVE.tick(dve.e.tensor_add(out=outb[b][:], in0=outb[b][:],
                                       in1=xd[b][:]))
            m_xd_dve[b] = cDVE.n
            m_out = cDVE.n
            sp.need(cDVE, m_out)
            sY[b].tick(sp.e.dma_start(out=y_out[io * 128:(io + 1) * 128, :],
                                      in_=outb[b][:]))
            m_out_sp[b] = sY[b].n

    for c in sync_sems:
        sp.need(c, c.n)
    for c in pool_sems:
        pool.need(c, c.n)

    ctxD.close()
    ctx.close()
    return nc, tap_outs


def _consts():
    scanm = np.ones((128, TH), dtype=np.float32)
    scanm[:, ::L] = 0.0
    masksc = np.tril(np.ones((64, 64), dtype=np.float32))
    masksc = np.triu(np.ones((64, 64), dtype=np.float32))
    tril = (np.arange(128)[:, None] < np.arange(128)[None, :]).astype(ml_dtypes.bfloat16)
    return {
        "c_ident": np.eye(128, dtype=ml_dtypes.bfloat16),
        "c_scanm": scanm,
        "c_masksc": masksc,
        "c_tril": tril,
        "c_iota8": np.broadcast_to(np.arange(8, dtype=np.float32), (128, 8)).copy(),
        "c_iota32": np.broadcast_to(np.tile(np.arange(8, dtype=np.float32), 4),
                                    (128, 32)).copy(),
        "c_mask84": np.broadcast_to(
            np.where(np.arange(32) % 8 == 0, 0.0, 1.0).astype(np.float32),
            (128, 32)).copy(),
        "c_ones8": np.ones((128, 8), dtype=np.float32),
        "c_onesc": np.ones((128, 1), dtype=ml_dtypes.bfloat16),
        "c_onesr": np.ones((1, 128), dtype=np.float32),
        "c_onesr16": np.ones((1, 768), dtype=ml_dtypes.bfloat16),
    }


def make_in_maps(inputs, no_coll=False, coreoffs=None):
    x = np.ascontiguousarray(
        np.asarray(inputs["L0_attn_hidden_states"], dtype=np.float32)
        .reshape(2 * 16384, D))
    com = _consts()
    wqT = np.ascontiguousarray(np.asarray(inputs["L0_attn_Wq"], np.float32).T)
    wfT = np.ascontiguousarray(np.asarray(inputs["L0_attn_Wf"], np.float32).T)
    wiT = np.ascontiguousarray(np.asarray(inputs["L0_attn_Wi"], np.float32).T)
    woT = np.ascontiguousarray(np.asarray(inputs["L0_attn_Wo"], np.float32).T)
    gw = np.ascontiguousarray(
        np.asarray(inputs["L0_attn_g_weight"], np.float32).reshape(NF, 128))
    rw = np.ascontiguousarray(np.asarray(inputs["L0_ffn_router_w"], np.float32))
    w1 = np.ascontiguousarray(np.asarray(inputs["L0_ffn_w1"], np.float32))
    b1 = np.ascontiguousarray(
        np.asarray(inputs["L0_ffn_b1"], np.float32)
        .reshape(E, FF // 128, 128).transpose(0, 2, 1))
    w2 = np.ascontiguousarray(np.asarray(inputs["L0_ffn_w2"], np.float32))
    b2 = np.ascontiguousarray(np.asarray(inputs["L0_ffn_b2"], np.float32))
    in_maps = []
    for c in range(8):
        own = np.ascontiguousarray(x[c * T:(c + 1) * T])
        if c % 4 == 0:
            halo = np.zeros((HALO, D), dtype=np.float32)
        else:
            halo = np.ascontiguousarray(x[c * T - HALO:c * T])
        cmsk = (np.arange(8) < c).astype(np.float32).reshape(8, 1)
        m = dict(x_own=own, x_halo=halo, wqT=wqT, wfT=wfT, wiT=wiT, woT=woT,
                 gwin=gw, rwin=rw, w1in=w1, b1in=b1, w2in=w2, b2in=b2,
                 c_cmask=np.ascontiguousarray(cmsk), **com)
        if no_coll:
            m["dbg_coreoff"] = np.ascontiguousarray(
                coreoffs[c].reshape(1, 8).astype(np.float32))
        in_maps.append(m)
    return in_maps


def kernel(**inputs):
    nc, _ = build()
    in_maps = make_in_maps(inputs)
    res = run_bass_kernel_spmd(nc, in_maps, core_ids=list(range(8)))
    y = np.concatenate([np.asarray(r["y"]) for r in res.results], axis=0)
    return y.reshape(2, 16384, D)


# version 16 (final)
# speedup vs baseline: 1.0409x; 1.0093x over previous
import math
import numpy as np
import ml_dtypes
import concourse.bass as bass
import concourse.mybir as mybir
from concourse.bass_core_rust import IndirectOffsetOnAxis
from concourse.bass_utils import run_bass_kernel_spmd
from contextlib import ExitStack

F32 = mybir.dt.float32
BF16 = mybir.dt.bfloat16
I32 = mybir.dt.int32
AF = mybir.ActivationFunctionType
ALU = mybir.AluOpType
AX = mybir.AxisListType

T = 4096
HALO = 128
TH = T + HALO
NT = 32
NTH = 33
D = 1024
NF = 8
E = 8
FF = 4096
ECAP = 768
NSLOT = E * ECAP
NH = 8
DF = 128
L = 64
NCH = TH // L
XW = 2048
CAP = 4096
LN_DFI = math.log(DF ** -0.5)


class Ctr:
    def __init__(self, nc, name, step=1):
        self.sem = nc.alloc_semaphore(name=name)
        self.step = step
        self.n = 0

    def tick(self, instr):
        instr.then_inc(self.sem, self.step)
        self.n += self.step
        return self.n


class Eng:
    def __init__(self, eng):
        self.e = eng
        self.seen = {}

    def need(self, ctr, val=None):
        if val is None:
            val = ctr.n
        if val <= 0:
            return
        if self.seen.get(id(ctr), -1) >= val:
            return
        self.e.wait_ge(ctr.sem, val)
        self.seen[id(ctr)] = val


def build(no_coll=False, taps=()):
    nc = bass.Bass("TRN2", target_bir_lowering=False, debug=False,
                   dynamic_dma_scratch_size=16384)

    x_own = nc.dram_tensor("x_own", [T, D], F32, kind="ExternalInput").ap()
    x_halo = nc.dram_tensor("x_halo", [HALO, D], F32, kind="ExternalInput").ap()
    wqT = nc.dram_tensor("wqT", [D, D], F32, kind="ExternalInput").ap()
    wfT = nc.dram_tensor("wfT", [D, D], F32, kind="ExternalInput").ap()
    wiT = nc.dram_tensor("wiT", [D, D], F32, kind="ExternalInput").ap()
    woT = nc.dram_tensor("woT", [D, D], F32, kind="ExternalInput").ap()
    gwin = nc.dram_tensor("gwin", [NF, 128], F32, kind="ExternalInput").ap()
    rwin = nc.dram_tensor("rwin", [D, E], F32, kind="ExternalInput").ap()
    w1in = nc.dram_tensor("w1in", [E, D, FF], F32, kind="ExternalInput").ap()
    b1in = nc.dram_tensor("b1in", [E, 128, FF // 128], F32, kind="ExternalInput").ap()
    w2in = nc.dram_tensor("w2in", [E, FF, D], F32, kind="ExternalInput").ap()
    b2in = nc.dram_tensor("b2in", [E, D], F32, kind="ExternalInput").ap()
    c_ident = nc.dram_tensor("c_ident", [128, 128], BF16, kind="ExternalInput").ap()
    c_scanm = nc.dram_tensor("c_scanm", [128, TH], F32, kind="ExternalInput").ap()
    c_masksc = nc.dram_tensor("c_masksc", [64, 64], F32, kind="ExternalInput").ap()
    c_tril = nc.dram_tensor("c_tril", [128, 128], BF16, kind="ExternalInput").ap()
    c_iota8 = nc.dram_tensor("c_iota8", [128, 8], F32, kind="ExternalInput").ap()
    c_iota32 = nc.dram_tensor("c_iota32", [128, 32], F32, kind="ExternalInput").ap()
    c_mask84 = nc.dram_tensor("c_mask84", [128, 32], F32, kind="ExternalInput").ap()
    c_ones8 = nc.dram_tensor("c_ones8", [128, 8], F32, kind="ExternalInput").ap()
    c_onesc = nc.dram_tensor("c_onesc", [128, 1], BF16, kind="ExternalInput").ap()
    c_onesr = nc.dram_tensor("c_onesr", [1, 128], F32, kind="ExternalInput").ap()
    c_onesr16 = nc.dram_tensor("c_onesr16", [1, 768], BF16, kind="ExternalInput").ap()
    c_cmask = nc.dram_tensor("c_cmask", [8, 1], F32, kind="ExternalInput").ap()
    if no_coll:
        dbg_coreoff = nc.dram_tensor("dbg_coreoff", [1, 8], F32,
                                     kind="ExternalInput").ap()

    y_out = nc.dram_tensor("y", [T, D], F32, kind="ExternalOutput").ap()

    tap_outs = {}

    def tap(name, shape, dt=F32):
        tap_outs[name] = nc.dram_tensor("tap_" + name, shape, dt,
                                        kind="ExternalOutput").ap()
        return tap_outs[name]

    dram = lambda name, shape, dt: nc.dram_tensor(name, shape, dt,
                                                  kind="Internal").ap()
    xg_dram = dram("xg_dram", [NSLOT, D], BF16)
    moe_dram = dram("moe_dram", [NSLOT, D], BF16)
    q_dram = dram("q_dram", [D, TH], BF16)
    vraw_dram = dram("vraw_dram", [D, TH], F32)
    vtok_dram = dram("vtok_dram", [TH, D], BF16)
    ofm_dram = dram("ofm_dram", [D, T], BF16)
    s_dram = dram("s_dram", [T, D], BF16)
    cc_in = dram("cc_in", [1, 8], F32)
    cc_out = dram("cc_out", [8, 8], F32)

    rbound = nc.gpsimd.alloc_register("rbound")
    nc.gpsimd.reg_mov(rbound, NSLOT - 1)
    pe = Eng(nc.tensor)
    dve = Eng(nc.vector)
    act = Eng(nc.scalar)
    pool = Eng(nc.gpsimd)
    sp = Eng(nc.sync)

    cPE = Ctr(nc, "cPE")
    cDVE = Ctr(nc, "cDVE")
    cACT = Ctr(nc, "cACT")
    cCC = Ctr(nc, "cCC", 1)
    sync_sems = []
    pool_sems = []

    def DS(name, q="sync"):
        c = Ctr(nc, name, 16)
        (sync_sems if q == "sync" else pool_sems).append(c)
        return c

    sW = DS("sW")
    sWP = DS("sWP", "pool")
    sXT = [DS(f"sXT{b}", "pool") for b in range(2)]
    sXG = [DS(f"sXG{b}", "pool") for b in range(2)]
    sTAP = DS("sTAP")
    sCC1 = DS("sCC1", "pool")
    sCC2 = DS("sCC2", "pool")
    sQ = [DS(f"sQ{b}") for b in range(2)]
    sVB = [DS(f"sVB{b}") for b in range(2)]
    sVV = [DS(f"sVV{b}") for b in range(2)]
    sBL2 = [DS(f"sBL{b}") for b in range(2)]
    sOF2 = [DS(f"sOF{b}") for b in range(2)]
    sKT2 = [DS(f"sKT{b}") for b in range(2)]
    sWO = [DS(f"sWO{b}") for b in range(2)]
    sOSL = [DS(f"sOSL{b}") for b in range(2)]
    sST = DS("sST")
    sW1 = [DS(f"sW1{b}", "pool") for b in range(2)]
    sW2 = [DS(f"sW2{b}", "pool") for b in range(2)]
    sXGT = DS("sXGT")
    sB1 = DS("sB1")
    sB2s = DS("sB2s")
    sMO = [DS(f"sMO{b}") for b in range(2)]
    sXD = [DS(f"sXD{b}") for b in range(2)]
    sSL = [DS(f"sSL{b}") for b in range(2)]
    sMG = [DS(f"sMG{b}", "pool") for b in range(2)]
    sY = [DS(f"sY{b}") for b in range(2)]
    sCO = DS("sCO")

    ctx = ExitStack()

    def barrier():
        for g in (pe, dve, act, sp, pool):
            for c in (cPE, cDVE, cACT):
                g.need(c, c.n)
            for c in sync_sems + pool_sems:
                g.need(c, c.n)

    def sb(name, shape, dt):
        return ctx.enter_context(nc.sbuf_tensor(name, shape, dt))

    ident16 = sb("ident16", [128, 128], BF16)
    iota8 = sb("iota8", [128, 8], F32)
    ones8 = sb("ones8", [128, 8], F32)
    onesc = sb("onesc", [128, 1], BF16)
    onesr = sb("onesr", [1, 128], F32)
    cmask = sb("cmask", [8, 1], F32)
    gw_sb = sb("gw_sb", [128, 8], F32)
    rwh = sb("rwh", [128, 64], BF16)
    rwl = sb("rwl", [128, 64], BF16)
    slotix = sb("slotix", [128, NT], I32)
    gate_sb = sb("gate_sb", [128, NT], F32)
    pos_sb = sb("pos_sb", [128, NT], F32)
    oh_sb = sb("oh_sb", [128, NT * 8], BF16)
    runoff = sb("runoff", [1, 8], F32)
    coreoff = sb("coreoff", [1, 8], F32)
    coff_rep = sb("coff_rep", [128, 8], F32)
    actb = sb("actb", [128, 3], F32)

    for dst, src in [(ident16, c_ident), (iota8, c_iota8), (ones8, c_ones8),
                     (onesc, c_onesc), (onesr, c_onesr), (cmask, c_cmask)]:
        sW.tick(sp.e.dma_start(out=dst[:], in_=src))
    for ci in range(NF):
        sW.tick(sp.e.dma_start(out=gw_sb[:, ci:ci + 1],
                               in_=gwin[ci:ci + 1, :].rearrange("a p -> p a")))

    ctx1 = ExitStack()
    trilS = ctx1.enter_context(nc.sbuf_tensor("trilS", [128, 128], BF16))
    sW.tick(sp.e.dma_start(out=trilS[:], in_=c_tril))
    rwf = ctx1.enter_context(nc.sbuf_tensor("rwf", [128, 64], F32))
    rdiff = ctx1.enter_context(nc.sbuf_tensor("rdiff", [128, 64], F32))
    sW.tick(sp.e.dma_start(out=rwf[:].rearrange("p (c e) -> p c e", e=8),
                           in_=rwin.rearrange("(c p) e -> p c e", p=128)))
    mW_consts = sW.n
    dve.need(sW)
    cDVE.tick(dve.e.tensor_copy(out=rwh[:], in_=rwf[:]))
    dve.need(cDVE)
    cDVE.tick(dve.e.tensor_sub(out=rdiff[:], in0=rwf[:], in1=rwh[:]))
    dve.need(cDVE)
    cDVE.tick(dve.e.tensor_copy(out=rwl[:], in_=rdiff[:]))
    mW_rw = cDVE.n
    cDVE.tick(dve.e.memset(runoff[:], 0.0))
    mW_runoff0 = cDVE.n
    cDVE.tick(dve.e.memset(actb[:, 0:1], 1e-5))
    cDVE.tick(dve.e.memset(actb[:, 1:2], 1.0))
    cDVE.tick(dve.e.memset(actb[:, 2:3], LN_DFI))
    mW_actb = cDVE.n

    xnfm = ctx1.enter_context(nc.sbuf_tensor("xnfm", [128, NF * XW], BF16))

    ctxA = ExitStack()
    sbA = lambda n, s, d: ctxA.enter_context(nc.sbuf_tensor(n, s, d))
    psA = lambda n, s, d=F32: ctxA.enter_context(nc.psum_tensor(n, s, d))

    GT = 4
    GW = GT * 1024
    xtg = [sbA(f"xtg{b}", [128, GW], F32) for b in range(2)]
    sqg = sbA("sqg", [128, 1024], F32)
    hig = [sbA(f"hig{b}", [128, GW], BF16) for b in range(2)]
    logt = [sbA(f"logt{b}", [128, GW], BF16) for b in range(2)]
    loslabg = [sbA(f"loslabg{b}", [128, GW], BF16) for b in range(2)]
    statg = [sbA(f"statg{b}", [128, 32], F32) for b in range(2)]
    prb = [sbA(f"prb{b}", [128, 32], F32) for b in range(2)]
    ism = [sbA(f"ism{b}", [128, 32], F32) for b in range(2)]
    cumg = [sbA(f"cumg{b}", [128, 32], F32) for b in range(2)]
    posmg = [sbA(f"posmg{b}", [128, 32], F32) for b in range(2)]
    s4 = [sbA(f"s4_{b}", [128, 16], F32) for b in range(2)]
    runvec = sbA("runvec", [1, 32], F32)
    cnt4 = sbA("cnt4", [1, 32], F32)
    iota32 = sbA("iota32", [128, 32], F32)
    mask84 = sbA("mask84", [128, 32], F32)
    zb16 = sbA("zb16", [128, D], BF16)
    psTr = [psA(f"psTrA{b}", [128, 128], BF16) for b in range(2)]
    psLg = psA("psLg", [128, 32], F32)
    psPos = [psA(f"psPos{b}", [128, 32], F32) for b in range(2)]
    psCnt = psA("psCnt", [1, 32], F32)

    ctxB0 = ExitStack()
    sbB0 = lambda n, s, d: ctxB0.enter_context(nc.sbuf_tensor(n, s, d))
    psB0 = lambda n, s, d=F32: ctxB0.enter_context(nc.psum_tensor(n, s, d))
    wq_sb = sbB0("wq_sb", [128, NF * D], BF16)
    wf_sb = sbB0("wf_sb", [128, NF * D], BF16)
    wi_sb = sbB0("wi_sb", [128, NF * D], BF16)
    psQ = [psB0(f"psB0q{b}", [128, 512], F32) for b in range(2)]
    qb16 = [sbB0(f"qb16{b}", [128, 512], BF16) for b in range(2)]
    ub = [sbB0(f"ub{b}", [128, 512], F32) for b in range(2)]
    vb = [sbB0(f"vb{b}", [128, 512], F32) for b in range(2)]
    vvb = [sbB0(f"vvb{b}", [128, 1024], BF16) for b in range(2)]
    m_qb_sp = [0, 0]
    m_vb_sp = [0, 0]
    m_vv_sp = [0, 0]
    m_psq_ev = [0, 0]
    m_psv_ev = [0, 0]
    m_ub_act = [0, 0]

    SSQ, SS1, SMEAN, SRSTD, SM, SZ, SGATE, SEVAL = range(8)
    QPOS, QBAD, QOKL, QSLT = range(4)

    sA8 = DS("sA8")
    cSP_A = sA8
    cDVE.tick(dve.e.memset(zb16[:], 0.0))
    mA_zb = cDVE.n
    sZX = DS("sZX")
    sp.need(cDVE, mA_zb)
    for r0 in range(0, NSLOT, 128):
        sZX.tick(sp.e.dma_start(out=xg_dram[r0:r0 + 128, :], in_=zb16[:]))
    mA_zx = sZX.n
    sA8.tick(sp.e.dma_start(out=iota32[:], in_=c_iota32))
    sA8.tick(sp.e.dma_start(out=mask84[:], in_=c_mask84))
    mA_c = sA8.n

    m_xnfm_tile = [0] * NTH
    m_runoff_dve = mW_runoff0

    sXT[0].tick(pool.e.dma_start(out=xtg[0][:, 0:1024], in_=x_halo))
    m_ld0 = sXT[0].n
    act.need(sXT[0], m_ld0)
    cACT.tick(act.e.activation(out=sqg[:], in_=xtg[0][:, 0:1024], func=AF.Square,
                               accum_out=statg[0][:, SSQ:SSQ + 1]))
    m_sq0 = cACT.n
    dve.need(sXT[0], m_ld0)
    cDVE.tick(dve.e.reduce_sum(out=statg[0][:, SS1:SS1 + 1],
                               in_=xtg[0][:, 0:1024], axis=AX.X))
    dve.need(cDVE)
    cDVE.tick(dve.e.tensor_scalar(out=statg[0][:, SMEAN:SMEAN + 1],
                                  in0=statg[0][:, SS1:SS1 + 1],
                                  scalar1=1.0 / D, scalar2=None, op0=ALU.mult))
    dve.need(cDVE)
    cDVE.tick(dve.e.tensor_tensor(out=statg[0][:, SRSTD:SRSTD + 1],
                                  in0=statg[0][:, SMEAN:SMEAN + 1],
                                  in1=statg[0][:, SMEAN:SMEAN + 1], op=ALU.mult))
    dve.need(cACT, m_sq0)
    dve.need(cDVE)
    cDVE.tick(dve.e.scalar_tensor_tensor(
        out=statg[0][:, SRSTD:SRSTD + 1], in0=statg[0][:, SSQ:SSQ + 1],
        scalar=1.0 / D, in1=statg[0][:, SRSTD:SRSTD + 1],
        op0=ALU.mult, op1=ALU.subtract))
    act.need(cDVE)
    act.need(cDVE, mW_actb)
    cACT.tick(act.e.activation(out=statg[0][:, SRSTD:SRSTD + 1],
                               in_=statg[0][:, SRSTD:SRSTD + 1], func=AF.Ln,
                               bias=actb[:, 0:1], scale=1.0))
    act.need(cACT)
    cACT.tick(act.e.activation(out=statg[0][:, SRSTD:SRSTD + 1],
                               in_=statg[0][:, SRSTD:SRSTD + 1],
                               func=AF.Exp, scale=-0.5))
    m_rstd0 = cACT.n
    dve.need(cACT, m_rstd0)
    cDVE.tick(dve.e.tensor_scalar(out=xtg[0][:, 0:1024], in0=xtg[0][:, 0:1024],
                                  scalar1=statg[0][:, SMEAN:SMEAN + 1],
                                  scalar2=statg[0][:, SRSTD:SRSTD + 1],
                                  op0=ALU.subtract, op1=ALU.mult))
    dve.need(cDVE)
    cDVE.tick(dve.e.tensor_copy(out=hig[0][:, 0:1024], in_=xtg[0][:, 0:1024]))
    m_hi0 = cDVE.n
    m_psTr_evD = [0, 0]
    m_psTr_evA = [0, 0]
    for ci in range(NF):
        pp = ci % 2
        pe.need(cDVE, m_hi0)
        pe.need(cDVE, m_psTr_evD[pp])
        cPE.tick(pe.e.transpose(psTr[pp][:], hig[0][:, ci * 128:(ci + 1) * 128],
                                ident16[:]))
        dve.need(cPE)
        cDVE.tick(dve.e.tensor_copy(
            out=xnfm[:, ci * XW: ci * XW + 128], in_=psTr[pp][:]))
        m_psTr_evD[pp] = cDVE.n
    m_xnfm_tile[0] = cDVE.n
    m_hi_pe = cPE.n

    m_xt_act = [m_sq0, 0]
    m_xt_dve = [m_hi0, 0]
    m_stat_dve = [0, 0]
    m_sq_act = m_sq0
    m_hi_dve = [m_hi0, 0]
    m_hi_used_pe = [m_hi_pe, 0]
    m_lo_used_pe = [0, 0]
    m_loslab_pe = [0, 0]
    m_lg_dve = 0
    m_ohv = [0, 0]
    m_pos_dve = [0, 0]
    m_cnt_dve = 0
    m_s4_dve = [0, 0]
    m_hi_pl = [0, 0]
    m_hiv = [0, 0]
    m_lov = [0, 0]

    def a_h1(g):
        nonlocal m_sq_act
        b = g % 2
        t0 = g * GT * 128
        pool.need(cACT, m_xt_act[b])
        pool.need(cDVE, m_xt_dve[b])
        sXT[b].tick(pool.e.dma_start(
            out=xtg[b][:].rearrange("p (t d) -> p t d", d=1024),
            in_=x_own[t0:t0 + GT * 128, :].rearrange("(t p) d -> p t d", p=128)))
        m_ld = sXT[b].n

        for k in range(GT):
            act.need(sXT[b], m_ld)
            act.need(cACT, m_sq_act)
            act.need(cDVE, m_stat_dve[b])
            cACT.tick(act.e.activation(
                out=sqg[:], in_=xtg[b][:, k * 1024:(k + 1) * 1024],
                func=AF.Square,
                accum_out=statg[b][:, k * 8 + SSQ: k * 8 + SSQ + 1]))
            m_sq_act = cACT.n
        m_sq = cACT.n
        dve.need(sXT[b], m_ld)
        dve.need(cDVE, m_stat_dve[b])
        cDVE.tick(dve.e.tensor_reduce(
            out=statg[b][:, SS1::8].rearrange("p (t o) -> p t o", o=1),
            in_=xtg[b][:].rearrange("p (t d) -> p t d", d=1024),
            axis=AX.X, op=ALU.add))
        dve.need(cDVE)
        cDVE.tick(dve.e.tensor_scalar(out=statg[b][:, SMEAN::8],
                                      in0=statg[b][:, SS1::8],
                                      scalar1=1.0 / D, scalar2=None,
                                      op0=ALU.mult))
        dve.need(cDVE)
        cDVE.tick(dve.e.tensor_tensor(out=statg[b][:, SRSTD::8],
                                      in0=statg[b][:, SMEAN::8],
                                      in1=statg[b][:, SMEAN::8], op=ALU.mult))
        dve.need(cACT, m_sq)
        dve.need(cDVE)
        cDVE.tick(dve.e.scalar_tensor_tensor(
            out=statg[b][:, SRSTD::8], in0=statg[b][:, SSQ::8],
            scalar=1.0 / D, in1=statg[b][:, SRSTD::8],
            op0=ALU.mult, op1=ALU.subtract))
        m_var = cDVE.n
        act.need(cDVE, m_var)
        cACT.tick(act.e.activation(out=statg[b][:, SRSTD::8],
                                   in_=statg[b][:, SRSTD::8], func=AF.Ln,
                                   bias=actb[:, 0:1], scale=1.0))
        act.need(cACT)
        cACT.tick(act.e.activation(out=statg[b][:, SRSTD::8],
                                   in_=statg[b][:, SRSTD::8],
                                   func=AF.Exp, scale=-0.5))
        m_rstd = cACT.n
        for k in range(GT):
            dve.need(cACT, m_rstd)
            dve.need(cDVE)
            cDVE.tick(dve.e.tensor_scalar(
                out=xtg[b][:, k * 1024:(k + 1) * 1024],
                in0=xtg[b][:, k * 1024:(k + 1) * 1024],
                scalar1=statg[b][:, k * 8 + SMEAN: k * 8 + SMEAN + 1],
                scalar2=statg[b][:, k * 8 + SRSTD: k * 8 + SRSTD + 1],
                op0=ALU.subtract, op1=ALU.mult))
        m_xn = cDVE.n
        m_xt_act[b] = m_sq
        dve.need(cDVE, m_xn)
        dve.need(cDVE, m_hi_dve[b])
        dve.need(cPE, m_hi_used_pe[b])
        dve.need(sXG[b], m_hi_pl[b])
        cDVE.tick(dve.e.tensor_copy(out=hig[b][:], in_=xtg[b][:]))
        m_hi = cDVE.n
        m_hiv[b] = m_hi
        dve.need(cPE, m_lo_used_pe[b])
        dve.need(cDVE)
        cDVE.tick(dve.e.tensor_sub(out=logt[b][:], in0=xtg[b][:], in1=hig[b][:]))
        m_lo = cDVE.n
        m_lov[b] = m_lo
        m_hi_dve[b] = cDVE.n
        m_xt_dve[b] = cDVE.n


    def a_h2(g):
        nonlocal m_runoff_dve, m_cnt_dve, m_lg_dve
        b = g % 2
        m_hi = m_hiv[b]
        m_lo = m_lov[b]
        for kt in range(GT):
            ti = g * GT + kt + 1
            for ci in range(NF):
                pp = ci % 2
                pe.need(cDVE, m_hi)
                pe.need(cDVE, m_psTr_evD[pp])
                pe.need(cACT, m_psTr_evA[pp])
                cPE.tick(pe.e.transpose(
                    psTr[pp][:],
                    hig[b][:, kt * 1024 + ci * 128: kt * 1024 + (ci + 1) * 128],
                    ident16[:]))
                dve.need(cPE)
                wc = (ti * 128) % XW
                cDVE.tick(dve.e.tensor_copy(
                    out=xnfm[:, ci * XW + wc: ci * XW + wc + 128],
                    in_=psTr[pp][:]))
                m_psTr_evD[pp] = cDVE.n
            m_xnfm_tile[ti] = cDVE.n
        m_hi_used_pe[b] = cPE.n
        for kt in range(GT):
            for ci in range(NF):
                pp = ci % 2
                pe.need(cDVE, m_lo)
                pe.need(cDVE, m_psTr_evD[pp])
                pe.need(cACT, m_psTr_evA[pp])
                pe.need(cPE, m_loslab_pe[b])
                cPE.tick(pe.e.transpose(
                    psTr[pp][:],
                    logt[b][:, kt * 1024 + ci * 128: kt * 1024 + (ci + 1) * 128],
                    ident16[:]))
                act.need(cPE)
                cACT.tick(act.e.copy(
                    out=loslabg[b][:, kt * 1024 + ci * 128: kt * 1024 + (ci + 1) * 128],
                    in_=psTr[pp][:]))
                m_psTr_evA[pp] = cACT.n
        m_lo_used_pe[b] = cPE.n
        m_loslab = cACT.n

        pe.need(cACT, m_loslab)
        pe.need(sW, mW_consts)
        pe.need(cDVE, mW_rw)
        pe.need(cDVE, m_lg_dve)
        for kt in range(GT):
            ti = g * GT + kt + 1
            wc = (ti * 128) % XW
            nmm = 0
            for ci in range(NF):
                xb = xnfm[:, ci * XW + wc: ci * XW + wc + 128]
                lb = loslabg[b][:, kt * 1024 + ci * 128: kt * 1024 + (ci + 1) * 128]
                for lhsT, rhs in ((xb, rwh), (xb, rwl), (lb, rwh)):
                    nmm += 1
                    cPE.tick(pe.e.matmul(psLg[:, kt * 8:(kt + 1) * 8], lhsT,
                                         rhs[:, ci * 8:(ci + 1) * 8],
                                         start=(nmm == 1), stop=(nmm == 24)))
        m_lg = cPE.n
        m_loslab_pe[b] = cPE.n

        lg3 = psLg[:].rearrange("p (t e) -> p t e", e=8)
        dve.need(cPE, m_lg)
        dve.need(cDVE, m_s4_dve[b])
        cDVE.tick(dve.e.tensor_reduce(
            out=statg[b][:, SM::8].rearrange("p (t o) -> p t o", o=1),
            in_=lg3, axis=AX.X, op=ALU.max))
        dve.need(cDVE)
        cDVE.tick(dve.e.tensor_tensor(
            out=prb[b][:].rearrange("p (t e) -> p t e", e=8), in0=lg3,
            in1=statg[b][:, SM::8].rearrange("p (t o) -> p t o", o=1)
                .broadcast_to([128, 4, 8]),
            op=ALU.subtract))
        m_lgsub = cDVE.n
        act.need(cDVE, m_lgsub)
        cACT.tick(act.e.activation(out=prb[b][:], in_=prb[b][:], func=AF.Exp))
        m_exp = cACT.n
        dve.need(cACT, m_exp)
        cDVE.tick(dve.e.tensor_reduce(
            out=statg[b][:, SZ::8].rearrange("p (t o) -> p t o", o=1),
            in_=prb[b][:].rearrange("p (t e) -> p t e", e=8),
            axis=AX.X, op=ALU.add))
        dve.need(cDVE)
        cDVE.tick(dve.e.reciprocal(out=statg[b][:, SGATE::8],
                                   in_=statg[b][:, SZ::8]))
        dve.need(cDVE)
        cDVE.tick(dve.e.tensor_tensor(
            out=ism[b][:].rearrange("p (t e) -> p t e", e=8), in0=lg3,
            in1=statg[b][:, SM::8].rearrange("p (t o) -> p t o", o=1)
                .broadcast_to([128, 4, 8]),
            op=ALU.is_ge))
        m_lg_dve = cDVE.n
        dve.need(cDVE)
        dve.need(sA8, mA_c)
        cDVE.tick(dve.e.tensor_tensor_scan(out=cumg[b][:], data0=mask84[:],
                                           data1=ism[b][:], initial=0.0,
                                           op0=ALU.mult, op1=ALU.add))
        dve.need(cDVE)
        cDVE.tick(dve.e.tensor_scalar(out=cumg[b][:], in0=cumg[b][:],
                                      scalar1=1.0, scalar2=None,
                                      op0=ALU.is_equal))
        dve.need(cDVE)
        cDVE.tick(dve.e.tensor_tensor(out=oh_sb[:, g * 32:(g + 1) * 32],
                                      in0=ism[b][:], in1=cumg[b][:],
                                      op=ALU.mult))
        m_oh = cDVE.n
        dve.need(cDVE)
        cDVE.tick(dve.e.tensor_tensor(out=ism[b][:],
                                      in0=oh_sb[:, g * 32:(g + 1) * 32],
                                      in1=iota32[:], op=ALU.mult))
        dve.need(cDVE)
        cDVE.tick(dve.e.tensor_reduce(
            out=statg[b][:, SEVAL::8].rearrange("p (t o) -> p t o", o=1),
            in_=ism[b][:].rearrange("p (t e) -> p t e", e=8),
            axis=AX.X, op=ALU.add))
        m_ohv[b] = m_oh

    def a_h2b(g):
        nonlocal m_runoff_dve, m_cnt_dve
        b = g % 2
        m_oh = m_ohv[b]
        m_hi = m_hiv[b]

        pe.need(cDVE, m_oh)
        pe.need(cDVE, m_pos_dve[b])
        for kt in range(GT):
            cPE.tick(pe.e.matmul(psPos[b][:, kt * 8:(kt + 1) * 8], trilS[:],
                                 oh_sb[:, (g * GT + kt) * 8:(g * GT + kt + 1) * 8],
                                 start=(kt == 0), stop=False))
        pe.need(cDVE, m_cnt_dve)
        cPE.tick(pe.e.matmul(psCnt[:], onesc[:], oh_sb[:, g * 32:(g + 1) * 32],
                             start=True, stop=True))
        m_cntmm = cPE.n
        dve.need(cPE, m_cntmm)
        dve.need(cDVE, m_runoff_dve)
        cDVE.tick(dve.e.tensor_copy(out=cnt4[:], in_=psCnt[:]))
        dve.need(cDVE)
        cDVE.tick(dve.e.tensor_copy(out=runvec[0:1, 0:8], in_=runoff[:]))
        for k in range(1, GT + 1):
            dve.need(cDVE)
            if k < GT:
                cDVE.tick(dve.e.tensor_tensor(
                    out=runvec[0:1, k * 8:(k + 1) * 8],
                    in0=runvec[0:1, (k - 1) * 8: k * 8],
                    in1=cnt4[0:1, (k - 1) * 8: k * 8], op=ALU.add))
            else:
                cDVE.tick(dve.e.tensor_tensor(
                    out=runoff[:], in0=runvec[0:1, (k - 1) * 8: k * 8],
                    in1=cnt4[0:1, (k - 1) * 8: k * 8], op=ALU.add))
        m_runoff_dve = cDVE.n
        m_cnt_dve = cDVE.n
        m_runvec = cDVE.n
        pe.need(cDVE, m_runvec)
        cPE.tick(pe.e.matmul(psPos[b][:], onesr[:], runvec[:],
                             start=False, stop=True))
        m_posmm = cPE.n
        dve.need(cPE, m_posmm)
        cDVE.tick(dve.e.tensor_tensor(out=posmg[b][:],
                                      in0=oh_sb[:, g * 32:(g + 1) * 32],
                                      in1=psPos[b][:], op=ALU.mult))
        m_pos_dve[b] = cDVE.n
        dve.need(cDVE)
        cDVE.tick(dve.e.tensor_reduce(
            out=pos_sb[:, g * GT:(g + 1) * GT]
                .rearrange("p (t o) -> p t o", o=1),
            in_=posmg[b][:].rearrange("p (t e) -> p t e", e=8),
            axis=AX.X, op=ALU.add))
        m_pos = cDVE.n

        dve.need(cDVE, m_pos)
        cDVE.tick(dve.e.tensor_scalar(out=s4[b][:, QBAD::4],
                                      in0=pos_sb[:, g * GT:(g + 1) * GT],
                                      scalar1=float(ECAP), scalar2=None,
                                      op0=ALU.is_ge))
        dve.need(cDVE)
        cDVE.tick(dve.e.tensor_scalar(out=s4[b][:, QOKL::4],
                                      in0=s4[b][:, QBAD::4],
                                      scalar1=-1.0, scalar2=1.0,
                                      op0=ALU.mult, op1=ALU.add))
        dve.need(cDVE)
        cDVE.tick(dve.e.tensor_tensor(out=gate_sb[:, g * GT:(g + 1) * GT],
                                      in0=statg[b][:, SGATE::8],
                                      in1=s4[b][:, QOKL::4], op=ALU.mult))
        dve.need(cDVE)
        cDVE.tick(dve.e.scalar_tensor_tensor(
            out=s4[b][:, QSLT::4], in0=statg[b][:, SEVAL::8],
            scalar=float(ECAP), in1=pos_sb[:, g * GT:(g + 1) * GT],
            op0=ALU.mult, op1=ALU.add))
        m_stat_dve[b] = cDVE.n
        dve.need(cDVE)
        cDVE.tick(dve.e.scalar_tensor_tensor(
            out=s4[b][:, QSLT::4], in0=s4[b][:, QBAD::4], scalar=1.0e6,
            in1=s4[b][:, QSLT::4], op0=ALU.mult, op1=ALU.add))
        dve.need(cDVE)
        cDVE.tick(dve.e.tensor_copy(out=slotix[:, g * GT:(g + 1) * GT],
                                    in_=s4[b][:, QSLT::4]))
        m_slot = cDVE.n
        m_s4_dve[b] = cDVE.n

        pool.need(sZX, mA_zx)
        for kt in range(GT):
            io = g * GT + kt
            pool.need(cDVE, m_slot)
            pool.need(cDVE, m_hi)
            sXG[b].tick(pool.e.indirect_dma_start(
                out=xg_dram,
                out_offset=IndirectOffsetOnAxis(slotix[:, io:io + 1], 0),
                in_=hig[b][:, kt * 1024:(kt + 1) * 1024], in_offset=None,
                bounds_check=rbound, oob_is_err=False))
        m_hi_pl[b] = sXG[b].n

    def b0_slab(t0, sl):
        tw = t0 % XW
        pe.need(sWP, mW_wproj)
        pe.need(cDVE, m_xnfm_tile[(t0 + sl - 1) // 128])
        for h in range(NH):
            bb = h % 2
            pe.need(cACT, m_psq_ev[bb])
            pe.need(cDVE, m_psv_ev[bb])
            for ci in range(NF):
                cPE.tick(pe.e.matmul(
                    psQ[bb][:, 0:sl],
                    wq_sb[:, ci * D + h * 128: ci * D + (h + 1) * 128],
                    xnfm[:, ci * XW + tw: ci * XW + tw + sl],
                    start=(ci == 0), stop=(ci == NF - 1)))
            m_mmq = cPE.n
            act.need(cPE, m_mmq)
            act.need(sQ[bb], m_qb_sp[bb])
            cACT.tick(act.e.activation(out=qb16[bb][:, 0:sl],
                                       in_=psQ[bb][:, 0:sl], func=AF.Silu))
            m_psq_ev[bb] = cACT.n
            sp.need(cACT)
            sQ[bb].tick(sp.e.dma_start(
                out=q_dram[h * 128:(h + 1) * 128, t0:t0 + sl],
                in_=qb16[bb][:, 0:sl]))
            m_qb_sp[bb] = sQ[bb].n
        for h in range(NH):
            bb = h % 2
            pe.need(cACT, m_psq_ev[bb])
            pe.need(cDVE, m_psv_ev[bb])
            for ci in range(NF):
                cPE.tick(pe.e.matmul(
                    psQ[bb][:, 0:sl],
                    wf_sb[:, ci * D + h * 128: ci * D + (h + 1) * 128],
                    xnfm[:, ci * XW + tw: ci * XW + tw + sl],
                    start=(ci == 0), stop=(ci == NF - 1)))
            m_mmz = cPE.n
            act.need(cPE, m_mmz)
            act.need(cACT, m_ub_act[bb])
            cACT.tick(act.e.activation(out=ub[bb][:, 0:sl], in_=psQ[bb][:, 0:sl],
                                       func=AF.Exp, scale=-1.0))
            m_psq_ev[bb] = cACT.n
            act.need(cACT)
            act.need(sVB[bb], m_vb_sp[bb])
            cACT.tick(act.e.activation(out=vb[bb][:, 0:sl], in_=ub[bb][:, 0:sl],
                                       func=AF.Ln, bias=actb[:, 1:2]))
            m_ub_act[bb] = cACT.n
            sp.need(cACT)
            sVB[bb].tick(sp.e.dma_start(
                out=vraw_dram[h * 128:(h + 1) * 128, t0:t0 + sl],
                in_=vb[bb][:, 0:sl]))
            m_vb_sp[bb] = sVB[bb].n
        for ts in range(sl // 128):
            bb = ts % 2
            tt0 = t0 + ts * 128
            tw2 = tt0 % XW
            for dh in range(2):
                pe.need(cDVE, m_psv_ev[bb])
                pe.need(cACT, m_psq_ev[bb])
                for ci in range(NF):
                    cPE.tick(pe.e.matmul(
                        psQ[bb][:],
                        xnfm[:, ci * XW + tw2: ci * XW + tw2 + 128],
                        wi_sb[:, ci * D + dh * 512: ci * D + (dh + 1) * 512],
                        start=(ci == 0), stop=(ci == NF - 1)))
                m_mmv = cPE.n
                dve.need(cPE, m_mmv)
                dve.need(sVV[bb], m_vv_sp[bb])
                cDVE.tick(dve.e.tensor_copy(
                    out=vvb[bb][:, dh * 512:(dh + 1) * 512], in_=psQ[bb][:]))
                m_psv_ev[bb] = cDVE.n
            sp.need(cDVE, m_psv_ev[bb])
            sVV[bb].tick(sp.e.dma_start(out=vtok_dram[tt0:tt0 + 128, :],
                                        in_=vvb[bb][:]))
            m_vv_sp[bb] = sVV[bb].n

    a_h1(0)
    for dst, srcw in [(wq_sb, wqT), (wf_sb, wfT), (wi_sb, wiT)]:
        sWP.tick(pool.e.dma_start(
            out=dst[:].rearrange("p (c n) -> p c n", c=NF),
            in_=srcw.rearrange("(c p) n -> p c n", p=128)))
    mW_wproj = sWP.n
    for g in range(8):
        a_h2(g)
        if g + 1 < 8:
            a_h1(g + 1)
        b0_slab(g * 512, 512)
        a_h2b(g)
    b0_slab(4096, 128)

    mA_dve = cDVE.n
    mA_counts = m_runoff_dve

    if "oh" in taps:
        tap("oh", [128, NT * 8], BF16)
        sp.need(cDVE, mA_dve)
        sTAP.tick(sp.e.dma_start(out=tap_outs["oh"], in_=oh_sb[:]))
    if "pos" in taps:
        tap("pos", [128, NT])
        sp.need(cDVE, mA_dve)
        sTAP.tick(sp.e.dma_start(out=tap_outs["pos"], in_=pos_sb[:]))
    if "gate" in taps:
        tap("gate", [128, NT])
        sp.need(cDVE, mA_dve)
        sTAP.tick(sp.e.dma_start(out=tap_outs["gate"], in_=gate_sb[:]))
    if "counts" in taps:
        tap("counts", [1, 8])
        sp.need(cDVE, mA_counts)
        sTAP.tick(sp.e.dma_start(out=tap_outs["counts"], in_=runoff[:]))
    if "slot" in taps:
        tap("slot", [128, NT], I32)
        sp.need(cDVE, mA_dve)
        sTAP.tick(sp.e.dma_start(out=tap_outs["slot"], in_=slotix[:]))

    if not no_coll:
        pool.need(cDVE, mA_counts)
        sCC1.tick(pool.e.dma_start(out=cc_in, in_=runoff[:]))
        pool.need(sCC1)
        cCC.tick(pool.e.collective_compute(
            "AllGather", ALU.bypass, replica_groups=[[0, 1, 2, 3, 4, 5, 6, 7]],
            ins=[cc_in], outs=[cc_out]))

    for bb in range(2):
        sp.need(sQ[bb], m_qb_sp[bb])
        sp.need(sVB[bb], m_vb_sp[bb])
        sp.need(sVV[bb], m_vv_sp[bb])
    if "q" in taps:
        tap("q", [D, TH], BF16)
        sTAP.tick(sp.e.dma_start(out=tap_outs["q"], in_=q_dram))
    if "vraw" in taps:
        tap("vraw", [D, TH])
        sTAP.tick(sp.e.dma_start(out=tap_outs["vraw"], in_=vraw_dram))
    if "vtok" in taps:
        tap("vtok", [TH, D], BF16)
        sTAP.tick(sp.e.dma_start(out=tap_outs["vtok"], in_=vtok_dram))

    ctxB0.close()
    ctxA.close()
    ctx1.close()
    barrier()

    SEG = 22
    SEGT = SEG * L
    NSEGG = SEG // 2
    ctxB = ExitStack()
    sbB = lambda n, s, d: ctxB.enter_context(nc.sbuf_tensor(n, s, d))
    psB = lambda n, s, d=F32: ctxB.enter_context(nc.psum_tensor(n, s, d))

    scanm = sbB("scanm", [128, TH], F32)
    masksc = sbB("masksc", [64, 64], F32)
    qhS = [[sbB(f"qhS{p}_{h}", [128, SEGT], BF16) for h in range(2)]
           for p in range(2)]
    vrS = [[sbB(f"vrS{p}_{h}", [128, SEGT], F32) for h in range(2)]
           for p in range(2)]
    vvS = [[sbB(f"vvS{p}_{h}", [128, SEGT], BF16) for h in range(2)]
           for p in range(2)]
    qtS = [[sbB(f"qtS{p}_{h}", [128, SEGT], BF16) for h in range(2)]
           for p in range(2)]
    ktS = [[sbB(f"ktS{p}_{h}", [128, SEGT], BF16) for h in range(2)]
           for p in range(2)]
    khS = [[sbB(f"khS{p}_{h}", [128, SEGT], BF16) for h in range(2)]
           for p in range(2)]
    ktokS = [[sbB(f"ktokS{p}_{h}", [128, SEGT], BF16) for h in range(2)]
             for p in range(2)]
    BnegS = [[sbB(f"BnegS{p}_{h}", [128, SEGT], F32) for h in range(2)]
             for p in range(2)]
    EmBS = [[sbB(f"EmBS{p}_{h}", [128, SEG * (L + 1)], F32) for h in range(2)]
            for p in range(2)]
    aendS = [[sbB(f"aendS{p}_{h}", [128, SEG], F32) for h in range(2)]
             for p in range(2)]
    S16 = [sbB(f"S16B{h}", [128, 128], BF16) for h in range(2)]
    sc16 = [sbB(f"sc16B{h}", [128, 64], BF16) for h in range(2)]
    ofh = [sbB(f"ofhB{h}", [128, T], BF16) for h in range(2)]
    psSC = [psB(f"psSCB{h}", [64, 64], F32) for h in range(2)]
    psO = [psB(f"psOB{h}", [128, 64], F32) for h in range(2)]
    psDS = [[psB(f"psDSB{h}_{b}", [128, 256], F32) for b in range(2)]
            for h in range(2)]

    sBL2[0].tick(sp.e.dma_start(out=scanm[:], in_=c_scanm))
    sBL2[0].tick(sp.e.dma_start(out=masksc[:], in_=c_masksc))
    mB_scanm_ld = sBL2[0].n
    for p in range(2):
        for h2 in range(2):
            EmBr = EmBS[p][h2][:].rearrange("p (g n) -> p g n", n=L + 1)
            cDVE.tick(dve.e.memset(EmBr[:, :, 0:1], 1.0))
    mB_emb1 = cDVE.n

    NSEG = 3
    NST = (NH // 2) * NSEG
    cPOOL = Ctr(nc, "cPOOL")

    mP_pe = [0, 0]
    mP_dve = [0, 0]
    mP_act = [0, 0]
    mP_pool = [0, 0]
    m_ld = [0, 0]
    m_ofh_sp = [0, 0]

    def b_loads(st):
        p = st % 2
        hp, s = divmod(st, NSEG)
        t0 = s * SEGT
        sp.need(cPE, mP_pe[p])
        sp.need(cDVE, mP_dve[p])
        sp.need(cACT, mP_act[p])
        sp.need(cPOOL, mP_pool[p])
        vt = vtok_dram.rearrange("(j p) d -> p j d", p=128)
        for h2 in range(2):
            h = hp * 2 + h2
            sBL2[p].tick(sp.e.dma_start(out=qhS[p][h2][:],
                                    in_=q_dram[h * 128:(h + 1) * 128, t0:t0 + SEGT]))
            sBL2[p].tick(sp.e.dma_start(out=vrS[p][h2][:],
                                    in_=vraw_dram[h * 128:(h + 1) * 128, t0:t0 + SEGT]))
            vvr = vvS[p][h2][:].rearrange("p (j d) -> p j d", d=128)
            g0 = s * NSEGG
            sBL2[p].tick(sp.e.dma_start(out=vvr[:, 0:6, :],
                                    in_=vt[:, g0:g0 + 6, h * 128:(h + 1) * 128]))
            sBL2[p].tick(sp.e.dma_start(out=vvr[:, 6:NSEGG, :],
                                    in_=vt[:, g0 + 6:g0 + NSEGG, h * 128:(h + 1) * 128]))
        m_ld[p] = sBL2[p].n

    def b_prep_emit(st):
        p = st % 2
        hp, s = divmod(st, NSEG)
        ops = []
        marks = {}

        def scan_h(h2):
            def t():
                dve.need(sBL2[0], sBL2[0].n)
                dve.need(sBL2[1], sBL2[1].n)
                dve.need(cPE, mP_pe[p])
                dve.need(cPOOL, mP_pool[p])
                cDVE.tick(dve.e.tensor_tensor_scan(
                    out=BnegS[p][h2][:], data0=scanm[:, s * SEGT:(s + 1) * SEGT],
                    data1=vrS[p][h2][:], initial=0.0,
                    op0=ALU.mult, op1=ALU.add))
                marks[("scan", h2)] = cDVE.n
            return t

        def emb_h(h2):
            def t():
                act.need(cDVE, marks[("scan", h2)])
                act.need(cDVE, mB_emb1)
                act.need(cPOOL, mP_pool[p])
                EmBr = EmBS[p][h2][:].rearrange("p (g n) -> p g n", n=L + 1)
                cACT.tick(act.e.activation(
                    out=EmBr[:, :, 1:L + 1],
                    in_=BnegS[p][h2][:].rearrange("p (g n) -> p g n", n=L),
                    func=AF.Exp, scale=1.0))
                cACT.tick(act.e.activation(
                    out=aendS[p][h2][:].rearrange("p (g o) -> p g o", o=1),
                    in_=BnegS[p][h2][:].rearrange("p (g n) -> p g n", n=L)[:, :, L - 1:L],
                    func=AF.Exp, scale=-1.0))
                act.need(cACT, cACT.n)
                cACT.tick(act.e.activation(out=BnegS[p][h2][:], in_=BnegS[p][h2][:],
                                           func=AF.Exp, scale=-1.0,
                                           bias=actb[:, 2:3]))
                marks[("emb", h2)] = cACT.n
            return t

        def kq_h(h2):
            def t():
                pool.need(cACT, marks[("emb", h2)])
                pool.need(cPE, mP_pe[p])
                EmBr = EmBS[p][h2][:].rearrange("p (g n) -> p g n", n=L + 1)
                cPOOL.tick(pool.e.tensor_tensor(
                    out=ktS[p][h2][:].rearrange("p (g n) -> p g n", n=L),
                    in0=EmBr[:, :, 1:L + 1], in1=EmBr[:, :, 0:L],
                    op=ALU.subtract))
                pool.need(cPOOL, cPOOL.n)
                cPOOL.tick(pool.e.tensor_tensor(
                    out=khS[p][h2][:].rearrange("p (g n) -> p g n", n=L),
                    in0=ktS[p][h2][:].rearrange("p (g n) -> p g n", n=L),
                    in1=aendS[p][h2][:].rearrange("p (g o) -> p g o", o=1)
                        .broadcast_to([128, SEG, L]),
                    op=ALU.mult))
                cPOOL.tick(pool.e.tensor_tensor(
                    out=qtS[p][h2][:], in0=qhS[p][h2][:], in1=BnegS[p][h2][:],
                    op=ALU.mult))
                marks[("kq", h2)] = cPOOL.n
            return t

        def tr_h(h2, g):
            def t():
                sp.need(cPOOL, marks[("kq", h2)])
                sp.need(cPE, mP_pe[p])
                sKT2[p].tick(sp.e.dma_start(
                    out=ktokS[p][h2][:, g * 128:(g + 1) * 128],
                    in_=khS[p][h2][:, g * 128:(g + 1) * 128],
                    transpose=True))
                marks[("ktr",)] = sKT2[p].n
            return t

        for h2 in range(2):
            ops.append(scan_h(h2))
            ops.append(emb_h(h2))
            ops.append(kq_h(h2))
        for g in range(NSEGG):
            for h2 in range(2):
                ops.append(tr_h(h2, g))
        return ops, marks

    m_sc_dve = [0, 0]
    m_psds_dve = [[0, 0], [0, 0]]
    m_o_dve = [0, 0]

    def b_chunks(st, prep_ops, prep_marks):
        p = st % 2
        hp, s = divmod(st, NSEG)
        nonlocal_marks = b_state[hp]
        m_state = nonlocal_marks["m_state"]
        m_omm = nonlocal_marks["m_omm"]
        pq = list(prep_ops)
        it = 0
        for cl in range(SEG):
            cg = s * SEG + cl
            own = cg >= 2
            j, po = cl // 2, (cl % 2) * 64
            dsb = cl % 2
            dsls = [psDS[h2][dsb][:, ((cl // 2) % 2) * 128:
                                  ((cl // 2) % 2 + 1) * 128] for h2 in range(2)]
            vvv = [vvS[p][h2][:].rearrange("p (j d) -> p j d", d=128)
                   [po:po + 64, j, :] for h2 in range(2)]
            m_ds = [0, 0]
            m_scmm = [0, 0]
            for h2 in range(2):
                pe.need(sKT2[p], prep_marks_cur[("ktr",)])
                pe.need(cPOOL, prep_marks_cur[("kq", h2)])
                pe.need(cDVE, m_psds_dve[h2][dsb])
                cPE.tick(pe.e.matmul(
                    dsls[h2], ktokS[p][h2][po:po + 64, j * 128:(j + 1) * 128],
                    vvv[h2], start=True, stop=True))
                m_ds[h2] = cPE.n
            if own:
                for h2 in range(2):
                    pe.need(cDVE, m_sc_dve[h2])
                    cPE.tick(pe.e.matmul(psSC[h2][:],
                                         ktS[p][h2][:, cl * 64:(cl + 1) * 64],
                                         qtS[p][h2][:, cl * 64:(cl + 1) * 64],
                                         start=True, stop=True))
                    m_scmm[h2] = cPE.n
                for h2 in range(2):
                    dve.need(cPE, m_scmm[h2])
                    cDVE.tick(dve.e.tensor_tensor(out=sc16[h2][po:po + 64, :],
                                                  in0=psSC[h2][:],
                                                  in1=masksc[:], op=ALU.mult))
                    m_sc_dve[h2] = cDVE.n
                for h2 in range(2):
                    pe.need(cDVE, m_sc_dve[h2])
                    pe.need(cACT, m_o_dve[h2])
                    cPE.tick(pe.e.matmul(psO[h2][:], vvv[h2],
                                         sc16[h2][po:po + 64, :],
                                         start=True, stop=False))
                    pe.need(cDVE, m_state[h2])
                    cPE.tick(pe.e.matmul(psO[h2][:], S16[h2][:],
                                         qtS[p][h2][:, cl * 64:(cl + 1) * 64],
                                         start=False, stop=True))
                    m_omm[h2] = cPE.n
            for h2 in range(2):
                dve.need(cPE, m_ds[h2])
                dve.need(cACT, prep_marks_cur[("emb", h2)])
                dve.need(cDVE, m_state[h2])
                if own:
                    dve.need(cPE, m_omm[h2])
                cDVE.tick(dve.e.scalar_tensor_tensor(
                    out=S16[h2][:], in0=S16[h2][:],
                    scalar=aendS[p][h2][:, cl:cl + 1], in1=dsls[h2],
                    op0=ALU.mult, op1=ALU.add))
                m_psds_dve[h2][dsb] = cDVE.n
                m_state[h2] = cDVE.n
            if own:
                for h2 in range(2):
                    act.need(cPE, m_omm[h2])
                    act.need(sOF2[h2], m_ofh_sp[h2])
                    cACT.tick(act.e.copy(
                        out=ofh[h2][:, (cg - 2) * 64:(cg - 1) * 64],
                        in_=psO[h2][:]))
                    m_o_dve[h2] = cACT.n
            it += 1
            if it >= 4:
                for _ in range(2):
                    if pq:
                        pq.pop(0)()
        while pq:
            pq.pop(0)()
        nonlocal_marks["m_state"] = m_state
        nonlocal_marks["m_omm"] = m_omm

    b_state = {}
    b_loads(0)
    prep0_ops, prep0_marks = b_prep_emit(0)
    for t in prep0_ops:
        t()
    prep_marks_cur = prep0_marks
    b_loads(1)
    next_prep = b_prep_emit(1)
    for st in range(NST):
        p = st % 2
        hp, s = divmod(st, NSEG)
        if s == 0:
            for h2 in range(2):
                dve.need(cACT, mP_act[p])
                dve.need(cDVE, cDVE.n)
                dve.need(cPE, b_state.get(hp - 1, {}).get("m_omm", [0, 0])[h2]
                         if hp > 0 else 0)
                cDVE.tick(dve.e.memset(S16[h2][:], 0.0))
            b_state[hp] = dict(m_state=[cDVE.n, cDVE.n], m_omm=[0, 0])
        if st + 1 < NST:
            prep_ops_next, prep_marks_next = next_prep
        else:
            prep_ops_next, prep_marks_next = [], {}
        b_chunks(st, prep_ops_next, prep_marks_next)
        mP_pe[p] = cPE.n
        mP_dve[p] = cDVE.n
        mP_act[p] = cACT.n
        mP_pool[p] = cPOOL.n
        prep_marks_cur = prep_marks_next if prep_marks_next else prep_marks_cur
        if st + 2 < NST:
            b_loads(st + 2)
            next_prep = b_prep_emit(st + 2)
        if s == NSEG - 1:
            for h2 in range(2):
                h = hp * 2 + h2
                sp.need(cACT, cACT.n)
                sOF2[h2].tick(sp.e.dma_start(out=ofm_dram[h * 128:(h + 1) * 128, :],
                                             in_=ofh[h2][:]))
                m_ofh_sp[h2] = sOF2[h2].n

    if "ofm" in taps:
        tap("ofm", [D, T], BF16)
        for h2 in range(2):
            sp.need(sOF2[h2], m_ofh_sp[h2])
        sTAP.tick(sp.e.dma_start(out=tap_outs["ofm"], in_=ofm_dram))

    ctxB.close()
    barrier()

    ctxB2 = ExitStack()
    sbB2 = lambda n, s, d: ctxB2.enter_context(nc.sbuf_tensor(n, s, d))
    psB2 = lambda n, s, d=F32: ctxB2.enter_context(nc.psum_tensor(n, s, d))

    wo16 = sbB2("wo16", [128, NF * D], BF16)
    wof = [sbB2(f"wof{b}", [128, D], F32) for b in range(2)]
    onesc16 = sbB2("onesc16", [128, 1], BF16)
    oslab = [sbB2(f"oslab{b}", [128, NF * 512], BF16) for b in range(2)]
    o2s = sbB2("o2s", [128, NF * 512], BF16)
    of16 = sbB2("of16", [128, NF * 512], BF16)
    rstdr = sbB2("rstdr", [1, 512], F32)
    s16b = [sbB2(f"s16b{b}", [128, 512], BF16) for b in range(2)]
    stok = sbB2("stok", [128, 4096], BF16)
    psSS = psB2("psSS", [1, 512], F32)
    psRep = psB2("psRep", [128, 512], F32)
    psS = [psB2(f"psS{b}", [128, 512], F32) for b in range(2)]
    psTr2 = [psB2(f"psTr2{b}", [128, 128], BF16) for b in range(2)]

    cDVE.tick(dve.e.memset(onesc16[:], 1.0))
    mB2_ones = cDVE.n
    m_wof_dve = [0, 0]
    for ci in range(NF):
        bb = ci % 2
        sp.need(cDVE, m_wof_dve[bb])
        sWO[bb].tick(sp.e.dma_start(out=wof[bb][:],
                                    in_=woT[ci * 128:(ci + 1) * 128, :]))
        dve.need(sWO[bb])
        dve.need(sW, mW_consts)
        cDVE.tick(dve.e.tensor_scalar(out=wo16[:, ci * D:(ci + 1) * D],
                                      in0=wof[bb][:],
                                      scalar1=gw_sb[:, ci:ci + 1], scalar2=None,
                                      op0=ALU.mult))
        m_wof_dve[bb] = cDVE.n
    mB2_wo = cDVE.n

    m_osl_act = [0, 0]
    m_osl_dve = [0, 0]
    m_o2_pe = 0
    m_of_pe = 0
    m_psS_dve = [0, 0]
    m_s16b_pe = [0, 0]
    m_psTr2_ev = [0, 0]
    m_rep_dve = 0
    m_stok_sp = 0
    ofr = ofm_dram.rearrange("(c p) t -> p c t", p=128)
    for sl8 in range(8):
        t0 = sl8 * 512
        bb = sl8 % 2
        sp.need(cACT, m_osl_act[bb])
        sp.need(cDVE, m_osl_dve[bb])
        sp.need(sOF2[0], m_ofh_sp[0])
        sp.need(sOF2[1], m_ofh_sp[1])
        sOSL[bb].tick(sp.e.dma_start(
            out=oslab[bb][:].rearrange("p (c n) -> p c n", n=512),
            in_=ofr[:, :, t0:t0 + 512]))
        m_osl = sOSL[bb].n
        act.need(sOSL[bb], m_osl)
        act.need(cPE, m_o2_pe)
        cACT.tick(act.e.activation(out=o2s[:], in_=oslab[bb][:], func=AF.Square))
        m_o2 = cACT.n
        m_osl_act[bb] = cACT.n
        pe.need(cACT, m_o2)
        pe.need(cDVE, mB2_ones)
        pe.need(cACT, 0)
        for ci in range(NF):
            cPE.tick(pe.e.matmul(psSS[:], onesc16[:],
                                 o2s[:, ci * 512:(ci + 1) * 512],
                                 start=(ci == 0), stop=(ci == NF - 1)))
        m_ss = cPE.n
        m_o2_pe = cPE.n
        act.need(cPE, m_ss)
        cACT.tick(act.e.activation(out=rstdr[:], in_=psSS[:], func=AF.Ln,
                                   scale=1.0 / D, bias=actb[0:1, 0:1]))
        act.need(cACT)
        cACT.tick(act.e.activation(out=rstdr[:], in_=rstdr[:], func=AF.Exp,
                                   scale=-0.5))
        m_rstd = cACT.n
        pe.need(cACT, m_rstd)
        pe.need(cDVE, m_rep_dve)
        cPE.tick(pe.e.matmul(psRep[:], onesr[:], rstdr[:], start=True, stop=True))
        m_rep = cPE.n
        dve.need(cPE, m_rep)
        dve.need(cPE, m_of_pe)
        for ci in range(NF):
            cDVE.tick(dve.e.tensor_tensor(out=of16[:, ci * 512:(ci + 1) * 512],
                                          in0=oslab[bb][:, ci * 512:(ci + 1) * 512],
                                          in1=psRep[:], op=ALU.mult))
        m_of = cDVE.n
        m_osl_dve[bb] = cDVE.n
        m_rep_dve = cDVE.n
        for dc in range(NF):
            b2 = dc % 2
            pe.need(cDVE, m_of)
            pe.need(cDVE, mB2_wo)
            pe.need(cDVE, m_psS_dve[b2])
            for ci in range(NF):
                cPE.tick(pe.e.matmul(
                    psS[b2][:],
                    wo16[:, ci * D + dc * 128: ci * D + (dc + 1) * 128],
                    of16[:, ci * 512:(ci + 1) * 512],
                    start=(ci == 0), stop=(ci == NF - 1)))
            m_smm = cPE.n
            m_of_pe = cPE.n
            dve.need(cPE, m_smm)
            dve.need(cPE, m_s16b_pe[b2])
            cDVE.tick(dve.e.tensor_copy(out=s16b[b2][:], in_=psS[b2][:]))
            m_s16c = cDVE.n
            m_psS_dve[b2] = cDVE.n
            for ts in range(4):
                pp = ts % 2
                pe.need(cDVE, m_s16c)
                pe.need(cDVE, m_psTr2_ev[pp])
                cPE.tick(pe.e.transpose(psTr2[pp][:],
                                        s16b[b2][:, ts * 128:(ts + 1) * 128],
                                        ident16[:]))
                dve.need(cPE)
                dve.need(sST, m_stok_sp)
                cDVE.tick(dve.e.tensor_copy(
                    out=stok[:, ts * 1024 + dc * 128: ts * 1024 + (dc + 1) * 128],
                    in_=psTr2[pp][:]))
                m_psTr2_ev[pp] = cDVE.n
            m_s16b_pe[b2] = cPE.n
        m_stok = cDVE.n
        for ts in range(4):
            sp.need(cDVE, m_stok)
            sST.tick(sp.e.dma_start(
                out=s_dram[t0 + ts * 128: t0 + (ts + 1) * 128, :],
                in_=stok[:, ts * 1024:(ts + 1) * 1024]))
        m_stok_sp = sST.n
    mB2_sp = sST.n

    if "s" in taps:
        tap("s", [T, D], BF16)
        sp.need(sST, mB2_sp)
        sTAP.tick(sp.e.dma_start(out=tap_outs["s"], in_=s_dram))

    ctxB2.close()
    barrier()

    ctxC = ExitStack()
    sbC = lambda n, s, d: ctxC.enter_context(nc.sbuf_tensor(n, s, d))
    psC = lambda n, s, d=F32: ctxC.enter_context(nc.psum_tensor(n, s, d))

    w1h = [sbC(f"w1h{b}", [128, NF * 2048], BF16) for b in range(2)]
    w2s = [sbC(f"w2s{b}", [128, 8 * 512], BF16) for b in range(2)]
    acc_sb = sbC("acc_sb", [128, 6 * 512], F32)
    xgfm = sbC("xgfm", [128, NF * ECAP], BF16)
    h_sb = sbC("h_sb", [128, 32 * ECAP], BF16)
    b1p = sbC("b1p", [128, FF // 128], F32)
    onesr16 = sbC("onesr16", [1, 768], BF16)
    b2f = sbC("b2f", [1, D], F32)
    b2h16 = sbC("b2h16", [1, D], BF16)
    b2l16 = sbC("b2l16", [1, D], BF16)
    mo6 = sbC("mo6", [128, 6 * D], BF16)
    psH = [psC(f"psH{b}", [128, 384], F32) for b in range(2)]
    psO2 = [psC(f"psO2_{j}", [128, 512], F32) for j in range(6)]

    w1r = w1in.rearrange("e (c p) f -> e p c f", p=128)
    w2r = w2in.rearrange("e (fc p) d -> e p fc d", p=128)
    xgr = xg_dram.rearrange("(jj p) d -> p jj d", p=128)

    m_w1_pe = [0, 0]
    m_w2_pe = [0, 0]
    m_xgt_pe = 0
    m_xgfm_pe = 0
    m_h_pe = 0
    m_psH_act = [0, 0]
    m_h_act = 0
    m_b1_act = 0
    m_b2_pe = 0
    m_b2_dve = 0
    m_psO2_dve = [0] * 6
    m_acc_dve = [0] * 6
    m_mo_sp = 0
    def c_w1_thunks(e_):
        thunks = []
        for fh in range(2):
            def t(fh=fh):
                pool.need(cPE, m_w1_pe[fh])
                sW1[fh].tick(pool.e.dma_start(
                    out=w1h[fh][:].rearrange("p (c f) -> p c f", c=NF),
                    in_=w1r[e_, :, :, fh * 2048:(fh + 1) * 2048]))
            thunks.append(t)
        return thunks

    def c_loads(e_):
        sp.need(cPE, m_xgt_pe)
        for bq in range(2):
            sp.need(sXG[bq], m_hi_pl[bq])
        sp.need(sZX, mA_zx)
        for j in range(6):
            r0 = e_ * ECAP + j * 128
            for ci in range(NF):
                sXGT.tick(sp.e.dma_start(
                    out=xgfm[:, ci * ECAP + j * 128: ci * ECAP + (j + 1) * 128],
                    in_=xg_dram[r0:r0 + 128, ci * 128:(ci + 1) * 128],
                    transpose=True))
        sp.need(cACT, m_b1_act)
        sB1.tick(sp.e.dma_start(out=b1p[:], in_=b1in[e_, :, :]))
        sp.need(cPE, m_b2_pe)
        sp.need(cDVE, m_b2_dve)
        sB2s.tick(sp.e.dma_start(out=b2f[:], in_=b2in[e_:e_ + 1, :]))
        return dict(xgl=sXGT.n, b1l=sB1.n, bl=sB2s.n)

    cDVE.tick(dve.e.memset(onesr16[:], 1.0))
    mC_ones16 = cDVE.n
    for t in c_w1_thunks(0):
        t()
    c_marks = c_loads(0)
    w1_thunks = []
    for e_ in range(E):
        m_xgl = c_marks["xgl"]
        m_b1l = c_marks["b1l"]
        m_bl = c_marks["bl"]
        dve.need(sB2s, m_bl)
        dve.need(cPE, m_b2_pe)
        cDVE.tick(dve.e.tensor_copy(out=b2h16[:], in_=b2f[:]))
        dve.need(cDVE)
        cDVE.tick(dve.e.tensor_sub(out=b2f[:], in0=b2f[:], in1=b2h16[:]))
        dve.need(cDVE)
        cDVE.tick(dve.e.tensor_copy(out=b2l16[:], in_=b2f[:]))
        m_b2_dve = cDVE.n
        m_b16 = cDVE.n

        m_xgfm = m_xgl

        for fh in range(2):
            for fc in range(16):
                fcg = fh * 16 + fc
                for th in range(2):
                    pb = (fcg * 2 + th) % 2
                    pe.need(sW1[0], sW1[0].n)
                    pe.need(sW1[1], sW1[1].n)
                    pe.need(sXGT, m_xgfm)
                    pe.need(cACT, m_psH_act[pb])
                    for ci in range(NF):
                        cPE.tick(pe.e.matmul(
                            psH[pb][:],
                            w1h[fh][:, ci * 2048 + fc * 128: ci * 2048 + (fc + 1) * 128],
                            xgfm[:, ci * ECAP + th * 384: ci * ECAP + (th + 1) * 384],
                            start=(ci == 0), stop=(ci == NF - 1)))
                    m_h1 = cPE.n
                    act.need(cPE, m_h1)
                    act.need(cPE, m_h_pe)
                    act.need(sB1, m_b1l)
                    cACT.tick(act.e.activation(
                        out=h_sb[:, fcg * ECAP + th * 384: fcg * ECAP + (th + 1) * 384],
                        in_=psH[pb][:], func=AF.Relu,
                        bias=b1p[:, fcg:fcg + 1], scale=1.0))
                    m_psH_act[pb] = cACT.n
        m_h_act = cACT.n
        m_b1_act = cACT.n
        m_w1_pe = [cPE.n, cPE.n]
        m_xgt_pe = cPE.n
        if e_ + 1 < E:
            c_marks = c_loads(e_ + 1)
            w1_thunks = c_w1_thunks(e_ + 1)

        def w2_load(q, dh):
            qb = q % 2
            pool.need(cPE, m_w2_pe[qb])
            sW2[qb].tick(pool.e.dma_start(
                out=w2s[qb][:].rearrange("p (s d) -> p s d", s=8),
                in_=w2r[e_, :, q * 8:(q + 1) * 8, dh * 512:(dh + 1) * 512]))
            return sW2[qb].n

        m_w2l = [0, 0, 0, 0]
        for dh in range(2):
            m_w2l[0] = w2_load(0, dh)
            m_w2l[1] = w2_load(1, dh)
            for q in range(4):
                qb = q % 2
                if w1_thunks and (q == 1 or q == 3):
                    w1_thunks.pop(0)()
                for jj in range(6):
                    pe.need(sW2[qb], m_w2l[q])
                    pe.need(cACT, m_h_act)
                    if q in (0, 2):
                        pe.need(cDVE, m_psO2_dve[jj])
                    for s8 in range(8):
                        fcg = q * 8 + s8
                        cPE.tick(pe.e.matmul(
                            psO2[jj][:],
                            h_sb[:, fcg * ECAP + jj * 128:
                                 fcg * ECAP + (jj + 1) * 128],
                            w2s[qb][:, s8 * 512:(s8 + 1) * 512],
                            start=(fcg % 16 == 0),
                            stop=(q == 1 and s8 == 7)))
                    if q == 3:
                        cPE.tick(pe.e.matmul(
                            psO2[jj][:], onesr16[0:1, 0:128],
                            b2h16[0:1, dh * 512:(dh + 1) * 512],
                            start=False, stop=False))
                        cPE.tick(pe.e.matmul(
                            psO2[jj][:], onesr16[0:1, 0:128],
                            b2l16[0:1, dh * 512:(dh + 1) * 512],
                            start=False, stop=True))
                    m_mm2 = cPE.n
                    if q == 1:
                        dve.need(cPE, m_mm2)
                        dve.need(cDVE, m_acc_dve[jj])
                        cDVE.tick(dve.e.tensor_copy(
                            out=acc_sb[:, jj * 512:(jj + 1) * 512],
                            in_=psO2[jj][:]))
                        m_psO2_dve[jj] = cDVE.n
                    elif q == 3:
                        dve.need(cPE, m_mm2)
                        dve.need(sMO[0], m_mo_sp)
                        cDVE.tick(dve.e.tensor_tensor(
                            out=mo6[:, jj * D + dh * 512:
                                 jj * D + (dh + 1) * 512],
                            in0=psO2[jj][:],
                            in1=acc_sb[:, jj * 512:(jj + 1) * 512],
                            op=ALU.add))
                        m_psO2_dve[jj] = cDVE.n
                        m_acc_dve[jj] = cDVE.n
                m_w2_pe[qb] = cPE.n
                if q + 2 < 4:
                    m_w2l[q + 2] = w2_load(q + 2, dh)
        m_mo = cDVE.n
        for jj in range(6):
            sp.need(cDVE, m_mo)
            sMO[0].tick(sp.e.dma_start(
                out=moe_dram[e_ * ECAP + jj * 128: e_ * ECAP + (jj + 1) * 128, :],
                in_=mo6[:, jj * D:(jj + 1) * D]))
        m_mo_sp = sMO[0].n
        m_b2_pe = cPE.n
        m_h_pe = cPE.n
    mC_mo = m_mo_sp

    if "moe" in taps:
        tap("moe", [NSLOT, D], BF16)
        sp.need(sMO[0], mC_mo)
        sTAP.tick(sp.e.dma_start(out=tap_outs["moe"], in_=moe_dram))
    if "xg" in taps:
        tap("xg", [NSLOT, D], BF16)
        for bq in range(2):
            sp.need(sXG[bq], m_hi_pl[bq])
        sTAP.tick(sp.e.dma_start(out=tap_outs["xg"], in_=xg_dram))

    ctxC.close()
    barrier()

    ctxD = ExitStack()
    sbD = lambda n, s, d: ctxD.enter_context(nc.sbuf_tensor(n, s, d))
    psD = lambda n, s, d=F32: ctxD.enter_context(nc.psum_tensor(n, s, d))

    cco = sbD("cco", [8, 8], F32)
    psCO = psD("psCO", [1, 8], F32)
    psCOr = psD("psCOr", [128, 8], F32)
    if no_coll:
        sCO.tick(sp.e.dma_start(out=coreoff[:], in_=dbg_coreoff))
        pe.need(sCO)
    else:
        pool.need(cCC, 1)
        sCC2.tick(pool.e.dma_start(out=cco[:], in_=cc_out))
        pe.need(sCC2)
        cPE.tick(pe.e.matmul(psCO[:], cmask[:], cco[:], start=True, stop=True))
        dve.need(cPE)
        cDVE.tick(dve.e.tensor_copy(out=coreoff[:], in_=psCO[:]))
        pe.need(cDVE)
    cPE.tick(pe.e.matmul(psCOr[:], onesr[:], coreoff[:], start=True, stop=True))
    dve.need(cPE)
    cDVE.tick(dve.e.tensor_copy(out=coff_rep[:], in_=psCOr[:]))
    mD_rep = cDVE.n

    xd = [sbD(f"xd{b}", [128, D], F32) for b in range(2)]
    mt16 = [sbD(f"mt16{b}", [128, D], BF16) for b in range(2)]
    st16 = [sbD(f"st16{b}", [128, D], BF16) for b in range(2)]
    mg = [sbD(f"mg{b}", [128, D], F32) for b in range(2)]
    gs = [sbD(f"gs{b}", [128, D], F32) for b in range(2)]
    outb = [sbD(f"outb{b}", [128, D], F32) for b in range(2)]
    d8 = [sbD(f"d8_{b}", [128, 32], F32) for b in range(2)]
    dc4 = [sbD(f"dc4_{b}", [128, 4], F32) for b in range(2)]

    m_xd_dve = [0, 0]
    m_mt_dve = [0, 0]
    m_st_dve = [0, 0]
    m_mg_act = [0, 0]
    m_gs_dve = [0, 0]
    m_out_sp = [0, 0]
    m_ge_dve = [0, 0]
    ge4 = [sbD(f"ge4_{b}", [128, 4], F32) for b in range(2)]
    for gq in range(8):
        bq = gq % 2
        dve.need(cDVE, mD_rep)
        dve.need(cDVE, m_ge_dve[bq])
        cDVE.tick(dve.e.tensor_tensor(
            out=d8[bq][:].rearrange("p (t e) -> p t e", e=8),
            in0=oh_sb[:, gq * 32:(gq + 1) * 32]
                .rearrange("p (t e) -> p t e", e=8),
            in1=coff_rep[:].rearrange("p (o e) -> p o e", o=1)
                .broadcast_to([128, 4, 8]),
            op=ALU.mult))
        dve.need(cDVE)
        cDVE.tick(dve.e.tensor_reduce(
            out=dc4[bq][:].rearrange("p (t o) -> p t o", o=1),
            in_=d8[bq][:].rearrange("p (t e) -> p t e", e=8),
            axis=AX.X, op=ALU.add))
        dve.need(cDVE)
        cDVE.tick(dve.e.tensor_tensor(out=dc4[bq][:], in0=dc4[bq][:],
                                      in1=pos_sb[:, gq * 4:(gq + 1) * 4],
                                      op=ALU.add))
        dve.need(cDVE)
        cDVE.tick(dve.e.tensor_scalar(out=dc4[bq][:], in0=dc4[bq][:],
                                      scalar1=float(CAP), scalar2=None,
                                      op0=ALU.is_lt))
        dve.need(cDVE)
        cDVE.tick(dve.e.tensor_tensor(out=ge4[bq][:], in0=dc4[bq][:],
                                      in1=gate_sb[:, gq * 4:(gq + 1) * 4],
                                      op=ALU.mult))
        m_ge = cDVE.n
        m_ge_dve[bq] = cDVE.n
        for kt in range(4):
            io = gq * 4 + kt
            b = io % 2
            sp.need(cDVE, m_xd_dve[b])
            sXD[b].tick(sp.e.dma_start(out=xd[b][:],
                                       in_=x_own[io * 128:(io + 1) * 128, :]))
            m_xl = sXD[b].n
            pool.need(cDVE, m_mt_dve[b])
            pool.need(sMO[0], mC_mo)
            sMG[b].tick(pool.e.indirect_dma_start(
                out=mt16[b][:], out_offset=None,
                in_=moe_dram,
                in_offset=IndirectOffsetOnAxis(slotix[:, io:io + 1], 0),
                bounds_check=rbound, oob_is_err=False))
            m_ml = sMG[b].n
            sp.need(cDVE, m_st_dve[b])
            sp.need(sST, mB2_sp)
            sSL[b].tick(sp.e.dma_start(out=st16[b][:],
                                       in_=s_dram[io * 128:(io + 1) * 128, :]))
            m_sl = sSL[b].n
            dve.need(sMG[b], m_ml)
            dve.need(cACT, m_mg_act[b])
            dve.need(cDVE, m_ge)
            cDVE.tick(dve.e.tensor_scalar(out=mg[b][:], in0=mt16[b][:],
                                          scalar1=ge4[bq][:, kt:kt + 1],
                                          scalar2=None, op0=ALU.mult))
            m_mt_dve[b] = cDVE.n
            m_mg = cDVE.n
            act.need(cDVE, m_mg)
            act.need(cDVE, m_gs_dve[b])
            cACT.tick(act.e.activation(out=gs[b][:], in_=mg[b][:], func=AF.Silu))
            m_mg_act[b] = cACT.n
            m_gsa = cACT.n
            dve.need(cACT, m_gsa)
            dve.need(sSL[b], m_sl)
            dve.need(sY[b], m_out_sp[b])
            cDVE.tick(dve.e.tensor_tensor(out=outb[b][:], in0=gs[b][:],
                                          in1=st16[b][:], op=ALU.mult))
            m_st_dve[b] = cDVE.n
            m_gs_dve[b] = cDVE.n
            dve.need(cDVE)
            dve.need(sXD[b], m_xl)
            cDVE.tick(dve.e.tensor_add(out=outb[b][:], in0=outb[b][:],
                                       in1=xd[b][:]))
            m_xd_dve[b] = cDVE.n
            m_out = cDVE.n
            sp.need(cDVE, m_out)
            sY[b].tick(sp.e.dma_start(out=y_out[io * 128:(io + 1) * 128, :],
                                      in_=outb[b][:]))
            m_out_sp[b] = sY[b].n

    for c in sync_sems:
        sp.need(c, c.n)
    for c in pool_sems:
        pool.need(c, c.n)

    ctxD.close()
    ctx.close()
    return nc, tap_outs


def _consts():
    scanm = np.ones((128, TH), dtype=np.float32)
    scanm[:, ::L] = 0.0
    masksc = np.tril(np.ones((64, 64), dtype=np.float32))
    masksc = np.triu(np.ones((64, 64), dtype=np.float32))
    tril = (np.arange(128)[:, None] < np.arange(128)[None, :]).astype(ml_dtypes.bfloat16)
    return {
        "c_ident": np.eye(128, dtype=ml_dtypes.bfloat16),
        "c_scanm": scanm,
        "c_masksc": masksc,
        "c_tril": tril,
        "c_iota8": np.broadcast_to(np.arange(8, dtype=np.float32), (128, 8)).copy(),
        "c_iota32": np.broadcast_to(np.tile(np.arange(8, dtype=np.float32), 4),
                                    (128, 32)).copy(),
        "c_mask84": np.broadcast_to(
            np.where(np.arange(32) % 8 == 0, 0.0, 1.0).astype(np.float32),
            (128, 32)).copy(),
        "c_ones8": np.ones((128, 8), dtype=np.float32),
        "c_onesc": np.ones((128, 1), dtype=ml_dtypes.bfloat16),
        "c_onesr": np.ones((1, 128), dtype=np.float32),
        "c_onesr16": np.ones((1, 768), dtype=ml_dtypes.bfloat16),
    }


def make_in_maps(inputs, no_coll=False, coreoffs=None):
    x = np.ascontiguousarray(
        np.asarray(inputs["L0_attn_hidden_states"], dtype=np.float32)
        .reshape(2 * 16384, D))
    com = _consts()
    wqT = np.ascontiguousarray(np.asarray(inputs["L0_attn_Wq"], np.float32).T)
    wfT = np.ascontiguousarray(np.asarray(inputs["L0_attn_Wf"], np.float32).T)
    wiT = np.ascontiguousarray(np.asarray(inputs["L0_attn_Wi"], np.float32).T)
    woT = np.ascontiguousarray(np.asarray(inputs["L0_attn_Wo"], np.float32).T)
    gw = np.ascontiguousarray(
        np.asarray(inputs["L0_attn_g_weight"], np.float32).reshape(NF, 128))
    rw = np.ascontiguousarray(np.asarray(inputs["L0_ffn_router_w"], np.float32))
    w1 = np.ascontiguousarray(np.asarray(inputs["L0_ffn_w1"], np.float32))
    b1 = np.ascontiguousarray(
        np.asarray(inputs["L0_ffn_b1"], np.float32)
        .reshape(E, FF // 128, 128).transpose(0, 2, 1))
    w2 = np.ascontiguousarray(np.asarray(inputs["L0_ffn_w2"], np.float32))
    b2 = np.ascontiguousarray(np.asarray(inputs["L0_ffn_b2"], np.float32))
    in_maps = []
    for c in range(8):
        own = np.ascontiguousarray(x[c * T:(c + 1) * T])
        if c % 4 == 0:
            halo = np.zeros((HALO, D), dtype=np.float32)
        else:
            halo = np.ascontiguousarray(x[c * T - HALO:c * T])
        cmsk = (np.arange(8) < c).astype(np.float32).reshape(8, 1)
        m = dict(x_own=own, x_halo=halo, wqT=wqT, wfT=wfT, wiT=wiT, woT=woT,
                 gwin=gw, rwin=rw, w1in=w1, b1in=b1, w2in=w2, b2in=b2,
                 c_cmask=np.ascontiguousarray(cmsk), **com)
        if no_coll:
            m["dbg_coreoff"] = np.ascontiguousarray(
                coreoffs[c].reshape(1, 8).astype(np.float32))
        in_maps.append(m)
    return in_maps


def kernel(**inputs):
    nc, _ = build()
    in_maps = make_in_maps(inputs)
    res = run_bass_kernel_spmd(nc, in_maps, core_ids=list(range(8)))
    y = np.concatenate([np.asarray(r["y"]) for r in res.results], axis=0)
    return y.reshape(2, 16384, D)
